# version 10
# baseline (speedup 1.0000x reference)
import numpy as np
import ml_dtypes
from contextlib import ExitStack

import concourse.bass as bass
import concourse.mybir as mybir
from concourse.bass_utils import run_bass_kernel_spmd

F32 = mybir.dt.float32
BF16 = mybir.dt.bfloat16
F16 = mybir.dt.float16
I8 = mybir.dt.int8
ALU = mybir.AluOpType
ACT = mybir.ActivationFunctionType
AX = mybir.AxisListType

S, D = 2048, 2048
NT, ND = S // 128, D // 128
H, HD, RD = 16, 128, 64
DQL, DKVL = 384, 512
NE = 8
DI, DS, DTR = 2048, 16, 64
SCALE_ATTN = float((HD + RD) ** -0.5)
LOG_SCALE_ATTN = float(np.log(SCALE_ATTN))
RND_C = 12582912.0
BIG = 1.0e7
TCH = 512
NCH = S // TCH
LN_EPS = 1e-5
RMS_EPS = 1e-6
Q_EPS = 1e-7

ENGS = ("sync", "gpsimd", "vector", "scalar", "tensor")
CHAIN_ENGS = ("vector",)


class Ctr:
    __slots__ = ("h", "v", "synced")

    def __init__(self, h):
        self.h = h
        self.v = 0
        self.synced = 0


class Prog:
    def __init__(self, nc):
        self.nc = nc
        self.items = []
        self.last_wait = {}
        self.nblk = 0

    def do(self, eng, fn, wait=(), inc=(), chain=None):
        w = []
        for c, val in wait:
            if val is None or val <= 0:
                continue
            key = (eng, id(c))
            if self.last_wait.get(key, -1) >= val:
                continue
            self.last_wait[key] = val
            w.append((c.h, val))
        incs = []
        for c, n in inc:
            c.v += n
            incs.append((c.h, n))
        if chain is None:
            chain = eng in ("vector", "scalar")
        self.items.append((eng, w, fn, incs, chain))

    def barrier(self):
        for c in getattr(self, "dma_ctrs", []):
            if c.v > c.synced:
                self.do("sync", None, wait=[(c, c.v)])
                c.synced = c.v
        items = self.items
        self.items = []
        self.last_wait = {}
        if not items:
            return
        self.nblk += 1
        with self.nc.Block(f"seg{self.nblk}") as blk:
            for ename in ENGS:
                mine = [it for it in items if it[0] == ename]
                if not mine:
                    continue

                def body(eng, mine=mine, ename=ename):
                    for _, waits, fn, incs, chain in mine:
                        for h, val in waits:
                            eng.wait_ge(h, val)
                        if chain and fn is not None:
                            eng.drain()
                        inst = fn(eng) if fn is not None else None
                        if incs:
                            if inst is None:
                                inst = eng.drain() if ename != "sync" else eng.nop()
                            for h, n in incs:
                                inst = inst.then_inc(h, n)
                getattr(blk, ename)(body)


def build_program():
    nc = bass.Bass("TRN2", target_bir_lowering=False)

    def din(name, shape, dt=F32):
        return nc.dram_tensor(name, shape, dt, kind="ExternalInput").ap()

    x_in = din("x", [S, D])
    Wq_down_d = din("Wq_down", [D, DQL])
    q_norm_w_d = din("q_norm_w", [DQL])
    Wq_up_d = din("Wq_up", [DQL, H * HD])
    Wq_rope_d = din("Wq_rope", [D, H * RD])
    Wkv_down_d = din("Wkv_down", [D, DKVL])
    kv_norm_w_d = din("kv_norm_w", [DKVL])
    Wk_up_d = din("Wk_up", [DKVL, H * HD])
    Wv_up_d = din("Wv_up", [DKVL, H * HD])
    Wk_rope_d = din("Wk_rope", [D, H * RD])
    Wout_d = din("Wout", [H * HD, D])
    gate_w_d = din("gate_w", [NE, D])
    expert_w_d = din("expert_w", [NE, D, D])
    expert_b_d = din("expert_b", [NE, D])
    Wx_d = din("Wx", [DI, DTR + 2 * DS])
    Wdt_d = din("Wdt", [DTR, DI])
    bdt_d = din("bdt", [DI])
    A_log_d = din("A_log", [DI, DS])
    Dp_d = din("Dp", [DI])
    w8_d = din("w8", [D, D], I8)
    w_scale_d = din("w_scale", [D, 1])
    bias8_d = din("bias8", [D])
    ident_d = din("ident", [128, 128])
    iota8_d = din("iota8", [128, NE])
    cos_d = din("cosrep", [S, H * RD // 2])
    sin_d = din("sinrep", [S, H * RD // 2])

    y_out = nc.dram_tensor("y", [S, D], F32, kind="ExternalOutput").ap()

    def dscr(name, shape, dt):
        return nc.dram_tensor(name, shape, dt, kind="Internal").ap()

    g0_scr = dscr("g0_scr", [S, D], BF16)
    ao_scr = dscr("ao_scr", [H * HD, S], BF16)
    x1_scr = dscr("x1_scr", [S, D], F32)
    qrT_scr = dscr("qrT_scr", [H * RD, S], BF16)
    krT_scr = dscr("krT_scr", [H * RD, S], BF16)
    uT_scr = dscr("uT_scr", [ND * NT * 128, 128], F32)
    dtT_scr = dscr("dtT_scr", [ND * NT * 128, 128], F32)
    ffnT_scr = dscr("ffnT_scr", [ND * NT * 128, 128], F16)
    bcT_scr = dscr("bcT_scr", [2 * DS, S], F32)

    P = Prog(nc)
    es = ExitStack()
    with es:
        def sb(name, shape, dt=F32, side=None):
            return es.enter_context(nc.sbuf_tensor(name, shape, dt, side=side))

        def mkctr(name):
            return Ctr(es.enter_context(nc.semaphore(name)))

        sW = mkctr("sW")
        sLa = mkctr("sLa")
        sLb = mkctr("sLb")
        sLc = mkctr("sLc")
        sLd = mkctr("sLd")
        sLe = mkctr("sLe")
        sSa = mkctr("sSa")
        sSb = mkctr("sSb")
        sSe = mkctr("sSe")
        sSd = mkctr("sSd")
        sWg = mkctr("sWg")
        sLg0 = mkctr("sLg0")
        sLg1 = mkctr("sLg1")
        sT = mkctr("sT")
        sV = mkctr("sV")
        sSc = mkctr("sSc")
        sP = mkctr("sP")
        sL = [sLa, sLb]
        sLg = [sLg0, sLg1]
        sS2 = [sSa, sSb]
        P.dma_ctrs = [sW, sWg, sLa, sLb, sLc, sLd, sLe, sLg0, sLg1, sSa, sSb, sSe, sSd]

        ident = sb("identsb", [128, 128])
        ident_b = sb("ident_b", [128, 128], BF16)
        ident_h = sb("ident_h", [128, 128], F16)
        iota8 = sb("iota8sb", [128, NE])
        ones_b = sb("ones_b", [128, 1], BF16)
        onesr_b = sb("onesr_b", [1, 128], BF16)
        onesr_f = sb("onesr_f", [1, 128])
        wdense = sb("wdense", [128, NT * NE])
        wdenseT = sb("wdenseT", [8, NT * 128], BF16)
        insc = sb("insc", [128, NT])
        bc_sb = sb("bc_sb", [128, NT * 2 * DS])
        cols = [sb(f"col{i}", [128, 1]) for i in range(8)]
        c_lneps = sb("c_lneps", [128, 1])
        c_rmseps = sb("c_rmseps", [128, 1])
        c_lsa = sb("c_lsa", [128, 1])
        c_one = sb("c_one", [128, 1])
        rows8 = [sb(f"row8{i}", [128, NE]) for i in range(2)]

        PS = [es.enter_context(nc.psum_tensor(f"ps{i}", [128, 512], F32)) for i in range(8)]

        trb = {"bank": 0, 0: 0, 1: 0}

        def transp_one(src, idm, waits, dsts=(), evac=None, evac_inc=()):
            p, w = src.shape
            b = trb["bank"]
            trb["bank"] = 1 - b
            pb = PS[b]
            dt_ = src.dtype
            P.do("tensor",
                 lambda e, src=src, pb=pb, p=p, w=w, idm=idm, dt_=dt_:
                 e.transpose(pb.ap().bitcast(dt_)[0:w, 0:p], src, idm[0:p, 0:p]),
                 wait=list(waits) + [(sV, trb[b])], inc=[(sT, 1)])
            t = sT.v
            if evac is not None:
                P.do("vector", lambda e, pb=pb, p=p, w=w, dt_=dt_: evac(e, pb.ap().bitcast(dt_)[0:w, 0:p]),
                     wait=[(sT, t)], inc=[(sV, 1)] + list(evac_inc), chain=False)
            else:
                nd = len(dsts)
                for ii, dst in enumerate(dsts):
                    P.do("vector",
                         lambda e, dst=dst, pb=pb, p=p, w=w, dt_=dt_:
                         e.tensor_copy(dst, pb.ap().bitcast(dt_)[0:w, 0:p]),
                         wait=[(sT, t)],
                         inc=([(sV, 1)] + list(evac_inc)) if ii == nd - 1 else (),
                         chain=False)
            trb[b] = sV.v
            return sV.v

        P.do("sync", lambda e: e.dma_start(ident[:], ident_d), inc=[(sW, 16)])
        P.do("sync", lambda e: e.dma_start(iota8[:], iota8_d), inc=[(sW, 16)])
        w0 = sW.v
        P.do("vector", lambda e: e.tensor_copy(ident_b[:], ident[:]), wait=[(sW, w0)])
        P.do("vector", lambda e: e.tensor_copy(ident_h[:], ident[:]))
        P.do("vector", lambda e: e.memset(ones_b[:], 1.0))
        P.do("vector", lambda e: e.memset(onesr_b[:], 1.0))
        P.do("vector", lambda e: e.memset(onesr_f[:], 1.0))
        P.do("vector", lambda e: e.memset(c_lneps[:], LN_EPS))
        P.do("vector", lambda e: e.memset(c_rmseps[:], RMS_EPS))
        P.do("vector", lambda e: e.memset(c_lsa[:], LOG_SCALE_ATTN))
        P.do("vector", lambda e: e.memset(c_one[:], 1.0))
        P.barrier()

        rstk = ExitStack()
        xnT = rstk.enter_context(nc.sbuf_tensor("xnT", [128, ND * S], BF16, side="right"))

        def emit_ln(buf, out_f, eps, done_ctr_wait, final_wait=()):
            P.do("vector", lambda e, buf=buf: e.tensor_scalar(
                sq_f[:], buf[:], 1.0, None, ALU.mult, ALU.add, accum_out=cols[0][:]),
                 wait=done_ctr_wait, inc=[(sV, 1)])
            vsum = sV.v
            P.do("scalar", lambda e, buf=buf: e.activation(sq_f[:], buf[:], ACT.Square,
                                                           accum_out=cols[1][:]),
                 wait=done_ctr_wait + [(sV, vsum)], inc=[(sSc, 1)])
            wsq = sSc.v
            P.do("vector", lambda e: e.tensor_scalar(cols[0][:], cols[0][:], 1.0 / D, None, ALU.mult),
                 wait=[(sSc, wsq)])
            P.do("vector", lambda e: e.tensor_scalar(cols[1][:], cols[1][:], 1.0 / D, None, ALU.mult))
            P.do("vector", lambda e: e.tensor_tensor(cols[2][:], cols[0][:], cols[0][:], ALU.mult))
            P.do("vector", lambda e: e.tensor_tensor(cols[1][:], cols[1][:], cols[2][:], ALU.subtract),
                 inc=[(sV, 1)])
            wvar = sV.v
            P.do("scalar", lambda e, eps=eps: e.activation(cols[1][:], cols[1][:], ACT.Ln, bias=eps[:]),
                 wait=[(sV, wvar)])
            P.do("scalar", lambda e: e.activation(cols[1][:], cols[1][:], ACT.Exp, scale=-0.5),
                 inc=[(sSc, 1)])
            wrs = sSc.v
            P.do("vector", lambda e: e.tensor_scalar(cols[2][:], cols[0][:], cols[1][:], -1.0,
                                                     ALU.mult, ALU.mult),
                 wait=[(sSc, wrs)], inc=[(sV, 1)])
            wnm = sV.v
            P.do("scalar", lambda e, buf=buf, out_f=out_f: e.activation(
                out_f[:], buf[:], ACT.Identity, bias=cols[2][:], scale=cols[1][:]),
                 wait=[(sV, wnm)] + list(final_wait), inc=[(sSc, 1)])
            return sSc.v

        with ExitStack() as ph:
            def psb(n, s_, dt=F32):
                return ph.enter_context(nc.sbuf_tensor(n, s_, dt))

            gw_sb = psb("gw_sb", [8, D])
            gwT = psb("gwT", [128, ND * NE])
            x_b = [psb(f"x{k}", [128, D]) for k in range(2)]
            xn_f = psb("xn_f", [128, D])
            sq_f = psb("sq_f", [128, D])
            strip = psb("strip", [128, ND * 128])
            lg = psb("lg", [128, NE])

            P.do("sync", lambda e: e.dma_start(gw_sb[:], gate_w_d), inc=[(sW, 16)])
            wgw = sW.v
            for j in range(ND):
                transp_one(gw_sb[0:8, j * 128:(j + 1) * 128], ident, [(sW, wgw)],
                           [gwT[:, j * NE:(j + 1) * NE]])
            tile_done = [0] * NT
            xn_lastT = 0
            for i in range(NT):
                buf = x_b[i % 2]
                P.do("sync", lambda e, buf=buf, i=i: e.dma_start(buf[:], x_in[i * 128:(i + 1) * 128, :]),
                     wait=([(sSc, tile_done[i - 2])] if i >= 2 else []), inc=[(sL[i % 2], 16)])
                wx = sL[i % 2].v
                wxn = emit_ln(buf, xn_f, c_lneps, [(sL[i % 2], wx)], final_wait=[(sT, xn_lastT)])
                tile_done[i] = wxn
                for j in range(ND):
                    transp_one(xn_f[:, j * 128:(j + 1) * 128], ident, [(sSc, wxn)],
                               [strip[:, j * 128:(j + 1) * 128],
                                xnT[:, j * S + i * 128: j * S + (i + 1) * 128]])
                xn_lastT = sT.v
                wstrip = sV.v
                for j in range(ND):
                    P.do("tensor", lambda e, j=j: e.matmul(
                        PS[6].ap()[:, 0:NE], strip[:, j * 128:(j + 1) * 128],
                        gwT[:, j * NE:(j + 1) * NE], start=(j == 0), stop=(j == ND - 1)),
                         wait=[(sV, wstrip)], inc=([(sT, 1)] if j == ND - 1 else []))
                wlg = sT.v
                P.do("vector", lambda e: e.tensor_copy(lg[:], PS[6].ap()[:, 0:NE]),
                     wait=[(sT, wlg)], inc=[(sV, 1)])
                P.do("vector", lambda e: e.tensor_reduce(cols[0][:], lg[:], AX.X, ALU.max))
                P.do("vector", lambda e: e.tensor_scalar(rows8[0][:], lg[:], cols[0][:], None, ALU.is_equal))
                P.do("vector", lambda e: e.tensor_scalar(rows8[0][:], rows8[0][:], -BIG, BIG, ALU.mult, ALU.add))
                P.do("vector", lambda e: e.tensor_tensor(rows8[0][:], iota8[:], rows8[0][:], ALU.add))
                P.do("vector", lambda e: e.tensor_reduce(cols[1][:], rows8[0][:], AX.X, ALU.min))
                P.do("vector", lambda e: e.tensor_scalar(rows8[0][:], iota8[:], cols[1][:], None, ALU.is_equal))
                P.do("vector", lambda e: e.scalar_tensor_tensor(rows8[1][:], rows8[0][:], -BIG, lg[:],
                                                                ALU.mult, ALU.add))
                P.do("vector", lambda e: e.tensor_reduce(cols[2][:], rows8[1][:], AX.X, ALU.max))
                P.do("vector", lambda e: e.tensor_scalar(rows8[1][:], rows8[1][:], cols[2][:], None, ALU.is_equal))
                P.do("vector", lambda e: e.tensor_scalar(rows8[1][:], rows8[1][:], -BIG, BIG, ALU.mult, ALU.add))
                P.do("vector", lambda e: e.tensor_tensor(rows8[1][:], iota8[:], rows8[1][:], ALU.add))
                P.do("vector", lambda e: e.tensor_reduce(cols[3][:], rows8[1][:], AX.X, ALU.min))
                P.do("vector", lambda e: e.tensor_scalar(rows8[1][:], iota8[:], cols[3][:], None, ALU.is_equal))
                P.do("vector", lambda e: e.tensor_tensor(cols[4][:], cols[2][:], cols[0][:], ALU.subtract),
                     inc=[(sV, 1)])
                wd = sV.v
                P.do("scalar", lambda e: e.activation(cols[4][:], cols[4][:], ACT.Exp),
                     wait=[(sV, wd)], inc=[(sSc, 1)])
                we = sSc.v
                P.do("vector", lambda e: e.tensor_scalar(cols[5][:], cols[4][:], 1.0, None, ALU.add),
                     wait=[(sSc, we)])
                P.do("vector", lambda e: e.reciprocal(cols[5][:], cols[5][:]))
                P.do("vector", lambda e: e.tensor_tensor(cols[6][:], cols[4][:], cols[5][:], ALU.mult))
                P.do("vector", lambda e: e.tensor_scalar(rows8[0][:], rows8[0][:], cols[5][:], None, ALU.mult))
                P.do("vector", lambda e, i=i: e.scalar_tensor_tensor(
                    wdense[:, i * NE:(i + 1) * NE], rows8[1][:], cols[6][:], rows8[0][:],
                    ALU.mult, ALU.add), inc=[(sV, 1)])
                wwd = sV.v
                transp_one(wdense[:, i * NE:(i + 1) * NE], ident, [(sV, wwd)],
                           [wdenseT[:, i * 128:(i + 1) * 128]])
            P.barrier()

        with ExitStack() as ph:
            def psb(n, s_, dt=F32):
                return ph.enter_context(nc.sbuf_tensor(n, s_, dt))

            eb_sb = psb("eb_sb", [8, D], BF16)
            Wsb = [psb(f"Wsb{k}", [128, 4 * D], BF16) for k in range(2)]
            WeT = [psb(f"WeT{k}", [128, ND * 512], BF16) for k in range(2)]
            acc = psb("acc", [128, NT * 512], BF16)
            sil = psb("sil", [128, 512], BF16)

            P.do("gpsimd", lambda e: e.dma_start(eb_sb[:], expert_b_d), inc=[(sWg, 16)])
            web = sWg.v
            wet_pe_done = [0, 0]
            wsb_free = [0, 0]
            seed_v = 0
            acc_v = [0] * NT
            silu_done = [0] * NT
            st_g = 0
            for oc in range(4):
                for i in range(NT):
                    P.do("tensor", lambda e, i=i, oc=oc: e.matmul(
                        PS[7].ap()[:], wdenseT[:, i * 128:(i + 1) * 128],
                        eb_sb[0:8, oc * 512:(oc + 1) * 512], start=True, stop=True),
                         wait=[(sWg, web), (sV, seed_v)], inc=[(sT, 1)])
                    ws = sT.v
                    P.do("vector", lambda e, i=i: e.tensor_copy(acc[:, i * 512:(i + 1) * 512],
                                                                PS[7].ap()[:]),
                         wait=[(sT, ws), (sSc, silu_done[i])], inc=[(sV, 1)], chain=False)
                    seed_v = sV.v
                    acc_v[i] = sV.v
                for e_i in range(NE):
                    k = e_i % 2
                    P.do("gpsimd", lambda e, e_i=e_i, oc=oc, k=k: e.dma_start(
                        Wsb[k][:].rearrange("p (o c) -> p o c", o=4),
                        expert_w_d[e_i, oc * 512:(oc + 1) * 512, :].rearrange("(o p) c -> p o c", p=128)),
                         wait=[(sV, wsb_free[k])], inc=[(sLg[k], 16)])
                    wload = sLg[k].v
                    for ot in range(4):
                        for j in range(ND):
                            transp_one(Wsb[k][:, ot * D + j * 128: ot * D + (j + 1) * 128],
                                       ident_b, [(sLg[k], wload), (sT, wet_pe_done[k])],
                                       [WeT[k][:, j * 512 + ot * 128: j * 512 + (ot + 1) * 128]])
                    wsb_free[k] = sV.v
                    wtr = sV.v
                    for i in range(NT):
                        pb = PS[2 + (i % 2)]
                        bank_prev = acc_v[i - 2] if i >= 2 else acc_v[NT - 2 + i]
                        for j in range(ND):
                            P.do("tensor", lambda e, i=i, j=j, k=k, pb=pb: e.matmul(
                                pb.ap()[:], xnT[:, j * S + i * 128: j * S + (i + 1) * 128],
                                WeT[k][:, j * 512:(j + 1) * 512],
                                start=(j == 0), stop=(j == ND - 1)),
                                 wait=[(sV, wtr), (sV, bank_prev)],
                                 inc=([(sT, 1)] if j == ND - 1 else []))
                        wmm = sT.v
                        if i == NT - 1:
                            wet_pe_done[k] = wmm
                        P.do("vector", lambda e, i=i, e_i=e_i, pb=pb: e.scalar_tensor_tensor(
                            acc[:, i * 512:(i + 1) * 512], pb.ap()[:],
                            wdense[:, i * NE + e_i: i * NE + e_i + 1],
                            acc[:, i * 512:(i + 1) * 512], ALU.mult, ALU.add),
                             wait=[(sT, wmm)], inc=[(sV, 1)], chain=False)
                        acc_v[i] = sV.v
                for i in range(NT):
                    P.do("scalar", lambda e, i=i: e.activation(sil[:], acc[:, i * 512:(i + 1) * 512],
                                                               ACT.Silu),
                         wait=[(sV, acc_v[i]), (sSa, st_g)], inc=[(sSc, 1)], chain=False)
                    wsil = sSc.v
                    silu_done[i] = wsil
                    P.do("gpsimd", lambda e, i=i, oc=oc: e.dma_start(
                        g0_scr[i * 128:(i + 1) * 128, oc * 512:(oc + 1) * 512], sil[:]),
                         wait=[(sSc, wsil)], inc=[(sSa, 16)])
                    st_g = sSa.v
            P.barrier()

        c_qT = rstk.enter_context(nc.sbuf_tensor("c_qT", [128, 3 * S], BF16, side="right"))
        c_kvT = rstk.enter_context(nc.sbuf_tensor("c_kvT", [128, 4 * S], BF16, side="right"))
        with ExitStack() as ph:
            def psb(n, s_, dt=F32):
                return ph.enter_context(nc.sbuf_tensor(n, s_, dt))

            Wd_sb = psb("Wd_sb", [128, ND * DKVL], BF16)
            c_sb = psb("c_sb", [128, DKVL], BF16)
            csq = psb("csq", [128, DKVL])
            Wr_sb = psb("Wr_sb", [128, ND * 512], BF16)
            cs_c = psb("cs_c", [128, 256])
            cs_s = psb("cs_s", [128, 256])
            re_f = psb("re_f", [128, 256])
            ro_f = psb("ro_f", [128, 256])
            rr_sb = psb("rr_sb", [128, 512], BF16)
            rp_b = psb("rp_b", [128, 4 * 128], BF16)

            wfree = 0
            for (W_d, RL, cT) in ((Wq_down_d, DQL, c_qT), (Wkv_down_d, DKVL, c_kvT)):
                NR = RL // 128
                P.do("gpsimd", lambda e, W_d=W_d, RL=RL: e.dma_start(
                    Wd_sb[:, 0:ND * RL].rearrange("p (j c) -> p j c", j=ND),
                    W_d.rearrange("(j p) c -> p j c", p=128)),
                     wait=[(sV, wfree)], inc=[(sWg, 16)])
                wW = sWg.v
                psfree = [0, 0]
                for i in range(NT):
                    pb = PS[2 + (i % 2)]
                    for j in range(ND):
                        P.do("tensor", lambda e, i=i, j=j, RL=RL, pb=pb: e.matmul(
                            pb.ap()[:, 0:RL], xnT[:, j * S + i * 128: j * S + (i + 1) * 128],
                            Wd_sb[:, j * RL:(j + 1) * RL], start=(j == 0), stop=(j == ND - 1)),
                             wait=[(sWg, wW), (sSc, psfree[i % 2])],
                             inc=([(sT, 1)] if j == ND - 1 else []))
                    wmm = sT.v
                    P.do("scalar", lambda e, RL=RL, pb=pb: e.activation(
                        csq[:, 0:RL], pb.ap()[:, 0:RL], ACT.Square, accum_out=cols[0][:]),
                         wait=[(sT, wmm)], inc=[(sSc, 1)], chain=False)
                    wsq = sSc.v
                    P.do("vector", lambda e, RL=RL: e.tensor_scalar(
                        cols[0][:], cols[0][:], 1.0 / RL, None, ALU.mult),
                         wait=[(sSc, wsq)], inc=[(sV, 1)])
                    wv = sV.v
                    P.do("scalar", lambda e: e.activation(cols[0][:], cols[0][:], ACT.Ln, bias=c_rmseps[:]),
                         wait=[(sV, wv)])
                    P.do("scalar", lambda e: e.activation(cols[0][:], cols[0][:], ACT.Exp, scale=-0.5))
                    P.do("scalar", lambda e, RL=RL, pb=pb: e.activation(
                        c_sb[:, 0:RL], pb.ap()[:, 0:RL], ACT.Copy, scale=cols[0][:]),
                         wait=[(sT, getattr(P, "_b3ct", 0))], inc=[(sSc, 1)])
                    psfree[i % 2] = sSc.v
                    wcs = sSc.v
                    for j in range(NR):
                        transp_one(c_sb[:, j * 128:(j + 1) * 128], ident_b, [(sSc, wcs)],
                                   [cT[:, j * S + i * 128: j * S + (i + 1) * 128]])
                    P._b3ct = sT.v
                P.do("vector", lambda e: None, inc=[(sV, 1)])
                wfree = sV.v

            for (Wr_d, rT_scr) in ((Wq_rope_d, qrT_scr), (Wk_rope_d, krT_scr)):
                for half in range(2):
                    P.do("gpsimd", lambda e, Wr_d=Wr_d, half=half: e.dma_start(
                        Wr_sb[:].rearrange("p (j c) -> p j c", j=ND),
                        Wr_d[:, half * 512:(half + 1) * 512].rearrange("(j p) c -> p j c", p=128)),
                         wait=[(sV, wfree)], inc=[(sWg, 16)])
                    wW = sWg.v
                    psfree = [0, 0]
                    wst = 0
                    for i in range(NT):
                        pb = PS[2 + (i % 2)]
                        for j in range(ND):
                            P.do("tensor", lambda e, i=i, j=j, pb=pb: e.matmul(
                                pb.ap()[:], xnT[:, j * S + i * 128: j * S + (i + 1) * 128],
                                Wr_sb[:, j * 512:(j + 1) * 512], start=(j == 0), stop=(j == ND - 1)),
                                 wait=[(sWg, wW), (sV, psfree[i % 2])],
                                 inc=([(sT, 1)] if j == ND - 1 else []))
                        wmm = sT.v
                        P.do("sync", lambda e, i=i, half=half: e.dma_start(
                            cs_c[:], cos_d[i * 128:(i + 1) * 128, half * 256:(half + 1) * 256]),
                             wait=[(sV, getattr(P, "_b3cs", 0))], inc=[(sLc, 16)])
                        P.do("sync", lambda e, i=i, half=half: e.dma_start(
                            cs_s[:], sin_d[i * 128:(i + 1) * 128, half * 256:(half + 1) * 256]),
                             inc=[(sLc, 16)])
                        wcs2 = sLc.v
                        pe = pb.ap().rearrange("p (k two) -> p k two", two=2)
                        rv = rr_sb[:].rearrange("p (k two) -> p k two", two=2)
                        P.do("vector", lambda e, pe=pe: e.tensor_tensor(re_f[:], pe[:, :, 0], cs_c[:], ALU.mult),
                             wait=[(sT, wmm), (sLc, wcs2)], chain=False)
                        P.do("vector", lambda e, pe=pe: e.tensor_tensor(ro_f[:], pe[:, :, 1], cs_s[:], ALU.mult),
                             chain=False)
                        P.do("vector", lambda e, rv=rv: e.tensor_tensor(rv[:, :, 0], re_f[:], ro_f[:], ALU.subtract),
                             chain=False)
                        P.do("vector", lambda e, pe=pe: e.tensor_tensor(re_f[:], pe[:, :, 0], cs_s[:], ALU.mult),
                             chain=False)
                        P.do("vector", lambda e, pe=pe: e.tensor_tensor(ro_f[:], pe[:, :, 1], cs_c[:], ALU.mult),
                             inc=[(sV, 1)], chain=False)
                        psfree[i % 2] = sV.v
                        P._b3cs = sV.v
                        P.do("vector", lambda e, rv=rv: e.tensor_tensor(rv[:, :, 1], re_f[:], ro_f[:], ALU.add),
                             inc=[(sV, 1)], chain=False)
                        wrr = sV.v
                        for c4 in range(4):
                            transp_one(rr_sb[:, c4 * 128:(c4 + 1) * 128], ident_b,
                                       [(sV, wrr), (sSa, wst)],
                                       [rp_b[:, c4 * 128:(c4 + 1) * 128]])
                        wev = sV.v
                        P.do("gpsimd", lambda e, rT_scr=rT_scr, half=half, i=i: e.dma_start(
                            rT_scr[half * 512:(half + 1) * 512, i * 128:(i + 1) * 128]
                            .rearrange("(c p) t -> p c t", p=128),
                            rp_b[:].rearrange("p (c t) -> p c t", c=4)),
                             wait=[(sV, wev)], inc=[(sSa, 16)])
                        wst = sSa.v
                    P.do("vector", lambda e: None, wait=[(sSa, wst)], inc=[(sV, 1)])
                    wfree = sV.v
            P.barrier()

        with ExitStack() as ph:
            def psb(n, s_, dt=F32):
                return ph.enter_context(nc.sbuf_tensor(n, s_, dt))

            Wqu = psb("Wqu", [128, 3 * 2048], BF16)
            Wku = psb("Wku", [128, 4 * 2048], BF16)
            Wvu = psb("Wvu", [128, 4 * 2048], BF16)
            nwq = psb("nwq", [128, 3])
            nwkv = psb("nwkv", [128, 4])
            qTh = [psb(f"qTh{k}", [128, S], BF16) for k in range(2)]
            kTh = [psb(f"kTh{k}", [128, S], BF16) for k in range(2)]
            v_sb = [psb(f"v_sb{k}", [128, NT * 128], BF16) for k in range(2)]
            qr_h = [psb(f"qr_h{k}", [64, S], BF16) for k in range(2)]
            kr_h = [psb(f"kr_h{k}", [64, S], BF16) for k in range(2)]
            sq_b = psb("sq_b", [128, S], BF16)
            sqr_b = psb("sqr_b", [64, S], BF16)
            row4 = psb("row4", [1, 4])
            sca = psb("sca", [1, 1])
            scb = psb("scb", [1, 1])
            shiftb = psb("shiftb", [128, 1])
            ex_b = [psb(f"ex_b{k}", [128, 512], BF16) for k in range(2)]
            rec_f = psb("rec_f", [1, 512])
            rec_b = psb("rec_b", [1, 512], BF16)
            rb_f = psb("rb_f", [128, 512])
            ao_b = [psb(f"ao_b{k}", [128, 512], BF16) for k in range(2)]

            P.do("gpsimd", lambda e: e.dma_start(
                Wqu[:].rearrange("p (j c) -> p j c", j=3), Wq_up_d.rearrange("(j p) c -> p j c", p=128)),
                 inc=[(sWg, 16)])
            P.do("gpsimd", lambda e: e.dma_start(
                Wku[:].rearrange("p (j c) -> p j c", j=4), Wk_up_d.rearrange("(j p) c -> p j c", p=128)),
                 inc=[(sWg, 16)])
            P.do("gpsimd", lambda e: e.dma_start(
                Wvu[:].rearrange("p (j c) -> p j c", j=4), Wv_up_d.rearrange("(j p) c -> p j c", p=128)),
                 inc=[(sWg, 16)])
            P.do("sync", lambda e: e.dma_start(nwq[:, 0:3], q_norm_w_d.rearrange("(j p) -> p j", p=128),
                                               allow_slow_non_contiguous=True), inc=[(sW, 16)])
            P.do("sync", lambda e: e.dma_start(nwkv[:, 0:4], kv_norm_w_d.rearrange("(j p) -> p j", p=128),
                                               allow_slow_non_contiguous=True), inc=[(sW, 16)])
            wld = sWg.v
            wldn = sW.v
            for j in range(3):
                P.do("vector", lambda e, j=j: e.tensor_scalar(
                    Wqu[:, j * 2048:(j + 1) * 2048], Wqu[:, j * 2048:(j + 1) * 2048],
                    nwq[:, j:j + 1], None, ALU.mult), wait=[(sWg, wld), (sW, wldn)], chain=False)
            for j in range(4):
                P.do("vector", lambda e, j=j: e.tensor_scalar(
                    Wku[:, j * 2048:(j + 1) * 2048], Wku[:, j * 2048:(j + 1) * 2048],
                    nwkv[:, j:j + 1], None, ALU.mult), chain=False)
                P.do("vector", lambda e, j=j: e.tensor_scalar(
                    Wvu[:, j * 2048:(j + 1) * 2048], Wvu[:, j * 2048:(j + 1) * 2048],
                    nwkv[:, j:j + 1], None, ALU.mult), inc=([(sV, 1)] if j == 3 else []), chain=False)
            wwup = sV.v

            rope_ld = [0, 0]
            proj_done = [0, 0]
            ao_st = [0, 0]
            exp_done = {}
            for h in range(H):
                k = h % 2
                P.do("sync", lambda e, h=h, k=k: e.dma_start(qr_h[k][:], qrT_scr[h * 64:(h + 1) * 64, :]),
                     wait=[(sT, proj_done[k])], inc=[(sL[k], 16)])
                P.do("sync", lambda e, h=h, k=k: e.dma_start(kr_h[k][:], krT_scr[h * 64:(h + 1) * 64, :]),
                     inc=[(sL[k], 16)])
                wrope = sL[k].v
                for qb in range(4):
                    pb = PS[2 + (qb % 2)]
                    for jr in range(3):
                        P.do("tensor", lambda e, h=h, qb=qb, jr=jr, pb=pb: e.matmul(
                            pb.ap()[:], Wqu[:, jr * 2048 + h * 128: jr * 2048 + (h + 1) * 128],
                            c_qT[:, jr * S + qb * 512: jr * S + (qb + 1) * 512],
                            start=(jr == 0), stop=(jr == 2)),
                             wait=[(sV, wwup), (sV, sV.v), (sT, proj_done[k]),
                                   (sSc, exp_done.get((2 + qb % 2), 0))],
                             inc=([(sT, 1)] if jr == 2 else []))
                    wmm = sT.v
                    P.do("vector", lambda e, qb=qb, k=k, pb=pb: e.tensor_copy(
                        qTh[k][:, qb * 512:(qb + 1) * 512], pb.ap()[:]), wait=[(sT, wmm)], inc=[(sV, 1)],
                         chain=False)
                for kb in range(4):
                    pb = PS[2 + (kb % 2)]
                    for jr in range(4):
                        P.do("tensor", lambda e, h=h, kb=kb, jr=jr, pb=pb: e.matmul(
                            pb.ap()[:], Wku[:, jr * 2048 + h * 128: jr * 2048 + (h + 1) * 128],
                            c_kvT[:, jr * S + kb * 512: jr * S + (kb + 1) * 512],
                            start=(jr == 0), stop=(jr == 3)),
                             wait=[(sV, sV.v)], inc=([(sT, 1)] if jr == 3 else []))
                    wmm = sT.v
                    P.do("vector", lambda e, kb=kb, k=k, pb=pb: e.tensor_copy(
                        kTh[k][:, kb * 512:(kb + 1) * 512], pb.ap()[:]), wait=[(sT, wmm)], inc=[(sV, 1)],
                         chain=False)
                for tt in range(NT):
                    pb = PS[2 + (tt % 2)]
                    for jr in range(4):
                        P.do("tensor", lambda e, h=h, tt=tt, jr=jr, pb=pb: e.matmul(
                            pb.ap()[:, 0:128], c_kvT[:, jr * S + tt * 128: jr * S + (tt + 1) * 128],
                            Wvu[:, jr * 2048 + h * 128: jr * 2048 + (h + 1) * 128],
                            start=(jr == 0), stop=(jr == 3)),
                             wait=[(sV, sV.v)], inc=([(sT, 1)] if jr == 3 else []))
                    wmm = sT.v
                    P.do("vector", lambda e, tt=tt, k=k, pb=pb: e.tensor_copy(
                        v_sb[k][:, tt * 128:(tt + 1) * 128], pb.ap()[:, 0:128]),
                         wait=[(sT, wmm)], inc=[(sV, 1)], chain=False)
                wproj = sV.v
                P.do("scalar", lambda e, k=k: e.activation(sq_b[:], qTh[k][:], ACT.Square),
                     wait=[(sV, wproj)], inc=[(sSc, 1)], chain=False)
                P.do("scalar", lambda e, k=k: e.activation(sqr_b[:], qr_h[k][:], ACT.Square),
                     wait=[(sL[k], wrope)], inc=[(sSc, 1)], chain=False)
                wsq2 = sSc.v
                for qb in range(4):
                    P.do("tensor", lambda e, qb=qb: e.matmul(
                        PS[6].ap()[0:1, :], ones_b[:], sq_b[:, qb * 512:(qb + 1) * 512],
                        start=True, stop=False),
                         wait=[(sSc, wsq2), (sV, getattr(P, "_b4n", 0)),
                               (sV, getattr(P, "_b4den", 0))])
                    P.do("tensor", lambda e, qb=qb: e.matmul(
                        PS[6].ap()[0:1, :], ones_b[0:64, :], sqr_b[0:64, qb * 512:(qb + 1) * 512],
                        start=False, stop=True), inc=[(sT, 1)])
                    wnr = sT.v
                    P.do("vector", lambda e, qb=qb: e.tensor_reduce(
                        row4[0:1, qb:qb + 1], PS[6].ap()[0:1, :], AX.X, ALU.max),
                         wait=[(sT, wnr)], inc=[(sV, 1)])
                    P._b4n = sV.v
                wqn = sT.v
                P.do("vector", lambda e: e.tensor_reduce(sca[0:1, :], row4[0:1, :], AX.X, ALU.max),
                     inc=[(sV, 1)])
                wmq = sV.v
                P.do("scalar", lambda e, k=k: e.activation(sq_b[:], kTh[k][:], ACT.Square),
                     wait=[(sT, wqn)], inc=[(sSc, 1)], chain=False)
                P.do("scalar", lambda e, k=k: e.activation(sqr_b[:], kr_h[k][:], ACT.Square),
                     inc=[(sSc, 1)], chain=False)
                wsq3 = sSc.v
                for qb in range(4):
                    P.do("tensor", lambda e, qb=qb: e.matmul(
                        PS[6].ap()[0:1, :], ones_b[:], sq_b[:, qb * 512:(qb + 1) * 512],
                        start=True, stop=False), wait=[(sSc, wsq3), (sV, P._b4n)])
                    P.do("tensor", lambda e, qb=qb: e.matmul(
                        PS[6].ap()[0:1, :], ones_b[0:64, :], sqr_b[0:64, qb * 512:(qb + 1) * 512],
                        start=False, stop=True), inc=[(sT, 1)])
                    wnr = sT.v
                    P.do("vector", lambda e, qb=qb: e.tensor_reduce(
                        row4[0:1, qb:qb + 1], PS[6].ap()[0:1, :], AX.X, ALU.max),
                         wait=[(sT, wnr)], inc=[(sV, 1)])
                    P._b4n = sV.v
                P.do("vector", lambda e: e.tensor_reduce(scb[0:1, :], row4[0:1, :], AX.X, ALU.max),
                     inc=[(sV, 1)])
                wmk = sV.v
                P.do("scalar", lambda e: e.activation(sca[0:1, :], sca[0:1, :], ACT.Ln),
                     wait=[(sV, wmq)])
                P.do("scalar", lambda e: e.activation(scb[0:1, :], scb[0:1, :], ACT.Ln),
                     wait=[(sV, wmk)], inc=[(sSc, 1)])
                wln = sSc.v
                P.do("vector", lambda e: e.tensor_tensor(sca[0:1, :], sca[0:1, :], scb[0:1, :], ALU.add),
                     wait=[(sSc, wln)], inc=[(sV, 1)])
                wadd = sV.v
                P.do("scalar", lambda e: e.activation(sca[0:1, :], sca[0:1, :], ACT.Exp,
                                                      scale=0.5, bias=c_lsa[0:1, :]),
                     wait=[(sV, wadd)], inc=[(sSc, 1)])
                wexp = sSc.v
                P.do("vector", lambda e: e.tensor_scalar(sca[0:1, :], sca[0:1, :], -1.0, None, ALU.mult),
                     wait=[(sSc, wexp)], inc=[(sV, 1)])
                wneg = sV.v
                P.do("tensor", lambda e: e.matmul(PS[5].ap()[:, 0:1], onesr_f[:], sca[0:1, 0:1],
                                                  start=True, stop=True),
                     wait=[(sV, wneg), (sV, getattr(P, "_b4s", 0))], inc=[(sT, 1)])
                wbc = sT.v
                P.do("vector", lambda e: e.tensor_copy(shiftb[:], PS[5].ap()[:, 0:1]),
                     wait=[(sT, wbc)], inc=[(sV, 1)])
                P._b4s = sV.v
                wshift = sV.v
                for qb in range(4):
                    av_v = getattr(P, "_b4ao", 0)
                    den_v = getattr(P, "_b4den", 0)
                    for kt in range(NT):
                        pb_i = 2 + (kt % 2)
                        pb = PS[pb_i]
                        P.do("tensor", lambda e, h=h, k=k, kt=kt, qb=qb, pb=pb: e.matmul(
                            pb.ap()[:], kTh[k][:, kt * 128:(kt + 1) * 128],
                            qTh[k][:, qb * 512:(qb + 1) * 512], start=True, stop=False),
                             wait=[(sSc, exp_done.get(pb_i, 0)), (sV, wshift)])
                        P.do("tensor", lambda e, h=h, k=k, kt=kt, qb=qb, pb=pb: e.matmul(
                            pb.ap()[:], kr_h[k][0:64, kt * 128:(kt + 1) * 128],
                            qr_h[k][0:64, qb * 512:(qb + 1) * 512], start=False, stop=True),
                             inc=[(sT, 1)])
                        wsc = sT.v
                        P.do("scalar", lambda e, kt=kt, pb=pb: e.activation(
                            ex_b[kt % 2][:], pb.ap()[:], ACT.Exp, scale=SCALE_ATTN, bias=shiftb[:]),
                             wait=[(sT, wsc)], inc=[(sSc, 1)], chain=False)
                        exp_done[pb_i] = sSc.v
                        we2 = sSc.v
                        P.do("tensor", lambda e, k=k, kt=kt: e.matmul(
                            PS[4].ap()[:], v_sb[k][:, kt * 128:(kt + 1) * 128], ex_b[kt % 2][:],
                            start=(kt == 0), stop=(kt == NT - 1)),
                             wait=[(sSc, we2), (sV, av_v)])
                        P.do("tensor", lambda e, kt=kt: e.matmul(
                            PS[6].ap()[0:1, :], ones_b[:], ex_b[kt % 2][:],
                            start=(kt == 0), stop=(kt == NT - 1)),
                             wait=[(sV, den_v), (sV, getattr(P, "_b4n", 0))],
                             inc=([(sT, 1)] if kt == NT - 1 else []))
                    wav = sT.v
                    if qb == 3:
                        proj_done[k] = wav
                    P.do("vector", lambda e: e.reciprocal(rec_f[0:1, :], PS[6].ap()[0:1, :]),
                         wait=[(sT, wav)], inc=[(sV, 1)], chain=False)
                    P._b4den = sV.v
                    P.do("vector", lambda e: e.tensor_copy(rec_b[0:1, :], rec_f[0:1, :]), inc=[(sV, 1)],
                         chain=False)
                    wrec = sV.v
                    P.do("tensor", lambda e: e.matmul(PS[5].ap()[:], onesr_b[:], rec_b[0:1, :],
                                                      start=True, stop=True),
                         wait=[(sV, wrec)], inc=[(sT, 1)])
                    wbc2 = sT.v
                    P.do("vector", lambda e: e.tensor_copy(rb_f[:], PS[5].ap()[:]),
                         wait=[(sT, wbc2)], inc=[(sV, 1)], chain=False)
                    P._b4s = sV.v
                    P.do("vector", lambda e, qb=qb: e.tensor_tensor(
                        ao_b[qb % 2][:], PS[4].ap()[:], rb_f[:], ALU.mult),
                         wait=[(sS2[qb % 2], ao_st[qb % 2])], inc=[(sV, 1)], chain=False)
                    P._b4ao = sV.v
                    wao = sV.v
                    P.do("gpsimd", lambda e, h=h, qb=qb: e.dma_start(
                        ao_scr[h * 128:(h + 1) * 128, qb * 512:(qb + 1) * 512], ao_b[qb % 2][:]),
                         wait=[(sV, wao)], inc=[(sS2[qb % 2], 16)])
                    ao_st[qb % 2] = sS2[qb % 2].v
            P.barrier()
        rstk.close()

        with ExitStack() as ph:
            def psb(n, s_, dt=F32):
                return ph.enter_context(nc.sbuf_tensor(n, s_, dt))

            Wo_sb = psb("Wo_sb", [128, ND * 2048], BF16)
            aoL = psb("aoL", [128, H * 2048], BF16)
            x_sb = [psb(f"b5x{k}", [128, D]) for k in range(2)]
            g0_sb = [psb(f"b5g{k}", [128, D], BF16) for k in range(2)]
            sg_f = psb("sg_f", [128, 512])
            x1_sb = [psb(f"b5o{k}", [128, D]) for k in range(2)]

            P.do("gpsimd", lambda e: e.dma_start(
                Wo_sb[:].rearrange("p (j c) -> p j c", j=ND), Wout_d.rearrange("(j p) c -> p j c", p=128)),
                 inc=[(sWg, 16)])
            P.do("sync", lambda e: e.dma_start(
                aoL[:].rearrange("p (j c) -> p j c", j=H), ao_scr.rearrange("(j p) c -> p j c", p=128)),
                 inc=[(sW, 16)])
            wW = sWg.v
            wao2 = sW.v
            st_x1 = [0, 0]
            for i in range(NT):
                P.do("sync", lambda e, i=i: e.dma_start(x_sb[i % 2][:], x_in[i * 128:(i + 1) * 128, :]),
                     wait=[(sS2[i % 2], st_x1[i % 2])], inc=[(sL[i % 2], 16)])
                P.do("sync", lambda e, i=i: e.dma_start(g0_sb[i % 2][:], g0_scr[i * 128:(i + 1) * 128, :]),
                     inc=[(sL[i % 2], 16)])
                wx = sL[i % 2].v
                for nb in range(4):
                    pb = PS[2 + (nb % 2)]
                    for ct in range(H):
                        P.do("tensor", lambda e, i=i, nb=nb, ct=ct, pb=pb: e.matmul(
                            pb.ap()[:], aoL[:, ct * 2048 + i * 128: ct * 2048 + (i + 1) * 128],
                            Wo_sb[:, ct * 2048 + nb * 512: ct * 2048 + (nb + 1) * 512],
                            start=(ct == 0), stop=(ct == H - 1)),
                             wait=[(sWg, wW), (sW, wao2), (sV, getattr(P, f"_b5p{nb % 2}", 0))],
                             inc=([(sT, 1)] if ct == H - 1 else []))
                    wmm = sT.v
                    P.do("vector", lambda e, i=i, nb=nb, pb=pb: e.tensor_tensor(
                        sg_f[:], pb.ap()[:], g0_sb[i % 2][:, nb * 512:(nb + 1) * 512], ALU.mult),
                         wait=[(sT, wmm), (sL[i % 2], wx)], inc=[(sV, 1)], chain=False)
                    setattr(P, f"_b5p{nb % 2}", sV.v)
                    P.do("vector", lambda e, i=i, nb=nb: e.tensor_tensor(
                        x1_sb[i % 2][:, nb * 512:(nb + 1) * 512], sg_f[:],
                        x_sb[i % 2][:, nb * 512:(nb + 1) * 512], ALU.add),
                         inc=([(sV, 1)] if nb == 3 else []), chain=False)
                wx1 = sV.v
                P.do("gpsimd", lambda e, i=i: e.dma_start(x1_scr[i * 128:(i + 1) * 128, :], x1_sb[i % 2][:]),
                     wait=[(sV, wx1)], inc=[(sS2[i % 2], 16)])
                st_x1[i % 2] = sS2[i % 2].v
            P.barrier()

        qT = rstk.enter_context(nc.sbuf_tensor("qT", [128, ND * S], BF16, side="right"))
        with ExitStack() as ph:
            def psb(n, s_, dt=F32):
                return ph.enter_context(nc.sbuf_tensor(n, s_, dt))

            x1_b = [psb(f"b6x{k}", [128, D]) for k in range(2)]
            xn_f = psb("b6xn", [128, D])
            sq_f = psb("b6sq", [128, D])
            scr_f = psb("b6scr", [128, D])
            q_b = psb("b6q", [128, D], BF16)
            strip = psb("b6strip", [128, ND * 128])
            strip2 = psb("b6strip2", [128, ND * 128])
            Wx_sb = psb("Wx_sb", [128, ND * 96])
            Wdt_sb = psb("Wdt_sb", [128, D])
            del_f = psb("del_f", [128, 65])
            delT = psb("delT", [128, 128])
            dt_f = psb("dt_f", [128, D])
            bt6 = psb("bt6", [32, 128])

            P.do("sync", lambda e: e.dma_start(
                Wx_sb[:].rearrange("p (j c) -> p j c", j=ND), Wx_d.rearrange("(j p) c -> p j c", p=128)),
                 inc=[(sW, 16)])
            P.do("sync", lambda e: e.dma_start(Wdt_sb[0:64, :], Wdt_d), inc=[(sW, 16)])
            P.do("sync", lambda e: e.dma_start(Wdt_sb[64:65, :], bdt_d.rearrange("(o c) -> o c", c=DI)),
                 inc=[(sW, 16)])
            wwx = sW.v
            P.do("vector", lambda e: e.memset(del_f[:, 64:65], 1.0))
            tile_done = [0] * NT
            st_u = 0
            st_dt = 0
            xn_lastT = 0
            wq = 0
            for i in range(NT):
                P.do("sync", lambda e, i=i: e.dma_start(x1_b[i % 2][:], x1_scr[i * 128:(i + 1) * 128, :]),
                     wait=([(sSc, tile_done[i - 2])] if i >= 2 else []), inc=[(sL[i % 2], 16)])
                wx = sL[i % 2].v
                wxn = emit_ln(x1_b[i % 2], xn_f, c_lneps, [(sL[i % 2], wx)],
                              final_wait=[(sT, xn_lastT), (sV, wq)])
                P.do("vector", lambda e: e.reduce_max(cols[3][:], xn_f[:], axis=AX.X,
                                                      apply_absolute_value=True),
                     wait=[(sSc, wxn)], chain=False)
                P.do("vector", lambda e: e.tensor_scalar(cols[3][:], cols[3][:], Q_EPS, None, ALU.max))
                P.do("vector", lambda e, i=i: e.tensor_scalar(insc[:, i:i + 1], cols[3][:], 1.0 / 127.0,
                                                              None, ALU.mult))
                P.do("vector", lambda e: e.reciprocal(cols[4][:], cols[3][:]))
                P.do("vector", lambda e: e.tensor_scalar(cols[4][:], cols[4][:], 127.0, None, ALU.mult))
                P.do("vector", lambda e: e.tensor_scalar(scr_f[:], xn_f[:], cols[4][:], None, ALU.mult))
                P.do("vector", lambda e: e.tensor_scalar(q_b[:], scr_f[:], RND_C, -RND_C, ALU.add, ALU.add),
                     inc=[(sV, 1)], chain=False)
                wq = sV.v
                for j in range(ND):
                    transp_one(q_b[:, j * 128:(j + 1) * 128], ident_b, [(sV, wq)],
                               [qT[:, j * S + i * 128: j * S + (i + 1) * 128]])
                for j in range(ND):
                    transp_one(xn_f[:, j * 128:(j + 1) * 128], ident, [(sSc, wxn), (sSa, st_u)],
                               [strip[:, j * 128:(j + 1) * 128]])
                xn_lastT = sT.v
                wstrip = sV.v
                tile_done[i] = wxn
                for hhalf in range(2):
                    P.do("gpsimd", lambda e, i=i, hh=hhalf: e.dma_start(
                        uT_scr[0:0 + ND * NT * 128, :]
                        .rearrange("(j i2 p) t -> p j i2 t", p=128, i2=NT)[:, hh * 8:(hh + 1) * 8, i, :],
                        strip[:, hh * 8 * 128:(hh + 1) * 8 * 128].rearrange("p (j t) -> p j t", j=8)),
                         wait=[(sV, wstrip)], inc=[(sSa, 16)])
                st_u = sSa.v
                for j in range(ND):
                    P.do("tensor", lambda e, j=j: e.matmul(
                        PS[7].ap()[:, 0:96], strip[:, j * 128:(j + 1) * 128],
                        Wx_sb[:, j * 96:(j + 1) * 96], start=(j == 0), stop=(j == ND - 1)),
                         wait=[(sW, wwx), (sV, wstrip), (sV, getattr(P, "_b6xz", 0))],
                         inc=([(sT, 1)] if j == ND - 1 else []))
                wxz = sT.v
                P.do("vector", lambda e: e.tensor_copy(del_f[:, 0:64], PS[7].ap()[:, 0:64]),
                     wait=[(sT, wxz)], chain=False)
                P.do("vector", lambda e, i=i: e.tensor_copy(bc_sb[:, i * 32:(i + 1) * 32],
                                                            PS[7].ap()[:, 64:96]), inc=[(sV, 1)], chain=False)
                P._b6xz = sV.v
                wdel = sV.v
                transp_one(bc_sb[:, i * 32:(i + 1) * 32], ident,
                           [(sV, wdel), (sSe, getattr(P, "_b6bc", 0))], [bt6[:]])
                wbt6 = sV.v
                P.do("gpsimd", lambda e, i=i: e.dma_start(
                    bcT_scr[:, i * 128:(i + 1) * 128], bt6[:]),
                     wait=[(sV, wbt6)], inc=[(sSe, 16)])
                P._b6bc = sSe.v
                transp_one(del_f[:], ident, [(sV, wdel)], [delT[0:65, :]])
                wdelT = sV.v
                for nb in range(4):
                    pb = PS[2 + (nb % 2)]
                    P.do("tensor", lambda e, nb=nb, pb=pb: e.matmul(
                        pb.ap()[:], delT[0:65, :], Wdt_sb[0:65, nb * 512:(nb + 1) * 512],
                        start=True, stop=True),
                         wait=[(sW, wwx), (sV, wdelT), (sSc, getattr(P, f"_b6dt{nb % 2}", 0))],
                         inc=[(sT, 1)])
                    wdt = sT.v
                    P.do("scalar", lambda e, nb=nb, pb=pb: e.activation(
                        scr_f[:, 0:512], pb.ap()[:], ACT.Exp),
                         wait=[(sT, wdt), (sV, wq)], inc=[(sSc, 1)], chain=False)
                    setattr(P, f"_b6dt{nb % 2}", sSc.v)
                    P.do("scalar", lambda e, nb=nb: e.activation(
                        dt_f[:, nb * 512:(nb + 1) * 512], scr_f[:, 0:512], ACT.Ln, bias=c_one[:]),
                         inc=[(sSc, 1)], chain=False)
                wsp = sSc.v
                for j in range(ND):
                    transp_one(dt_f[:, j * 128:(j + 1) * 128], ident, [(sSc, wsp), (sSb, st_dt)],
                               [strip2[:, j * 128:(j + 1) * 128]])
                wst2 = sV.v
                for hhalf in range(2):
                    P.do("gpsimd", lambda e, i=i, hh=hhalf: e.dma_start(
                        dtT_scr.rearrange("(j i2 p) t -> p j i2 t", p=128, i2=NT)[:, hh * 8:(hh + 1) * 8, i, :],
                        strip2[:, hh * 8 * 128:(hh + 1) * 8 * 128].rearrange("p (j t) -> p j t", j=8)),
                         wait=[(sV, wst2)], inc=[(sSb, 16)])
                st_dt = sSb.v
            P.barrier()

        with ExitStack() as ph:
            def psb(n, s_, dt=F32):
                return ph.enter_context(nc.sbuf_tensor(n, s_, dt))

            w8_sb = [psb(f"w8s{k}", [128, 4 * D], I8) for k in range(2)]
            w8b = [psb(f"w8b{k}", [128, 4 * D], BF16) for k in range(2)]
            wT = [psb(f"wT{k}", [128, ND * 512], BF16) for k in range(2)]
            wsc_bc = psb("wsc_bc", [128, D])
            bias_bc = psb("bias_bc", [128, D])
            dq_f = psb("dq_f", [128, 512])
            ff_h = psb("ff_h", [128, 512], F16)
            fp_b = psb("fp_b", [128, 4 * 128], F16)

            P.do("gpsimd", lambda e: e.dma_start(
                wsc_bc[:], w_scale_d.rearrange("c o -> o c").partition_broadcast(128)
                .rearrange("p o c -> p (o c)")), inc=[(sWg, 16)])
            P.do("gpsimd", lambda e: e.dma_start(
                bias_bc[:], bias8_d.rearrange("(o c) -> o c", c=D).partition_broadcast(128)
                .rearrange("p o c -> p (o c)")), inc=[(sWg, 16)])
            wrows = sWg.v
            P.do("vector", lambda e: None, wait=[(sWg, wrows)], inc=[(sV, 1)])

            wt_pe_done = [0, 0]
            st_ff = 0
            for oc in range(4):
                k = oc % 2
                P.do("sync", lambda e, oc=oc, k=k: e.dma_start(
                    w8_sb[k][:].rearrange("p (o c) -> p o c", o=4),
                    w8_d[oc * 512:(oc + 1) * 512, :].rearrange("(o p) c -> p o c", p=128)),
                     wait=[(sT, wt_pe_done[k])], inc=[(sL[k], 16)])
                wld = sL[k].v
                for ot in range(4):
                    P.do("vector", lambda e, k=k, ot=ot: e.tensor_copy(
                        w8b[k][:, ot * D:(ot + 1) * D], w8_sb[k][:, ot * D:(ot + 1) * D]),
                         wait=[(sL[k], wld)], inc=[(sV, 1)], chain=False)
                wup = sV.v
                for ot in range(4):
                    for j in range(ND):
                        transp_one(w8b[k][:, ot * D + j * 128: ot * D + (j + 1) * 128],
                                   ident_b, [(sV, wup), (sT, wt_pe_done[k])],
                                   [wT[k][:, j * 512 + ot * 128: j * 512 + (ot + 1) * 128]])
                wtr = sV.v
                for i in range(NT):
                    pb = PS[2 + (i % 2)]
                    for j in range(ND):
                        P.do("tensor", lambda e, i=i, j=j, k=k, pb=pb: e.matmul(
                            pb.ap()[:], qT[:, j * S + i * 128: j * S + (i + 1) * 128],
                            wT[k][:, j * 512:(j + 1) * 512], start=(j == 0), stop=(j == ND - 1)),
                             wait=[(sV, wtr), (sSc, getattr(P, f"_b7p{i % 2}", 0))],
                             inc=([(sT, 1)] if j == ND - 1 else []))
                    wmm = sT.v
                    if i == NT - 1:
                        wt_pe_done[k] = wmm
                    P.do("scalar", lambda e, i=i, pb=pb: e.activation(
                        dq_f[:], pb.ap()[:], ACT.Copy, scale=insc[:, i:i + 1]),
                         wait=[(sT, wmm), (sV, getattr(P, "_b7ff", 0))], inc=[(sSc, 1)], chain=False)
                    setattr(P, f"_b7p{i % 2}", sSc.v)
                    wdq = sSc.v
                    P.do("vector", lambda e, oc=oc: e.tensor_tensor(
                        dq_f[:], dq_f[:], wsc_bc[:, oc * 512:(oc + 1) * 512], ALU.mult),
                         wait=[(sSc, wdq)], chain=False)
                    P.do("vector", lambda e, oc=oc: e.tensor_tensor(
                        ff_h[:], dq_f[:], bias_bc[:, oc * 512:(oc + 1) * 512], ALU.add),
                         inc=[(sV, 1)], chain=False)
                    P._b7ff = sV.v
                    wffh = sV.v
                    for ot in range(4):
                        transp_one(ff_h[:, ot * 128:(ot + 1) * 128], ident_h,
                                   [(sV, wffh), (sSa, st_ff)],
                                   [fp_b[:, ot * 128:(ot + 1) * 128]])
                    wev = sV.v
                    P.do("gpsimd", lambda e, i=i, oc=oc: e.dma_start(
                        ffnT_scr.rearrange("(j i2 p) t -> p j i2 t", p=128, i2=NT)
                        [:, oc * 4:(oc + 1) * 4, i, :],
                        fp_b[:].rearrange("p (o t) -> p o t", o=4)),
                         wait=[(sV, wev)], inc=[(sSa, 16)])
                    st_ff = sSa.v
            P.barrier()
        rstk.close()

        with ExitStack() as ph:
            def psb(n, s_, dt=F32):
                return ph.enter_context(nc.sbuf_tensor(n, s_, dt))

            al_sb = psb("al_sb", [128, ND * DS])
            negAn = psb("negAn", [128, ND * DS])
            dp_sb = psb("dp_sb", [128, ND])
            Bb = psb("Bb", [128, 16 * TCH])
            Cb = psb("Cb", [128, 16 * TCH])
            carry = psb("carry", [128, ND * DS])
            dtb = [psb(f"dtb{k}", [128, TCH]) for k in range(2)]
            ub = [psb(f"ub{k}", [128, TCH]) for k in range(2)]
            du_f = psb("du_f", [128, TCH])
            ab_f = [psb(f"ab{k}", [128, TCH]) for k in range(2)]
            inp2 = [psb(f"inp{k}", [128, TCH]) for k in range(2)]
            tmp2 = [psb(f"tmp{k}", [128, TCH]) for k in range(2)]
            h_all = psb("h_all", [128, DS * TCH])
            y_f = psb("y_f", [128, TCH])
            yg_f = psb("yg_f", [128, TCH])
            g_f = psb("g_f", [128, TCH])
            ffh = psb("ffh", [128, TCH], F16)
            e_f = psb("e_f", [128, TCH])
            r_scr = psb("r_scr", [128, TCH])
            x1t = [psb(f"b8x{k}", [128, D]) for k in range(4)]
            out_sb = [psb(f"b8o{k}", [128, D]) for k in range(4)]

            P.do("sync", lambda e: e.dma_start(
                al_sb[:].rearrange("p (j n) -> p j n", j=ND),
                A_log_d.rearrange("(j p) n -> p j n", p=128)), inc=[(sW, 16)])
            P.do("sync", lambda e: e.dma_start(
                dp_sb[:], Dp_d.rearrange("(j p) -> p j", p=128), allow_slow_non_contiguous=True),
                 inc=[(sW, 16)])
            wal = sW.v
            P.do("scalar", lambda e: e.activation(negAn[:], al_sb[:], ACT.Exp), wait=[(sW, wal)],
                 inc=[(sSc, 1)])
            wna = sSc.v
            P.do("vector", lambda e: e.tensor_scalar(negAn[:], negAn[:], -1.0, None, ALU.mult),
                 wait=[(sSc, wna)], inc=[(sV, 1)])
            wna2 = sV.v

            out_prev = 0
            ld_sc = [0, 0]
            ld_gp = [0, 0]
            ld_v = [0, 0]
            scan_used = [0, 0]
            tmp_used = [0, 0]
            carry_sc = [0] * 16
            for c in range(NCH):
                useB = getattr(P, "_b8useB", 0)
                useC = getattr(P, "_b8useC", 0)
                for (boff, dst, wt) in ((0, Bb, [(sP, useB)]), (DS, Cb, [(sV, useC)])):
                    P.do("sync", lambda e, c=c, boff=boff, dst=dst: e.dma_start(
                        dst[:].rearrange("p (n t) -> p n t", n=16),
                        bcT_scr[boff:boff + DS, c * TCH:(c + 1) * TCH].partition_broadcast(128)),
                         wait=wt, inc=[(sLe, 16)])
                wbcd = sLe.v
                for ii in range(4):
                    P.do("sync", lambda e, c=c, ii=ii: e.dma_start(
                        x1t[ii][:], x1_scr[(c * 4 + ii) * 128:(c * 4 + ii + 1) * 128, :]),
                         wait=[(sSd, out_prev)], inc=[(sLc, 16)])
                wx1 = sLc.v
                for j in range(ND):
                    k = j % 2
                    blkr = (j * NT + c * 4) * 128
                    P.do("sync", lambda e, blkr=blkr, k=k: e.dma_start(
                        dtb[k][:].rearrange("p (i2 t) -> p i2 t", i2=4),
                        dtT_scr[blkr:blkr + 512, :].rearrange("(i2 p) t -> p i2 t", p=128)),
                         wait=[(sSc, ld_sc[k]), (sP, ld_gp[k]), (sV, ld_v[k])],
                         inc=[(sL[k], 16)])
                    P.do("sync", lambda e, blkr=blkr, k=k: e.dma_start(
                        ub[k][:].rearrange("p (i2 t) -> p i2 t", i2=4),
                        uT_scr[blkr:blkr + 512, :].rearrange("(i2 p) t -> p i2 t", p=128)),
                         inc=[(sL[k], 16)])
                    wld = sL[k].v
                    P.do("sync", lambda e, blkr=blkr: e.dma_start(
                        ffh[:].rearrange("p (i2 t) -> p i2 t", i2=4),
                        ffnT_scr[blkr:blkr + 512, :].rearrange("(i2 p) t -> p i2 t", p=128)),
                         wait=[(sV, getattr(P, "_b8g", 0))], inc=[(sLd, 16)])
                    wff = sLd.v
                    P.do("vector", lambda e: e.tensor_scalar(e_f[:], ffh[:], -1.0, 80.0, ALU.mult, ALU.min),
                         wait=[(sLd, wff)], inc=[(sV, 1)], chain=False)
                    wneg8 = sV.v
                    P.do("scalar", lambda e: e.activation(e_f[:], e_f[:], ACT.Exp),
                         wait=[(sV, wneg8)], inc=[(sSc, 1)], chain=False)
                    wsg = sSc.v
                    P.do("vector", lambda e: e.tensor_scalar(e_f[:], e_f[:], 1.0, None, ALU.add),
                         wait=[(sSc, wsg)], chain=False)
                    P.do("vector", lambda e: e.reciprocal(e_f[:], e_f[:]), chain=False)
                    P.do("vector", lambda e: e.tensor_tensor(g_f[:], ffh[:], e_f[:], ALU.mult),
                         inc=[(sV, 1)], chain=False)
                    P._b8g = sV.v
                    P.do("gpsimd", lambda e, k=k: e.tensor_tensor(du_f[:], dtb[k][:], ub[k][:], ALU.mult),
                         wait=[(sL[k], wld), (sLe, wbcd)], inc=[(sP, 1)])
                    for n in range(2):
                        P.do("gpsimd", lambda e, n=n: e.tensor_tensor(
                            inp2[n][:], du_f[:], Bb[:, n * TCH:(n + 1) * TCH], ALU.mult),
                             wait=[(sV, scan_used[n])], inc=[(sP, 1)])
                        setattr(P, f"_b8i{n}", sP.v)
                    cm_done = {}
                    scan_v = {}
                    for n in range(16):
                        kn = n % 2
                        P.do("scalar", lambda e, j=j, n=n, k=k, kn=kn: e.activation(
                            ab_f[kn][:], dtb[k][:], ACT.Exp, scale=negAn[:, j * DS + n: j * DS + n + 1]),
                             wait=[(sL[k], wld), (sV, wna2), (sV, scan_used[kn])],
                             inc=[(sSc, 1)], chain=False)
                        wab = sSc.v
                        ini = 0.0 if c == 0 else carry[:, j * DS + n: j * DS + n + 1]
                        P.do("vector", lambda e, n=n, kn=kn, ini=ini: e.tensor_tensor_scan(
                            h_all[:, n * TCH:(n + 1) * TCH], ab_f[kn][:], inp2[kn][:], ini,
                            ALU.mult, ALU.add),
                             wait=[(sSc, wab), (sP, getattr(P, f"_b8i{kn}", 0)),
                                   (sSc, carry_sc[n])],
                             inc=[(sV, 1)], chain=False)
                        scan_used[kn] = sV.v
                        scan_v[n] = sV.v
                        if n >= 2:
                            P.do("scalar", lambda e, j=j, n2=n - 2: e.activation(
                                carry[:, j * DS + n2: j * DS + n2 + 1],
                                h_all[:, (n2 + 1) * TCH - 1:(n2 + 1) * TCH], ACT.Copy),
                                 wait=[(sV, scan_v[n - 2])], inc=[(sSc, 1)], chain=False)
                            carry_sc[n - 2] = sSc.v
                        if n + 2 < 16:
                            P.do("gpsimd", lambda e, n=n, kn=kn: e.tensor_tensor(
                                inp2[kn][:], du_f[:], Bb[:, (n + 2) * TCH:(n + 3) * TCH], ALU.mult),
                                 wait=[(sV, scan_used[kn])], inc=[(sP, 1)])
                            setattr(P, f"_b8i{kn}", sP.v)
                        P.do("vector", lambda e, n=n, kn=kn: e.tensor_tensor(
                            tmp2[kn][:], h_all[:, n * TCH:(n + 1) * TCH],
                            Cb[:, n * TCH:(n + 1) * TCH], ALU.mult),
                             wait=[(sP, tmp_used[kn])], inc=[(sV, 1)], chain=False)
                        cm_done[n] = sV.v
                        if n == 0:
                            P.do("gpsimd", lambda e: e.tensor_copy(y_f[:], tmp2[0][:]),
                                 wait=[(sV, cm_done[0]), (sV, getattr(P, "_b8yg", 0))],
                                 inc=[(sP, 1)])
                        else:
                            P.do("gpsimd", lambda e, kn=kn: e.tensor_tensor(
                                y_f[:], y_f[:], tmp2[kn][:], ALU.add),
                                 wait=[(sV, cm_done[n])], inc=[(sP, 1)])
                        tmp_used[kn] = sP.v
                    P._b8useC = cm_done[15]
                    for n2 in (14, 15):
                        P.do("scalar", lambda e, j=j, n2=n2: e.activation(
                            carry[:, j * DS + n2: j * DS + n2 + 1],
                            h_all[:, (n2 + 1) * TCH - 1:(n2 + 1) * TCH], ACT.Copy),
                             wait=[(sV, scan_v[n2])], inc=[(sSc, 1)], chain=False)
                        carry_sc[n2] = sSc.v
                    P.do("gpsimd", lambda e: e.engine_nop(), inc=[(sP, 1)])
                    ld_gp[k] = sP.v
                    ld_sc[k] = sSc.v
                    wyp = sP.v
                    P._b8useB = sP.v
                    P.do("vector", lambda e, j=j, k=k: e.scalar_tensor_tensor(
                        y_f[:], ub[k][:], dp_sb[:, j:j + 1], y_f[:], ALU.mult, ALU.add),
                         wait=[(sP, wyp)], inc=[(sV, 1)], chain=False)
                    ld_v[k] = sV.v
                    P.do("vector", lambda e: e.tensor_tensor(yg_f[:], y_f[:], g_f[:], ALU.mult),
                         inc=[(sV, 1)], chain=False)
                    P._b8yg = sV.v
                    wdel2 = sV.v
                    for ii in range(4):
                        def evac(e, pap, ii=ii, j=j):
                            return e.tensor_tensor(out_sb[ii][:, j * 128:(j + 1) * 128], pap,
                                                   x1t[ii][:, j * 128:(j + 1) * 128], ALU.add)
                        transp_one(yg_f[:, ii * 128:(ii + 1) * 128], ident,
                                   [(sV, wdel2), (sLc, wx1), (sSd, out_prev)], evac=evac)
                wout = sV.v
                for ii in range(4):
                    P.do("scalar", lambda e, c=c, ii=ii: e.dma_start(
                        y_out[(c * 4 + ii) * 128:(c * 4 + ii + 1) * 128, :], out_sb[ii][:]),
                         wait=[(sV, wout)], inc=[(sSd, 16)])
                out_prev = sSd.v
            P.do("gpsimd", lambda e: None, wait=[(sSd, sSd.v)])
            P.barrier()

    nc.finalize()
    return nc


_CACHE = {}


def _consts():
    ident = np.eye(128, dtype=np.float32)
    iota8 = np.tile(np.arange(NE, dtype=np.float32), (128, 1))
    freqs = 1.0 / (10000.0 ** (np.arange(0, RD, 2, dtype=np.float32) / RD))
    t = np.arange(S, dtype=np.float32)
    ang = np.outer(t, freqs)
    cosr = np.tile(np.cos(ang).astype(np.float32), (1, H))
    sinr = np.tile(np.sin(ang).astype(np.float32), (1, H))
    return ident, iota8, cosr, sinr


def kernel(**inputs):
    if "nc" not in _CACHE:
        _CACHE["nc"] = build_program()
    nc = _CACHE["nc"]
    ident, iota8, cosr, sinr = _consts()
    xfull = np.ascontiguousarray(inputs["L0_attn_x"], dtype=np.float32)
    common = {
        "Wq_down": inputs["L0_attn_Wq_down"], "q_norm_w": inputs["L0_attn_q_norm_w"],
        "Wq_up": inputs["L0_attn_Wq_up"], "Wq_rope": inputs["L0_attn_Wq_rope"],
        "Wkv_down": inputs["L0_attn_Wkv_down"], "kv_norm_w": inputs["L0_attn_kv_norm_w"],
        "Wk_up": inputs["L0_attn_Wk_up"], "Wv_up": inputs["L0_attn_Wv_up"],
        "Wk_rope": inputs["L0_attn_Wk_rope"], "Wout": inputs["L0_attn_Wout"],
        "gate_w": inputs["L0_ffn_gate_w"], "expert_w": inputs["L0_ffn_expert_w"],
        "expert_b": inputs["L0_ffn_expert_b"], "Wx": inputs["L1_attn_Wx"],
        "Wdt": inputs["L1_attn_Wdt"], "bdt": inputs["L1_attn_bdt"],
        "A_log": inputs["L1_attn_A_log"], "Dp": inputs["L1_attn_Dp"],
        "w8": inputs["L1_ffn_weight"], "w_scale": inputs["L1_ffn_w_scale"],
        "bias8": inputs["L1_ffn_bias"],
        "ident": ident, "iota8": iota8, "cosrep": cosr, "sinrep": sinr,
    }
    common = {k: np.ascontiguousarray(v) for k, v in common.items()}
    in_maps = []
    for b in range(8):
        m = dict(common)
        m["x"] = np.ascontiguousarray(xfull[b])
        in_maps.append(m)
    res = run_bass_kernel_spmd(nc, in_maps, core_ids=list(range(8)))
    out = np.stack([np.asarray(res.results[b]["y"], dtype=np.float32) for b in range(8)], axis=0)
    return out


# version 11
# speedup vs baseline: 1.0860x; 1.0860x over previous
import numpy as np
import ml_dtypes
from contextlib import ExitStack

import concourse.bass as bass
import concourse.mybir as mybir
from concourse.bass_utils import run_bass_kernel_spmd

F32 = mybir.dt.float32
BF16 = mybir.dt.bfloat16
F16 = mybir.dt.float16
I8 = mybir.dt.int8
ALU = mybir.AluOpType
ACT = mybir.ActivationFunctionType
AX = mybir.AxisListType

S, D = 2048, 2048
NT, ND = S // 128, D // 128
H, HD, RD = 16, 128, 64
DQL, DKVL = 384, 512
NE = 8
DI, DS, DTR = 2048, 16, 64
SCALE_ATTN = float((HD + RD) ** -0.5)
LOG_SCALE_ATTN = float(np.log(SCALE_ATTN))
RND_C = 12582912.0
BIG = 1.0e7
TCH = 512
NCH = S // TCH
LN_EPS = 1e-5
RMS_EPS = 1e-6
Q_EPS = 1e-7

ENGS = ("sync", "gpsimd", "vector", "scalar", "tensor")
CHAIN_ENGS = ("vector",)


class Ctr:
    __slots__ = ("h", "v", "synced")

    def __init__(self, h):
        self.h = h
        self.v = 0
        self.synced = 0


class Prog:
    def __init__(self, nc):
        self.nc = nc
        self.items = []
        self.last_wait = {}
        self.nblk = 0

    def do(self, eng, fn, wait=(), inc=(), chain=None):
        w = []
        for c, val in wait:
            if val is None or val <= 0:
                continue
            key = (eng, id(c))
            if self.last_wait.get(key, -1) >= val:
                continue
            self.last_wait[key] = val
            w.append((c.h, val))
        incs = []
        for c, n in inc:
            c.v += n
            incs.append((c.h, n))
        if chain is None:
            chain = eng in ("vector", "scalar")
        self.items.append((eng, w, fn, incs, chain))

    def barrier(self):
        for c in getattr(self, "dma_ctrs", []):
            if c.v > c.synced:
                self.do("sync", None, wait=[(c, c.v)])
                c.synced = c.v
        items = self.items
        self.items = []
        self.last_wait = {}
        if not items:
            return
        self.nblk += 1
        with self.nc.Block(f"seg{self.nblk}") as blk:
            for ename in ENGS:
                mine = [it for it in items if it[0] == ename]
                if not mine:
                    continue

                def body(eng, mine=mine, ename=ename):
                    for _, waits, fn, incs, chain in mine:
                        for h, val in waits:
                            eng.wait_ge(h, val)
                        if chain and fn is not None:
                            eng.drain()
                        inst = fn(eng) if fn is not None else None
                        if incs:
                            if inst is None:
                                inst = eng.drain() if ename != "sync" else eng.nop()
                            for h, n in incs:
                                inst = inst.then_inc(h, n)
                getattr(blk, ename)(body)


def build_program():
    nc = bass.Bass("TRN2", target_bir_lowering=False)

    def din(name, shape, dt=F32):
        return nc.dram_tensor(name, shape, dt, kind="ExternalInput").ap()

    x_in = din("x", [S, D])
    Wq_down_d = din("Wq_down", [D, DQL])
    q_norm_w_d = din("q_norm_w", [DQL])
    Wq_up_d = din("Wq_up", [DQL, H * HD])
    Wq_rope_d = din("Wq_rope", [D, H * RD])
    Wkv_down_d = din("Wkv_down", [D, DKVL])
    kv_norm_w_d = din("kv_norm_w", [DKVL])
    Wk_up_d = din("Wk_up", [DKVL, H * HD])
    Wv_up_d = din("Wv_up", [DKVL, H * HD])
    Wk_rope_d = din("Wk_rope", [D, H * RD])
    Wout_d = din("Wout", [H * HD, D])
    gate_w_d = din("gate_w", [NE, D])
    expert_w_d = din("expert_w", [NE, D, D])
    expert_b_d = din("expert_b", [NE, D])
    Wx_d = din("Wx", [DI, DTR + 2 * DS])
    Wdt_d = din("Wdt", [DTR, DI])
    bdt_d = din("bdt", [DI])
    A_log_d = din("A_log", [DI, DS])
    Dp_d = din("Dp", [DI])
    w8_d = din("w8", [D, D], I8)
    w_scale_d = din("w_scale", [D, 1])
    bias8_d = din("bias8", [D])
    ident_d = din("ident", [128, 128])
    iota8_d = din("iota8", [128, NE])
    cos_d = din("cosrep", [S, H * RD // 2])
    sin_d = din("sinrep", [S, H * RD // 2])

    y_out = nc.dram_tensor("y", [S, D], F32, kind="ExternalOutput").ap()

    def dscr(name, shape, dt):
        return nc.dram_tensor(name, shape, dt, kind="Internal").ap()

    g0_scr = dscr("g0_scr", [S, D], BF16)
    ao_scr = dscr("ao_scr", [H * HD, S], BF16)
    x1_scr = dscr("x1_scr", [S, D], F32)
    qrT_scr = dscr("qrT_scr", [H * RD, S], BF16)
    krT_scr = dscr("krT_scr", [H * RD, S], BF16)
    uT_scr = dscr("uT_scr", [ND * NT * 128, 128], F32)
    dtT_scr = dscr("dtT_scr", [ND * NT * 128, 128], F32)
    ffnT_scr = dscr("ffnT_scr", [ND * NT * 128, 128], F16)
    bcT_scr = dscr("bcT_scr", [2 * DS, S], F32)

    P = Prog(nc)
    es = ExitStack()
    with es:
        def sb(name, shape, dt=F32, side=None):
            return es.enter_context(nc.sbuf_tensor(name, shape, dt, side=side))

        def mkctr(name):
            return Ctr(es.enter_context(nc.semaphore(name)))

        sW = mkctr("sW")
        sLa = mkctr("sLa")
        sLb = mkctr("sLb")
        sLc = mkctr("sLc")
        sLd = mkctr("sLd")
        sLe = mkctr("sLe")
        sSa = mkctr("sSa")
        sSb = mkctr("sSb")
        sSe = mkctr("sSe")
        sSd = mkctr("sSd")
        sWg = mkctr("sWg")
        sLg0 = mkctr("sLg0")
        sLg1 = mkctr("sLg1")
        sT = mkctr("sT")
        sV = mkctr("sV")
        sSc = mkctr("sSc")
        sP = mkctr("sP")
        sL = [sLa, sLb]
        sLg = [sLg0, sLg1]
        sS2 = [sSa, sSb]
        P.dma_ctrs = [sW, sWg, sLa, sLb, sLc, sLd, sLe, sLg0, sLg1, sSa, sSb, sSe, sSd]

        ident = sb("identsb", [128, 128])
        ident_b = sb("ident_b", [128, 128], BF16)
        ident_h = sb("ident_h", [128, 128], F16)
        iota8 = sb("iota8sb", [128, NE])
        ones_b = sb("ones_b", [128, 1], BF16)
        onesr_b = sb("onesr_b", [1, 128], BF16)
        onesr_f = sb("onesr_f", [1, 128])
        wdense = sb("wdense", [128, NT * NE])
        wdenseT = sb("wdenseT", [8, NT * 128], BF16)
        insc = sb("insc", [128, NT])
        bc_sb = sb("bc_sb", [128, NT * 2 * DS])
        cols = [sb(f"col{i}", [128, 1]) for i in range(8)]
        c_lneps = sb("c_lneps", [128, 1])
        c_rmseps = sb("c_rmseps", [128, 1])
        c_lsa = sb("c_lsa", [128, 1])
        c_one = sb("c_one", [128, 1])
        rows8 = [sb(f"row8{i}", [128, NE]) for i in range(2)]

        PS = [es.enter_context(nc.psum_tensor(f"ps{i}", [128, 512], F32)) for i in range(8)]

        trb = {"bank": 0, 0: 0, 1: 0}

        def transp_one(src, idm, waits, dsts=(), evac=None, evac_inc=()):
            p, w = src.shape
            b = trb["bank"]
            trb["bank"] = 1 - b
            pb = PS[b]
            dt_ = src.dtype
            P.do("tensor",
                 lambda e, src=src, pb=pb, p=p, w=w, idm=idm, dt_=dt_:
                 e.transpose(pb.ap().bitcast(dt_)[0:w, 0:p], src, idm[0:p, 0:p]),
                 wait=list(waits) + [(sV, trb[b])], inc=[(sT, 1)])
            t = sT.v
            if evac is not None:
                P.do("vector", lambda e, pb=pb, p=p, w=w, dt_=dt_: evac(e, pb.ap().bitcast(dt_)[0:w, 0:p]),
                     wait=[(sT, t)], inc=[(sV, 1)] + list(evac_inc), chain=False)
            else:
                nd = len(dsts)
                for ii, dst in enumerate(dsts):
                    P.do("vector",
                         lambda e, dst=dst, pb=pb, p=p, w=w, dt_=dt_:
                         e.tensor_copy(dst, pb.ap().bitcast(dt_)[0:w, 0:p]),
                         wait=[(sT, t)],
                         inc=([(sV, 1)] + list(evac_inc)) if ii == nd - 1 else (),
                         chain=False)
            trb[b] = sV.v
            return sV.v

        P.do("sync", lambda e: e.dma_start(ident[:], ident_d), inc=[(sW, 16)])
        P.do("sync", lambda e: e.dma_start(iota8[:], iota8_d), inc=[(sW, 16)])
        w0 = sW.v
        P.do("vector", lambda e: e.tensor_copy(ident_b[:], ident[:]), wait=[(sW, w0)])
        P.do("vector", lambda e: e.tensor_copy(ident_h[:], ident[:]))
        P.do("vector", lambda e: e.memset(ones_b[:], 1.0))
        P.do("vector", lambda e: e.memset(onesr_b[:], 1.0))
        P.do("vector", lambda e: e.memset(onesr_f[:], 1.0))
        P.do("vector", lambda e: e.memset(c_lneps[:], LN_EPS))
        P.do("vector", lambda e: e.memset(c_rmseps[:], RMS_EPS))
        P.do("vector", lambda e: e.memset(c_lsa[:], LOG_SCALE_ATTN))
        P.do("vector", lambda e: e.memset(c_one[:], 1.0))
        P.barrier()

        rstk = ExitStack()
        xnT = rstk.enter_context(nc.sbuf_tensor("xnT", [128, ND * S], BF16, side="right"))

        def emit_ln(buf, out_f, eps, done_ctr_wait, final_wait=()):
            P.do("vector", lambda e, buf=buf: e.tensor_scalar(
                sq_f[:], buf[:], 1.0, None, ALU.mult, ALU.add, accum_out=cols[0][:]),
                 wait=done_ctr_wait, inc=[(sV, 1)])
            vsum = sV.v
            P.do("scalar", lambda e, buf=buf: e.activation(sq_f[:], buf[:], ACT.Square,
                                                           accum_out=cols[1][:]),
                 wait=done_ctr_wait + [(sV, vsum)], inc=[(sSc, 1)])
            wsq = sSc.v
            P.do("vector", lambda e: e.tensor_scalar(cols[0][:], cols[0][:], 1.0 / D, None, ALU.mult),
                 wait=[(sSc, wsq)])
            P.do("vector", lambda e: e.tensor_scalar(cols[1][:], cols[1][:], 1.0 / D, None, ALU.mult))
            P.do("vector", lambda e: e.tensor_tensor(cols[2][:], cols[0][:], cols[0][:], ALU.mult))
            P.do("vector", lambda e: e.tensor_tensor(cols[1][:], cols[1][:], cols[2][:], ALU.subtract),
                 inc=[(sV, 1)])
            wvar = sV.v
            P.do("scalar", lambda e, eps=eps: e.activation(cols[1][:], cols[1][:], ACT.Ln, bias=eps[:]),
                 wait=[(sV, wvar)])
            P.do("scalar", lambda e: e.activation(cols[1][:], cols[1][:], ACT.Exp, scale=-0.5),
                 inc=[(sSc, 1)])
            wrs = sSc.v
            P.do("vector", lambda e: e.tensor_scalar(cols[2][:], cols[0][:], cols[1][:], -1.0,
                                                     ALU.mult, ALU.mult),
                 wait=[(sSc, wrs)], inc=[(sV, 1)])
            wnm = sV.v
            P.do("scalar", lambda e, buf=buf, out_f=out_f: e.activation(
                out_f[:], buf[:], ACT.Identity, bias=cols[2][:], scale=cols[1][:]),
                 wait=[(sV, wnm)] + list(final_wait), inc=[(sSc, 1)])
            return sSc.v

        with ExitStack() as ph:
            def psb(n, s_, dt=F32):
                return ph.enter_context(nc.sbuf_tensor(n, s_, dt))

            gw_sb = psb("gw_sb", [8, D])
            gwT = psb("gwT", [128, ND * NE])
            x_b = [psb(f"x{k}", [128, D]) for k in range(2)]
            xn_f = psb("xn_f", [128, D])
            sq_f = psb("sq_f", [128, D])
            strip = psb("strip", [128, ND * 128])
            lg = psb("lg", [128, NE])

            P.do("sync", lambda e: e.dma_start(gw_sb[:], gate_w_d), inc=[(sW, 16)])
            wgw = sW.v
            for j in range(ND):
                transp_one(gw_sb[0:8, j * 128:(j + 1) * 128], ident, [(sW, wgw)],
                           [gwT[:, j * NE:(j + 1) * NE]])
            tile_done = [0] * NT
            xn_lastT = 0
            for i in range(NT):
                buf = x_b[i % 2]
                P.do("sync", lambda e, buf=buf, i=i: e.dma_start(buf[:], x_in[i * 128:(i + 1) * 128, :]),
                     wait=([(sSc, tile_done[i - 2])] if i >= 2 else []), inc=[(sL[i % 2], 16)])
                wx = sL[i % 2].v
                wxn = emit_ln(buf, xn_f, c_lneps, [(sL[i % 2], wx)], final_wait=[(sT, xn_lastT)])
                tile_done[i] = wxn
                for j in range(ND):
                    transp_one(xn_f[:, j * 128:(j + 1) * 128], ident, [(sSc, wxn)],
                               [strip[:, j * 128:(j + 1) * 128],
                                xnT[:, j * S + i * 128: j * S + (i + 1) * 128]])
                xn_lastT = sT.v
                wstrip = sV.v
                for j in range(ND):
                    P.do("tensor", lambda e, j=j: e.matmul(
                        PS[6].ap()[:, 0:NE], strip[:, j * 128:(j + 1) * 128],
                        gwT[:, j * NE:(j + 1) * NE], start=(j == 0), stop=(j == ND - 1)),
                         wait=[(sV, wstrip)], inc=([(sT, 1)] if j == ND - 1 else []))
                wlg = sT.v
                P.do("vector", lambda e: e.tensor_copy(lg[:], PS[6].ap()[:, 0:NE]),
                     wait=[(sT, wlg)], inc=[(sV, 1)])
                P.do("vector", lambda e: e.tensor_reduce(cols[0][:], lg[:], AX.X, ALU.max))
                P.do("vector", lambda e: e.tensor_scalar(rows8[0][:], lg[:], cols[0][:], None, ALU.is_equal))
                P.do("vector", lambda e: e.tensor_scalar(rows8[0][:], rows8[0][:], -BIG, BIG, ALU.mult, ALU.add))
                P.do("vector", lambda e: e.tensor_tensor(rows8[0][:], iota8[:], rows8[0][:], ALU.add))
                P.do("vector", lambda e: e.tensor_reduce(cols[1][:], rows8[0][:], AX.X, ALU.min))
                P.do("vector", lambda e: e.tensor_scalar(rows8[0][:], iota8[:], cols[1][:], None, ALU.is_equal))
                P.do("vector", lambda e: e.scalar_tensor_tensor(rows8[1][:], rows8[0][:], -BIG, lg[:],
                                                                ALU.mult, ALU.add))
                P.do("vector", lambda e: e.tensor_reduce(cols[2][:], rows8[1][:], AX.X, ALU.max))
                P.do("vector", lambda e: e.tensor_scalar(rows8[1][:], rows8[1][:], cols[2][:], None, ALU.is_equal))
                P.do("vector", lambda e: e.tensor_scalar(rows8[1][:], rows8[1][:], -BIG, BIG, ALU.mult, ALU.add))
                P.do("vector", lambda e: e.tensor_tensor(rows8[1][:], iota8[:], rows8[1][:], ALU.add))
                P.do("vector", lambda e: e.tensor_reduce(cols[3][:], rows8[1][:], AX.X, ALU.min))
                P.do("vector", lambda e: e.tensor_scalar(rows8[1][:], iota8[:], cols[3][:], None, ALU.is_equal))
                P.do("vector", lambda e: e.tensor_tensor(cols[4][:], cols[2][:], cols[0][:], ALU.subtract),
                     inc=[(sV, 1)])
                wd = sV.v
                P.do("scalar", lambda e: e.activation(cols[4][:], cols[4][:], ACT.Exp),
                     wait=[(sV, wd)], inc=[(sSc, 1)])
                we = sSc.v
                P.do("vector", lambda e: e.tensor_scalar(cols[5][:], cols[4][:], 1.0, None, ALU.add),
                     wait=[(sSc, we)])
                P.do("vector", lambda e: e.reciprocal(cols[5][:], cols[5][:]))
                P.do("vector", lambda e: e.tensor_tensor(cols[6][:], cols[4][:], cols[5][:], ALU.mult))
                P.do("vector", lambda e: e.tensor_scalar(rows8[0][:], rows8[0][:], cols[5][:], None, ALU.mult))
                P.do("vector", lambda e, i=i: e.scalar_tensor_tensor(
                    wdense[:, i * NE:(i + 1) * NE], rows8[1][:], cols[6][:], rows8[0][:],
                    ALU.mult, ALU.add), inc=[(sV, 1)])
                wwd = sV.v
                transp_one(wdense[:, i * NE:(i + 1) * NE], ident, [(sV, wwd)],
                           [wdenseT[:, i * 128:(i + 1) * 128]])
            P.barrier()

        with ExitStack() as ph:
            def psb(n, s_, dt=F32):
                return ph.enter_context(nc.sbuf_tensor(n, s_, dt))

            eb_sb = psb("eb_sb", [8, D], BF16)
            Wsb = [psb(f"Wsb{k}", [128, 4 * D], BF16) for k in range(2)]
            WeT = [psb(f"WeT{k}", [128, ND * 512], BF16) for k in range(2)]
            acc = psb("acc", [128, NT * 512], BF16)
            sil = psb("sil", [128, 512], BF16)

            P.do("gpsimd", lambda e: e.dma_start(eb_sb[:], expert_b_d), inc=[(sWg, 16)])
            web = sWg.v
            wet_pe_done = [0, 0]
            wsb_free = [0, 0]
            seed_v = 0
            acc_v = [0] * NT
            silu_done = [0] * NT
            st_g = 0
            for oc in range(4):
                for i in range(NT):
                    P.do("tensor", lambda e, i=i, oc=oc: e.matmul(
                        PS[7].ap()[:], wdenseT[:, i * 128:(i + 1) * 128],
                        eb_sb[0:8, oc * 512:(oc + 1) * 512], start=True, stop=True),
                         wait=[(sWg, web), (sV, seed_v)], inc=[(sT, 1)])
                    ws = sT.v
                    P.do("vector", lambda e, i=i: e.tensor_copy(acc[:, i * 512:(i + 1) * 512],
                                                                PS[7].ap()[:]),
                         wait=[(sT, ws), (sSc, silu_done[i])], inc=[(sV, 1)], chain=False)
                    seed_v = sV.v
                    acc_v[i] = sV.v
                for e_i in range(NE):
                    k = e_i % 2
                    P.do("gpsimd", lambda e, e_i=e_i, oc=oc, k=k: e.dma_start(
                        Wsb[k][:].rearrange("p (o c) -> p o c", o=4),
                        expert_w_d[e_i, oc * 512:(oc + 1) * 512, :].rearrange("(o p) c -> p o c", p=128)),
                         wait=[(sV, wsb_free[k])], inc=[(sLg[k], 16)])
                    wload = sLg[k].v
                    for ot in range(4):
                        for j in range(ND):
                            transp_one(Wsb[k][:, ot * D + j * 128: ot * D + (j + 1) * 128],
                                       ident_b, [(sLg[k], wload), (sT, wet_pe_done[k])],
                                       [WeT[k][:, j * 512 + ot * 128: j * 512 + (ot + 1) * 128]])
                    wsb_free[k] = sV.v
                    wtr = sV.v
                    for i in range(NT):
                        pb = PS[2 + (i % 2)]
                        bank_prev = acc_v[i - 2] if i >= 2 else acc_v[NT - 2 + i]
                        for j in range(ND):
                            P.do("tensor", lambda e, i=i, j=j, k=k, pb=pb: e.matmul(
                                pb.ap()[:], xnT[:, j * S + i * 128: j * S + (i + 1) * 128],
                                WeT[k][:, j * 512:(j + 1) * 512],
                                start=(j == 0), stop=(j == ND - 1)),
                                 wait=[(sV, wtr), (sV, bank_prev)],
                                 inc=([(sT, 1)] if j == ND - 1 else []))
                        wmm = sT.v
                        if i == NT - 1:
                            wet_pe_done[k] = wmm
                        P.do("vector", lambda e, i=i, e_i=e_i, pb=pb: e.scalar_tensor_tensor(
                            acc[:, i * 512:(i + 1) * 512], pb.ap()[:],
                            wdense[:, i * NE + e_i: i * NE + e_i + 1],
                            acc[:, i * 512:(i + 1) * 512], ALU.mult, ALU.add),
                             wait=[(sT, wmm)], inc=[(sV, 1)], chain=False)
                        acc_v[i] = sV.v
                for i in range(NT):
                    P.do("scalar", lambda e, i=i: e.activation(sil[:], acc[:, i * 512:(i + 1) * 512],
                                                               ACT.Silu),
                         wait=[(sV, acc_v[i]), (sSa, st_g)], inc=[(sSc, 1)], chain=False)
                    wsil = sSc.v
                    silu_done[i] = wsil
                    P.do("gpsimd", lambda e, i=i, oc=oc: e.dma_start(
                        g0_scr[i * 128:(i + 1) * 128, oc * 512:(oc + 1) * 512], sil[:]),
                         wait=[(sSc, wsil)], inc=[(sSa, 16)])
                    st_g = sSa.v
            P.barrier()

        c_qT = rstk.enter_context(nc.sbuf_tensor("c_qT", [128, 3 * S], BF16, side="right"))
        c_kvT = rstk.enter_context(nc.sbuf_tensor("c_kvT", [128, 4 * S], BF16, side="right"))
        with ExitStack() as ph:
            def psb(n, s_, dt=F32):
                return ph.enter_context(nc.sbuf_tensor(n, s_, dt))

            Wd_sb = psb("Wd_sb", [128, ND * DKVL], BF16)
            c_sb = psb("c_sb", [128, DKVL], BF16)
            csq = psb("csq", [128, DKVL])
            Wr_sb = psb("Wr_sb", [128, ND * 512], BF16)
            cs_c = psb("cs_c", [128, 256])
            cs_s = psb("cs_s", [128, 256])
            re_f = psb("re_f", [128, 256])
            ro_f = psb("ro_f", [128, 256])
            rr_sb = psb("rr_sb", [128, 512], BF16)
            rp_b = psb("rp_b", [128, 4 * 128], BF16)

            wfree = 0
            for (W_d, RL, cT) in ((Wq_down_d, DQL, c_qT), (Wkv_down_d, DKVL, c_kvT)):
                NR = RL // 128
                P.do("gpsimd", lambda e, W_d=W_d, RL=RL: e.dma_start(
                    Wd_sb[:, 0:ND * RL].rearrange("p (j c) -> p j c", j=ND),
                    W_d.rearrange("(j p) c -> p j c", p=128)),
                     wait=[(sV, wfree)], inc=[(sWg, 16)])
                wW = sWg.v
                psfree = [0, 0]
                for i in range(NT):
                    pb = PS[2 + (i % 2)]
                    for j in range(ND):
                        P.do("tensor", lambda e, i=i, j=j, RL=RL, pb=pb: e.matmul(
                            pb.ap()[:, 0:RL], xnT[:, j * S + i * 128: j * S + (i + 1) * 128],
                            Wd_sb[:, j * RL:(j + 1) * RL], start=(j == 0), stop=(j == ND - 1)),
                             wait=[(sWg, wW), (sSc, psfree[i % 2])],
                             inc=([(sT, 1)] if j == ND - 1 else []))
                    wmm = sT.v
                    P.do("scalar", lambda e, RL=RL, pb=pb: e.activation(
                        csq[:, 0:RL], pb.ap()[:, 0:RL], ACT.Square, accum_out=cols[0][:]),
                         wait=[(sT, wmm)], inc=[(sSc, 1)], chain=False)
                    wsq = sSc.v
                    P.do("vector", lambda e, RL=RL: e.tensor_scalar(
                        cols[0][:], cols[0][:], 1.0 / RL, None, ALU.mult),
                         wait=[(sSc, wsq)], inc=[(sV, 1)])
                    wv = sV.v
                    P.do("scalar", lambda e: e.activation(cols[0][:], cols[0][:], ACT.Ln, bias=c_rmseps[:]),
                         wait=[(sV, wv)])
                    P.do("scalar", lambda e: e.activation(cols[0][:], cols[0][:], ACT.Exp, scale=-0.5))
                    P.do("scalar", lambda e, RL=RL, pb=pb: e.activation(
                        c_sb[:, 0:RL], pb.ap()[:, 0:RL], ACT.Copy, scale=cols[0][:]),
                         wait=[(sT, getattr(P, "_b3ct", 0))], inc=[(sSc, 1)])
                    psfree[i % 2] = sSc.v
                    wcs = sSc.v
                    for j in range(NR):
                        transp_one(c_sb[:, j * 128:(j + 1) * 128], ident_b, [(sSc, wcs)],
                                   [cT[:, j * S + i * 128: j * S + (i + 1) * 128]])
                    P._b3ct = sT.v
                P.do("vector", lambda e: None, inc=[(sV, 1)])
                wfree = sV.v

            for (Wr_d, rT_scr) in ((Wq_rope_d, qrT_scr), (Wk_rope_d, krT_scr)):
                for half in range(2):
                    P.do("gpsimd", lambda e, Wr_d=Wr_d, half=half: e.dma_start(
                        Wr_sb[:].rearrange("p (j c) -> p j c", j=ND),
                        Wr_d[:, half * 512:(half + 1) * 512].rearrange("(j p) c -> p j c", p=128)),
                         wait=[(sV, wfree)], inc=[(sWg, 16)])
                    wW = sWg.v
                    psfree = [0, 0]
                    wst = 0
                    for i in range(NT):
                        pb = PS[2 + (i % 2)]
                        for j in range(ND):
                            P.do("tensor", lambda e, i=i, j=j, pb=pb: e.matmul(
                                pb.ap()[:], xnT[:, j * S + i * 128: j * S + (i + 1) * 128],
                                Wr_sb[:, j * 512:(j + 1) * 512], start=(j == 0), stop=(j == ND - 1)),
                                 wait=[(sWg, wW), (sV, psfree[i % 2])],
                                 inc=([(sT, 1)] if j == ND - 1 else []))
                        wmm = sT.v
                        P.do("sync", lambda e, i=i, half=half: e.dma_start(
                            cs_c[:], cos_d[i * 128:(i + 1) * 128, half * 256:(half + 1) * 256]),
                             wait=[(sV, getattr(P, "_b3cs", 0))], inc=[(sLc, 16)])
                        P.do("sync", lambda e, i=i, half=half: e.dma_start(
                            cs_s[:], sin_d[i * 128:(i + 1) * 128, half * 256:(half + 1) * 256]),
                             inc=[(sLc, 16)])
                        wcs2 = sLc.v
                        pe = pb.ap().rearrange("p (k two) -> p k two", two=2)
                        rv = rr_sb[:].rearrange("p (k two) -> p k two", two=2)
                        P.do("vector", lambda e, pe=pe: e.tensor_tensor(re_f[:], pe[:, :, 0], cs_c[:], ALU.mult),
                             wait=[(sT, wmm), (sLc, wcs2)], chain=False)
                        P.do("vector", lambda e, pe=pe: e.tensor_tensor(ro_f[:], pe[:, :, 1], cs_s[:], ALU.mult),
                             chain=False)
                        P.do("vector", lambda e, rv=rv: e.tensor_tensor(rv[:, :, 0], re_f[:], ro_f[:], ALU.subtract),
                             chain=False)
                        P.do("vector", lambda e, pe=pe: e.tensor_tensor(re_f[:], pe[:, :, 0], cs_s[:], ALU.mult),
                             chain=False)
                        P.do("vector", lambda e, pe=pe: e.tensor_tensor(ro_f[:], pe[:, :, 1], cs_c[:], ALU.mult),
                             inc=[(sV, 1)], chain=False)
                        psfree[i % 2] = sV.v
                        P._b3cs = sV.v
                        P.do("vector", lambda e, rv=rv: e.tensor_tensor(rv[:, :, 1], re_f[:], ro_f[:], ALU.add),
                             inc=[(sV, 1)], chain=False)
                        wrr = sV.v
                        for c4 in range(4):
                            transp_one(rr_sb[:, c4 * 128:(c4 + 1) * 128], ident_b,
                                       [(sV, wrr), (sSa, wst)],
                                       [rp_b[:, c4 * 128:(c4 + 1) * 128]])
                        wev = sV.v
                        P.do("gpsimd", lambda e, rT_scr=rT_scr, half=half, i=i: e.dma_start(
                            rT_scr[half * 512:(half + 1) * 512, i * 128:(i + 1) * 128]
                            .rearrange("(c p) t -> p c t", p=128),
                            rp_b[:].rearrange("p (c t) -> p c t", c=4)),
                             wait=[(sV, wev)], inc=[(sSa, 16)])
                        wst = sSa.v
                    P.do("vector", lambda e: None, wait=[(sSa, wst)], inc=[(sV, 1)])
                    wfree = sV.v
            P.barrier()

        with ExitStack() as ph:
            def psb(n, s_, dt=F32):
                return ph.enter_context(nc.sbuf_tensor(n, s_, dt))

            Wqu = psb("Wqu", [128, 3 * 2048], BF16)
            Wku = psb("Wku", [128, 4 * 2048], BF16)
            Wvu = psb("Wvu", [128, 4 * 2048], BF16)
            nwq = psb("nwq", [128, 3])
            nwkv = psb("nwkv", [128, 4])
            qTh = [psb(f"qTh{k}", [128, S], BF16) for k in range(2)]
            kTh = [psb(f"kTh{k}", [128, S], BF16) for k in range(2)]
            v_sb = [psb(f"v_sb{k}", [128, NT * 128], BF16) for k in range(2)]
            qr_h = [psb(f"qr_h{k}", [64, S], BF16) for k in range(2)]
            kr_h = [psb(f"kr_h{k}", [64, S], BF16) for k in range(2)]
            sq_b = psb("sq_b", [128, S], BF16)
            sqr_b = psb("sqr_b", [64, S], BF16)
            row4 = psb("row4", [1, 4])
            sca = psb("sca", [1, 1])
            scb = psb("scb", [1, 1])
            shiftb = psb("shiftb", [128, 1])
            ex_b = [psb(f"ex_b{k}", [128, 512], BF16) for k in range(2)]
            rec_f = psb("rec_f", [1, 512])
            rec_b = psb("rec_b", [1, 512], BF16)
            rb_f = psb("rb_f", [128, 512])
            ao_b = [psb(f"ao_b{k}", [128, 512], BF16) for k in range(2)]

            P.do("gpsimd", lambda e: e.dma_start(
                Wqu[:].rearrange("p (j c) -> p j c", j=3), Wq_up_d.rearrange("(j p) c -> p j c", p=128)),
                 inc=[(sWg, 16)])
            P.do("gpsimd", lambda e: e.dma_start(
                Wku[:].rearrange("p (j c) -> p j c", j=4), Wk_up_d.rearrange("(j p) c -> p j c", p=128)),
                 inc=[(sWg, 16)])
            P.do("gpsimd", lambda e: e.dma_start(
                Wvu[:].rearrange("p (j c) -> p j c", j=4), Wv_up_d.rearrange("(j p) c -> p j c", p=128)),
                 inc=[(sWg, 16)])
            P.do("sync", lambda e: e.dma_start(nwq[:, 0:3], q_norm_w_d.rearrange("(j p) -> p j", p=128),
                                               allow_slow_non_contiguous=True), inc=[(sW, 16)])
            P.do("sync", lambda e: e.dma_start(nwkv[:, 0:4], kv_norm_w_d.rearrange("(j p) -> p j", p=128),
                                               allow_slow_non_contiguous=True), inc=[(sW, 16)])
            wld = sWg.v
            wldn = sW.v
            for j in range(3):
                P.do("vector", lambda e, j=j: e.tensor_scalar(
                    Wqu[:, j * 2048:(j + 1) * 2048], Wqu[:, j * 2048:(j + 1) * 2048],
                    nwq[:, j:j + 1], None, ALU.mult), wait=[(sWg, wld), (sW, wldn)], chain=False)
            for j in range(4):
                P.do("vector", lambda e, j=j: e.tensor_scalar(
                    Wku[:, j * 2048:(j + 1) * 2048], Wku[:, j * 2048:(j + 1) * 2048],
                    nwkv[:, j:j + 1], None, ALU.mult), chain=False)
                P.do("vector", lambda e, j=j: e.tensor_scalar(
                    Wvu[:, j * 2048:(j + 1) * 2048], Wvu[:, j * 2048:(j + 1) * 2048],
                    nwkv[:, j:j + 1], None, ALU.mult), inc=([(sV, 1)] if j == 3 else []), chain=False)
            wwup = sV.v

            rope_ld = [0, 0]
            proj_done = [0, 0]
            ao_st = [0, 0]
            exp_done = {}
            for h in range(H):
                k = h % 2
                P.do("sync", lambda e, h=h, k=k: e.dma_start(qr_h[k][:], qrT_scr[h * 64:(h + 1) * 64, :]),
                     wait=[(sT, proj_done[k])], inc=[(sL[k], 16)])
                P.do("sync", lambda e, h=h, k=k: e.dma_start(kr_h[k][:], krT_scr[h * 64:(h + 1) * 64, :]),
                     inc=[(sL[k], 16)])
                wrope = sL[k].v
                for qb in range(4):
                    pb = PS[2 + (qb % 2)]
                    for jr in range(3):
                        P.do("tensor", lambda e, h=h, qb=qb, jr=jr, pb=pb: e.matmul(
                            pb.ap()[:], Wqu[:, jr * 2048 + h * 128: jr * 2048 + (h + 1) * 128],
                            c_qT[:, jr * S + qb * 512: jr * S + (qb + 1) * 512],
                            start=(jr == 0), stop=(jr == 2)),
                             wait=[(sV, wwup), (sV, sV.v), (sT, proj_done[k]),
                                   (sSc, exp_done.get((2 + qb % 2), 0))],
                             inc=([(sT, 1)] if jr == 2 else []))
                    wmm = sT.v
                    P.do("vector", lambda e, qb=qb, k=k, pb=pb: e.tensor_copy(
                        qTh[k][:, qb * 512:(qb + 1) * 512], pb.ap()[:]), wait=[(sT, wmm)], inc=[(sV, 1)],
                         chain=False)
                for kb in range(4):
                    pb = PS[2 + (kb % 2)]
                    for jr in range(4):
                        P.do("tensor", lambda e, h=h, kb=kb, jr=jr, pb=pb: e.matmul(
                            pb.ap()[:], Wku[:, jr * 2048 + h * 128: jr * 2048 + (h + 1) * 128],
                            c_kvT[:, jr * S + kb * 512: jr * S + (kb + 1) * 512],
                            start=(jr == 0), stop=(jr == 3)),
                             wait=[(sV, sV.v)], inc=([(sT, 1)] if jr == 3 else []))
                    wmm = sT.v
                    P.do("vector", lambda e, kb=kb, k=k, pb=pb: e.tensor_copy(
                        kTh[k][:, kb * 512:(kb + 1) * 512], pb.ap()[:]), wait=[(sT, wmm)], inc=[(sV, 1)],
                         chain=False)
                for tt in range(NT):
                    pb = PS[2 + (tt % 2)]
                    for jr in range(4):
                        P.do("tensor", lambda e, h=h, tt=tt, jr=jr, pb=pb: e.matmul(
                            pb.ap()[:, 0:128], c_kvT[:, jr * S + tt * 128: jr * S + (tt + 1) * 128],
                            Wvu[:, jr * 2048 + h * 128: jr * 2048 + (h + 1) * 128],
                            start=(jr == 0), stop=(jr == 3)),
                             wait=[(sV, sV.v)], inc=([(sT, 1)] if jr == 3 else []))
                    wmm = sT.v
                    P.do("vector", lambda e, tt=tt, k=k, pb=pb: e.tensor_copy(
                        v_sb[k][:, tt * 128:(tt + 1) * 128], pb.ap()[:, 0:128]),
                         wait=[(sT, wmm)], inc=[(sV, 1)], chain=False)
                wproj = sV.v
                P.do("scalar", lambda e, k=k: e.activation(sq_b[:], qTh[k][:], ACT.Square),
                     wait=[(sV, wproj)], inc=[(sSc, 1)], chain=False)
                P.do("scalar", lambda e, k=k: e.activation(sqr_b[:], qr_h[k][:], ACT.Square),
                     wait=[(sL[k], wrope)], inc=[(sSc, 1)], chain=False)
                wsq2 = sSc.v
                for qb in range(4):
                    P.do("tensor", lambda e, qb=qb: e.matmul(
                        PS[6].ap()[0:1, :], ones_b[:], sq_b[:, qb * 512:(qb + 1) * 512],
                        start=True, stop=False),
                         wait=[(sSc, wsq2), (sV, getattr(P, "_b4n", 0)),
                               (sV, getattr(P, "_b4den", 0))])
                    P.do("tensor", lambda e, qb=qb: e.matmul(
                        PS[6].ap()[0:1, :], ones_b[0:64, :], sqr_b[0:64, qb * 512:(qb + 1) * 512],
                        start=False, stop=True), inc=[(sT, 1)])
                    wnr = sT.v
                    P.do("vector", lambda e, qb=qb: e.tensor_reduce(
                        row4[0:1, qb:qb + 1], PS[6].ap()[0:1, :], AX.X, ALU.max),
                         wait=[(sT, wnr)], inc=[(sV, 1)])
                    P._b4n = sV.v
                wqn = sT.v
                P.do("vector", lambda e: e.tensor_reduce(sca[0:1, :], row4[0:1, :], AX.X, ALU.max),
                     inc=[(sV, 1)])
                wmq = sV.v
                P.do("scalar", lambda e, k=k: e.activation(sq_b[:], kTh[k][:], ACT.Square),
                     wait=[(sT, wqn)], inc=[(sSc, 1)], chain=False)
                P.do("scalar", lambda e, k=k: e.activation(sqr_b[:], kr_h[k][:], ACT.Square),
                     inc=[(sSc, 1)], chain=False)
                wsq3 = sSc.v
                for qb in range(4):
                    P.do("tensor", lambda e, qb=qb: e.matmul(
                        PS[6].ap()[0:1, :], ones_b[:], sq_b[:, qb * 512:(qb + 1) * 512],
                        start=True, stop=False), wait=[(sSc, wsq3), (sV, P._b4n)])
                    P.do("tensor", lambda e, qb=qb: e.matmul(
                        PS[6].ap()[0:1, :], ones_b[0:64, :], sqr_b[0:64, qb * 512:(qb + 1) * 512],
                        start=False, stop=True), inc=[(sT, 1)])
                    wnr = sT.v
                    P.do("vector", lambda e, qb=qb: e.tensor_reduce(
                        row4[0:1, qb:qb + 1], PS[6].ap()[0:1, :], AX.X, ALU.max),
                         wait=[(sT, wnr)], inc=[(sV, 1)])
                    P._b4n = sV.v
                P.do("vector", lambda e: e.tensor_reduce(scb[0:1, :], row4[0:1, :], AX.X, ALU.max),
                     inc=[(sV, 1)])
                wmk = sV.v
                P.do("scalar", lambda e: e.activation(sca[0:1, :], sca[0:1, :], ACT.Ln),
                     wait=[(sV, wmq)])
                P.do("scalar", lambda e: e.activation(scb[0:1, :], scb[0:1, :], ACT.Ln),
                     wait=[(sV, wmk)], inc=[(sSc, 1)])
                wln = sSc.v
                P.do("vector", lambda e: e.tensor_tensor(sca[0:1, :], sca[0:1, :], scb[0:1, :], ALU.add),
                     wait=[(sSc, wln)], inc=[(sV, 1)])
                wadd = sV.v
                P.do("scalar", lambda e: e.activation(sca[0:1, :], sca[0:1, :], ACT.Exp,
                                                      scale=0.5, bias=c_lsa[0:1, :]),
                     wait=[(sV, wadd)], inc=[(sSc, 1)])
                wexp = sSc.v
                P.do("vector", lambda e: e.tensor_scalar(sca[0:1, :], sca[0:1, :], -1.0, None, ALU.mult),
                     wait=[(sSc, wexp)], inc=[(sV, 1)])
                wneg = sV.v
                P.do("tensor", lambda e: e.matmul(PS[5].ap()[:, 0:1], onesr_f[:], sca[0:1, 0:1],
                                                  start=True, stop=True),
                     wait=[(sV, wneg), (sV, getattr(P, "_b4s", 0))], inc=[(sT, 1)])
                wbc = sT.v
                P.do("vector", lambda e: e.tensor_copy(shiftb[:], PS[5].ap()[:, 0:1]),
                     wait=[(sT, wbc)], inc=[(sV, 1)])
                P._b4s = sV.v
                wshift = sV.v
                for qb in range(4):
                    av_v = getattr(P, "_b4ao", 0)
                    den_v = getattr(P, "_b4den", 0)
                    for kt in range(NT):
                        pb_i = 2 + (kt % 2)
                        pb = PS[pb_i]
                        P.do("tensor", lambda e, h=h, k=k, kt=kt, qb=qb, pb=pb: e.matmul(
                            pb.ap()[:], kTh[k][:, kt * 128:(kt + 1) * 128],
                            qTh[k][:, qb * 512:(qb + 1) * 512], start=True, stop=False),
                             wait=[(sSc, exp_done.get(pb_i, 0)), (sV, wshift)])
                        P.do("tensor", lambda e, h=h, k=k, kt=kt, qb=qb, pb=pb: e.matmul(
                            pb.ap()[:], kr_h[k][0:64, kt * 128:(kt + 1) * 128],
                            qr_h[k][0:64, qb * 512:(qb + 1) * 512], start=False, stop=True),
                             inc=[(sT, 1)])
                        wsc = sT.v
                        P.do("scalar", lambda e, kt=kt, pb=pb: e.activation(
                            ex_b[kt % 2][:], pb.ap()[:], ACT.Exp, scale=SCALE_ATTN, bias=shiftb[:]),
                             wait=[(sT, wsc)], inc=[(sSc, 1)], chain=False)
                        exp_done[pb_i] = sSc.v
                        we2 = sSc.v
                        P.do("tensor", lambda e, k=k, kt=kt: e.matmul(
                            PS[4].ap()[:], v_sb[k][:, kt * 128:(kt + 1) * 128], ex_b[kt % 2][:],
                            start=(kt == 0), stop=(kt == NT - 1)),
                             wait=[(sSc, we2), (sV, av_v)])
                        P.do("tensor", lambda e, kt=kt: e.matmul(
                            PS[6].ap()[0:1, :], ones_b[:], ex_b[kt % 2][:],
                            start=(kt == 0), stop=(kt == NT - 1)),
                             wait=[(sV, den_v), (sV, getattr(P, "_b4n", 0))],
                             inc=([(sT, 1)] if kt == NT - 1 else []))
                    wav = sT.v
                    if qb == 3:
                        proj_done[k] = wav
                    P.do("vector", lambda e: e.reciprocal(rec_f[0:1, :], PS[6].ap()[0:1, :]),
                         wait=[(sT, wav)], inc=[(sV, 1)], chain=False)
                    P._b4den = sV.v
                    P.do("vector", lambda e: e.tensor_copy(rec_b[0:1, :], rec_f[0:1, :]), inc=[(sV, 1)],
                         chain=False)
                    wrec = sV.v
                    P.do("tensor", lambda e: e.matmul(PS[5].ap()[:], onesr_b[:], rec_b[0:1, :],
                                                      start=True, stop=True),
                         wait=[(sV, wrec)], inc=[(sT, 1)])
                    wbc2 = sT.v
                    P.do("vector", lambda e: e.tensor_copy(rb_f[:], PS[5].ap()[:]),
                         wait=[(sT, wbc2)], inc=[(sV, 1)], chain=False)
                    P._b4s = sV.v
                    P.do("vector", lambda e, qb=qb: e.tensor_tensor(
                        ao_b[qb % 2][:], PS[4].ap()[:], rb_f[:], ALU.mult),
                         wait=[(sS2[qb % 2], ao_st[qb % 2])], inc=[(sV, 1)], chain=False)
                    P._b4ao = sV.v
                    wao = sV.v
                    P.do("gpsimd", lambda e, h=h, qb=qb: e.dma_start(
                        ao_scr[h * 128:(h + 1) * 128, qb * 512:(qb + 1) * 512], ao_b[qb % 2][:]),
                         wait=[(sV, wao)], inc=[(sS2[qb % 2], 16)])
                    ao_st[qb % 2] = sS2[qb % 2].v
            P.barrier()
        rstk.close()

        with ExitStack() as ph:
            def psb(n, s_, dt=F32):
                return ph.enter_context(nc.sbuf_tensor(n, s_, dt))

            Wo_sb = psb("Wo_sb", [128, ND * 2048], BF16)
            aoL = psb("aoL", [128, H * 2048], BF16)
            x_sb = [psb(f"b5x{k}", [128, D]) for k in range(2)]
            g0_sb = [psb(f"b5g{k}", [128, D], BF16) for k in range(2)]
            sg_f = psb("sg_f", [128, 512])
            x1_sb = [psb(f"b5o{k}", [128, D]) for k in range(2)]

            P.do("gpsimd", lambda e: e.dma_start(
                Wo_sb[:].rearrange("p (j c) -> p j c", j=ND), Wout_d.rearrange("(j p) c -> p j c", p=128)),
                 inc=[(sWg, 16)])
            P.do("sync", lambda e: e.dma_start(
                aoL[:].rearrange("p (j c) -> p j c", j=H), ao_scr.rearrange("(j p) c -> p j c", p=128)),
                 inc=[(sW, 16)])
            wW = sWg.v
            wao2 = sW.v
            st_x1 = [0, 0]
            for i in range(NT):
                P.do("sync", lambda e, i=i: e.dma_start(x_sb[i % 2][:], x_in[i * 128:(i + 1) * 128, :]),
                     wait=[(sS2[i % 2], st_x1[i % 2])], inc=[(sL[i % 2], 16)])
                P.do("sync", lambda e, i=i: e.dma_start(g0_sb[i % 2][:], g0_scr[i * 128:(i + 1) * 128, :]),
                     inc=[(sL[i % 2], 16)])
                wx = sL[i % 2].v
                for nb in range(4):
                    pb = PS[2 + (nb % 2)]
                    for ct in range(H):
                        P.do("tensor", lambda e, i=i, nb=nb, ct=ct, pb=pb: e.matmul(
                            pb.ap()[:], aoL[:, ct * 2048 + i * 128: ct * 2048 + (i + 1) * 128],
                            Wo_sb[:, ct * 2048 + nb * 512: ct * 2048 + (nb + 1) * 512],
                            start=(ct == 0), stop=(ct == H - 1)),
                             wait=[(sWg, wW), (sW, wao2), (sV, getattr(P, f"_b5p{nb % 2}", 0))],
                             inc=([(sT, 1)] if ct == H - 1 else []))
                    wmm = sT.v
                    P.do("vector", lambda e, i=i, nb=nb, pb=pb: e.tensor_tensor(
                        sg_f[:], pb.ap()[:], g0_sb[i % 2][:, nb * 512:(nb + 1) * 512], ALU.mult),
                         wait=[(sT, wmm), (sL[i % 2], wx)], inc=[(sV, 1)], chain=False)
                    setattr(P, f"_b5p{nb % 2}", sV.v)
                    P.do("vector", lambda e, i=i, nb=nb: e.tensor_tensor(
                        x1_sb[i % 2][:, nb * 512:(nb + 1) * 512], sg_f[:],
                        x_sb[i % 2][:, nb * 512:(nb + 1) * 512], ALU.add),
                         inc=([(sV, 1)] if nb == 3 else []), chain=False)
                wx1 = sV.v
                P.do("gpsimd", lambda e, i=i: e.dma_start(x1_scr[i * 128:(i + 1) * 128, :], x1_sb[i % 2][:]),
                     wait=[(sV, wx1)], inc=[(sS2[i % 2], 16)])
                st_x1[i % 2] = sS2[i % 2].v
            P.barrier()

        qT = rstk.enter_context(nc.sbuf_tensor("qT", [128, ND * S], BF16, side="right"))
        with ExitStack() as ph:
            def psb(n, s_, dt=F32):
                return ph.enter_context(nc.sbuf_tensor(n, s_, dt))

            x1_b = [psb(f"b6x{k}", [128, D]) for k in range(2)]
            xn_f = psb("b6xn", [128, D])
            sq_f = psb("b6sq", [128, D])
            scr_f = psb("b6scr", [128, D])
            q_b = psb("b6q", [128, D], BF16)
            strip = psb("b6strip", [128, ND * 128])
            strip2 = psb("b6strip2", [128, ND * 128])
            Wx_sb = psb("Wx_sb", [128, ND * 96])
            Wdt_sb = psb("Wdt_sb", [128, D])
            del_f = psb("del_f", [128, 65])
            delT = psb("delT", [128, 128])
            dt_f = psb("dt_f", [128, D])
            bt6 = psb("bt6", [32, 128])

            P.do("sync", lambda e: e.dma_start(
                Wx_sb[:].rearrange("p (j c) -> p j c", j=ND), Wx_d.rearrange("(j p) c -> p j c", p=128)),
                 inc=[(sW, 16)])
            P.do("sync", lambda e: e.dma_start(Wdt_sb[0:64, :], Wdt_d), inc=[(sW, 16)])
            P.do("sync", lambda e: e.dma_start(Wdt_sb[64:65, :], bdt_d.rearrange("(o c) -> o c", c=DI)),
                 inc=[(sW, 16)])
            wwx = sW.v
            P.do("vector", lambda e: e.memset(del_f[:, 64:65], 1.0))
            tile_done = [0] * NT
            st_u = 0
            st_dt = 0
            xn_lastT = 0
            wq = 0
            for i in range(NT):
                P.do("sync", lambda e, i=i: e.dma_start(x1_b[i % 2][:], x1_scr[i * 128:(i + 1) * 128, :]),
                     wait=([(sSc, tile_done[i - 2])] if i >= 2 else []), inc=[(sL[i % 2], 16)])
                wx = sL[i % 2].v
                wxn = emit_ln(x1_b[i % 2], xn_f, c_lneps, [(sL[i % 2], wx)],
                              final_wait=[(sT, xn_lastT), (sV, wq)])
                P.do("vector", lambda e: e.reduce_max(cols[3][:], xn_f[:], axis=AX.X,
                                                      apply_absolute_value=True),
                     wait=[(sSc, wxn)], chain=False)
                P.do("vector", lambda e: e.tensor_scalar(cols[3][:], cols[3][:], Q_EPS, None, ALU.max))
                P.do("vector", lambda e, i=i: e.tensor_scalar(insc[:, i:i + 1], cols[3][:], 1.0 / 127.0,
                                                              None, ALU.mult))
                P.do("vector", lambda e: e.reciprocal(cols[4][:], cols[3][:]))
                P.do("vector", lambda e: e.tensor_scalar(cols[4][:], cols[4][:], 127.0, None, ALU.mult))
                P.do("vector", lambda e: e.tensor_scalar(scr_f[:], xn_f[:], cols[4][:], None, ALU.mult))
                P.do("vector", lambda e: e.tensor_scalar(q_b[:], scr_f[:], RND_C, -RND_C, ALU.add, ALU.add),
                     inc=[(sV, 1)], chain=False)
                wq = sV.v
                for j in range(ND):
                    transp_one(q_b[:, j * 128:(j + 1) * 128], ident_b, [(sV, wq)],
                               [qT[:, j * S + i * 128: j * S + (i + 1) * 128]])
                for j in range(ND):
                    transp_one(xn_f[:, j * 128:(j + 1) * 128], ident, [(sSc, wxn), (sSa, st_u)],
                               [strip[:, j * 128:(j + 1) * 128]])
                xn_lastT = sT.v
                wstrip = sV.v
                tile_done[i] = wxn
                for hhalf in range(2):
                    P.do("gpsimd", lambda e, i=i, hh=hhalf: e.dma_start(
                        uT_scr[0:0 + ND * NT * 128, :]
                        .rearrange("(j i2 p) t -> p j i2 t", p=128, i2=NT)[:, hh * 8:(hh + 1) * 8, i, :],
                        strip[:, hh * 8 * 128:(hh + 1) * 8 * 128].rearrange("p (j t) -> p j t", j=8)),
                         wait=[(sV, wstrip)], inc=[(sSa, 16)])
                st_u = sSa.v
                for j in range(ND):
                    P.do("tensor", lambda e, j=j: e.matmul(
                        PS[7].ap()[:, 0:96], strip[:, j * 128:(j + 1) * 128],
                        Wx_sb[:, j * 96:(j + 1) * 96], start=(j == 0), stop=(j == ND - 1)),
                         wait=[(sW, wwx), (sV, wstrip), (sV, getattr(P, "_b6xz", 0))],
                         inc=([(sT, 1)] if j == ND - 1 else []))
                wxz = sT.v
                P.do("vector", lambda e: e.tensor_copy(del_f[:, 0:64], PS[7].ap()[:, 0:64]),
                     wait=[(sT, wxz)], chain=False)
                P.do("vector", lambda e, i=i: e.tensor_copy(bc_sb[:, i * 32:(i + 1) * 32],
                                                            PS[7].ap()[:, 64:96]), inc=[(sV, 1)], chain=False)
                P._b6xz = sV.v
                wdel = sV.v
                transp_one(bc_sb[:, i * 32:(i + 1) * 32], ident,
                           [(sV, wdel), (sSe, getattr(P, "_b6bc", 0))], [bt6[:]])
                wbt6 = sV.v
                P.do("gpsimd", lambda e, i=i: e.dma_start(
                    bcT_scr[:, i * 128:(i + 1) * 128], bt6[:]),
                     wait=[(sV, wbt6)], inc=[(sSe, 16)])
                P._b6bc = sSe.v
                transp_one(del_f[:], ident, [(sV, wdel)], [delT[0:65, :]])
                wdelT = sV.v
                for nb in range(4):
                    pb = PS[2 + (nb % 2)]
                    P.do("tensor", lambda e, nb=nb, pb=pb: e.matmul(
                        pb.ap()[:], delT[0:65, :], Wdt_sb[0:65, nb * 512:(nb + 1) * 512],
                        start=True, stop=True),
                         wait=[(sW, wwx), (sV, wdelT), (sSc, getattr(P, f"_b6dt{nb % 2}", 0))],
                         inc=[(sT, 1)])
                    wdt = sT.v
                    P.do("scalar", lambda e, nb=nb, pb=pb: e.activation(
                        scr_f[:, 0:512], pb.ap()[:], ACT.Exp),
                         wait=[(sT, wdt), (sV, wq)], inc=[(sSc, 1)], chain=False)
                    setattr(P, f"_b6dt{nb % 2}", sSc.v)
                    P.do("scalar", lambda e, nb=nb: e.activation(
                        dt_f[:, nb * 512:(nb + 1) * 512], scr_f[:, 0:512], ACT.Ln, bias=c_one[:]),
                         inc=[(sSc, 1)], chain=False)
                wsp = sSc.v
                for j in range(ND):
                    transp_one(dt_f[:, j * 128:(j + 1) * 128], ident, [(sSc, wsp), (sSb, st_dt)],
                               [strip2[:, j * 128:(j + 1) * 128]])
                wst2 = sV.v
                for hhalf in range(2):
                    P.do("gpsimd", lambda e, i=i, hh=hhalf: e.dma_start(
                        dtT_scr.rearrange("(j i2 p) t -> p j i2 t", p=128, i2=NT)[:, hh * 8:(hh + 1) * 8, i, :],
                        strip2[:, hh * 8 * 128:(hh + 1) * 8 * 128].rearrange("p (j t) -> p j t", j=8)),
                         wait=[(sV, wst2)], inc=[(sSb, 16)])
                st_dt = sSb.v
            P.barrier()

        with ExitStack() as ph:
            def psb(n, s_, dt=F32):
                return ph.enter_context(nc.sbuf_tensor(n, s_, dt))

            w8_sb = [psb(f"w8s{k}", [128, 4 * D], I8) for k in range(2)]
            w8b = [psb(f"w8b{k}", [128, 4 * D], BF16) for k in range(2)]
            wT = [psb(f"wT{k}", [128, ND * 512], BF16) for k in range(2)]
            wsc_bc = psb("wsc_bc", [128, D])
            bias_bc = psb("bias_bc", [128, D])
            dq_f = psb("dq_f", [128, 512])
            ff_h = psb("ff_h", [128, 512], F16)
            fp_b = psb("fp_b", [128, 4 * 128], F16)

            P.do("gpsimd", lambda e: e.dma_start(
                wsc_bc[:], w_scale_d.rearrange("c o -> o c").partition_broadcast(128)
                .rearrange("p o c -> p (o c)")), inc=[(sWg, 16)])
            P.do("gpsimd", lambda e: e.dma_start(
                bias_bc[:], bias8_d.rearrange("(o c) -> o c", c=D).partition_broadcast(128)
                .rearrange("p o c -> p (o c)")), inc=[(sWg, 16)])
            wrows = sWg.v
            P.do("vector", lambda e: None, wait=[(sWg, wrows)], inc=[(sV, 1)])

            wt_pe_done = [0, 0]
            st_ff = 0
            for oc in range(4):
                k = oc % 2
                P.do("sync", lambda e, oc=oc, k=k: e.dma_start(
                    w8_sb[k][:].rearrange("p (o c) -> p o c", o=4),
                    w8_d[oc * 512:(oc + 1) * 512, :].rearrange("(o p) c -> p o c", p=128)),
                     wait=[(sT, wt_pe_done[k])], inc=[(sL[k], 16)])
                wld = sL[k].v
                for ot in range(4):
                    P.do("vector", lambda e, k=k, ot=ot: e.tensor_copy(
                        w8b[k][:, ot * D:(ot + 1) * D], w8_sb[k][:, ot * D:(ot + 1) * D]),
                         wait=[(sL[k], wld)], inc=[(sV, 1)], chain=False)
                wup = sV.v
                for ot in range(4):
                    for j in range(ND):
                        transp_one(w8b[k][:, ot * D + j * 128: ot * D + (j + 1) * 128],
                                   ident_b, [(sV, wup), (sT, wt_pe_done[k])],
                                   [wT[k][:, j * 512 + ot * 128: j * 512 + (ot + 1) * 128]])
                wtr = sV.v
                for i in range(NT):
                    pb = PS[2 + (i % 2)]
                    for j in range(ND):
                        P.do("tensor", lambda e, i=i, j=j, k=k, pb=pb: e.matmul(
                            pb.ap()[:], qT[:, j * S + i * 128: j * S + (i + 1) * 128],
                            wT[k][:, j * 512:(j + 1) * 512], start=(j == 0), stop=(j == ND - 1)),
                             wait=[(sV, wtr), (sSc, getattr(P, f"_b7p{i % 2}", 0))],
                             inc=([(sT, 1)] if j == ND - 1 else []))
                    wmm = sT.v
                    if i == NT - 1:
                        wt_pe_done[k] = wmm
                    P.do("scalar", lambda e, i=i, pb=pb: e.activation(
                        dq_f[:], pb.ap()[:], ACT.Copy, scale=insc[:, i:i + 1]),
                         wait=[(sT, wmm), (sV, getattr(P, "_b7ff", 0))], inc=[(sSc, 1)], chain=False)
                    setattr(P, f"_b7p{i % 2}", sSc.v)
                    wdq = sSc.v
                    P.do("vector", lambda e, oc=oc: e.tensor_tensor(
                        dq_f[:], dq_f[:], wsc_bc[:, oc * 512:(oc + 1) * 512], ALU.mult),
                         wait=[(sSc, wdq)], chain=False)
                    P.do("vector", lambda e, oc=oc: e.tensor_tensor(
                        ff_h[:], dq_f[:], bias_bc[:, oc * 512:(oc + 1) * 512], ALU.add),
                         inc=[(sV, 1)], chain=False)
                    P._b7ff = sV.v
                    wffh = sV.v
                    for ot in range(4):
                        transp_one(ff_h[:, ot * 128:(ot + 1) * 128], ident_h,
                                   [(sV, wffh), (sSa, st_ff)],
                                   [fp_b[:, ot * 128:(ot + 1) * 128]])
                    wev = sV.v
                    P.do("gpsimd", lambda e, i=i, oc=oc: e.dma_start(
                        ffnT_scr.rearrange("(j i2 p) t -> p j i2 t", p=128, i2=NT)
                        [:, oc * 4:(oc + 1) * 4, i, :],
                        fp_b[:].rearrange("p (o t) -> p o t", o=4)),
                         wait=[(sV, wev)], inc=[(sSa, 16)])
                    st_ff = sSa.v
            P.barrier()
        rstk.close()

        with ExitStack() as ph:
            def psb(n, s_, dt=F32):
                return ph.enter_context(nc.sbuf_tensor(n, s_, dt))

            al_sb = psb("al_sb", [128, ND * DS])
            negAn = psb("negAn", [128, ND * DS])
            dp_sb = psb("dp_sb", [128, ND])
            Bb = psb("Bb", [128, 16 * TCH])
            Cb = psb("Cb", [128, 16 * TCH])
            carry = psb("carry", [128, ND * DS])
            dtb = [psb(f"dtb{k}", [128, TCH]) for k in range(2)]
            ub = [psb(f"ub{k}", [128, TCH]) for k in range(2)]
            du_f = psb("du_f", [128, TCH])
            ab_f = [psb(f"ab{k}", [128, TCH]) for k in range(2)]
            inp2 = [psb(f"inp{k}", [128, TCH]) for k in range(2)]
            tmp2 = [psb(f"tmp{k}", [128, TCH]) for k in range(2)]
            h_all = psb("h_all", [128, DS * TCH])
            y_f = psb("y_f", [128, TCH])
            yg_f = psb("yg_f", [128, TCH])
            g_f = psb("g_f", [128, TCH])
            ffh = psb("ffh", [128, TCH], F16)
            e_f = psb("e_f", [128, TCH])
            r_scr = psb("r_scr", [128, TCH])
            x1t = [psb(f"b8x{k}", [128, D]) for k in range(4)]
            out_sb = [psb(f"b8o{k}", [128, D]) for k in range(4)]

            P.do("sync", lambda e: e.dma_start(
                al_sb[:].rearrange("p (j n) -> p j n", j=ND),
                A_log_d.rearrange("(j p) n -> p j n", p=128)), inc=[(sW, 16)])
            P.do("sync", lambda e: e.dma_start(
                dp_sb[:], Dp_d.rearrange("(j p) -> p j", p=128), allow_slow_non_contiguous=True),
                 inc=[(sW, 16)])
            wal = sW.v
            P.do("scalar", lambda e: e.activation(negAn[:], al_sb[:], ACT.Exp), wait=[(sW, wal)],
                 inc=[(sSc, 1)])
            wna = sSc.v
            P.do("vector", lambda e: e.tensor_scalar(negAn[:], negAn[:], -1.0, None, ALU.mult),
                 wait=[(sSc, wna)], inc=[(sV, 1)])
            wna2 = sV.v

            out_prev = 0
            ld_sc = [0, 0]
            ld_gp = [0, 0]
            ld_v = [0, 0]
            scan_used = [0, 0]
            tmp_used = [0, 0]
            carry_sc = [0] * 16
            for c in range(NCH):
                useB = getattr(P, "_b8useB", 0)
                useC = getattr(P, "_b8useC", 0)
                for (boff, dst, wt) in ((0, Bb, [(sV, useB)]), (DS, Cb, [(sV, useC)])):
                    P.do("sync", lambda e, c=c, boff=boff, dst=dst: e.dma_start(
                        dst[:].rearrange("p (n t) -> p n t", n=16),
                        bcT_scr[boff:boff + DS, c * TCH:(c + 1) * TCH].partition_broadcast(128)),
                         wait=wt, inc=[(sLe, 16)])
                wbcd = sLe.v
                for ii in range(4):
                    P.do("sync", lambda e, c=c, ii=ii: e.dma_start(
                        x1t[ii][:], x1_scr[(c * 4 + ii) * 128:(c * 4 + ii + 1) * 128, :]),
                         wait=[(sSd, out_prev)], inc=[(sLc, 16)])
                wx1 = sLc.v
                for j in range(ND):
                    k = j % 2
                    blkr = (j * NT + c * 4) * 128
                    P.do("sync", lambda e, blkr=blkr, k=k: e.dma_start(
                        dtb[k][:].rearrange("p (i2 t) -> p i2 t", i2=4),
                        dtT_scr[blkr:blkr + 512, :].rearrange("(i2 p) t -> p i2 t", p=128)),
                         wait=[(sSc, ld_sc[k]), (sV, ld_v[k])],
                         inc=[(sL[k], 16)])
                    P.do("sync", lambda e, blkr=blkr, k=k: e.dma_start(
                        ub[k][:].rearrange("p (i2 t) -> p i2 t", i2=4),
                        uT_scr[blkr:blkr + 512, :].rearrange("(i2 p) t -> p i2 t", p=128)),
                         inc=[(sL[k], 16)])
                    wld = sL[k].v
                    P.do("sync", lambda e, blkr=blkr: e.dma_start(
                        ffh[:].rearrange("p (i2 t) -> p i2 t", i2=4),
                        ffnT_scr[blkr:blkr + 512, :].rearrange("(i2 p) t -> p i2 t", p=128)),
                         wait=[(sV, getattr(P, "_b8g", 0))], inc=[(sLd, 16)])
                    wff = sLd.v
                    P.do("scalar", lambda e: e.activation(e_f[:], ffh[:], ACT.Tanh, scale=0.5),
                         wait=[(sLd, wff)], inc=[(sSc, 1)], chain=False)
                    wsg = sSc.v
                    P.do("vector", lambda e, k=k: e.tensor_tensor(du_f[:], dtb[k][:], ub[k][:], ALU.mult),
                         wait=[(sL[k], wld), (sLe, wbcd)], inc=[(sV, 1)], chain=False)
                    for n in range(2):
                        P.do("vector", lambda e, n=n: e.tensor_tensor(
                            inp2[n][:], du_f[:], Bb[:, n * TCH:(n + 1) * TCH], ALU.mult),
                             inc=[(sV, 1)], chain=False)
                    cm_done = {}
                    scan_v = {}
                    for n in range(16):
                        kn = n % 2
                        P.do("scalar", lambda e, j=j, n=n, k=k, kn=kn: e.activation(
                            ab_f[kn][:], dtb[k][:], ACT.Exp, scale=negAn[:, j * DS + n: j * DS + n + 1]),
                             wait=[(sL[k], wld), (sV, wna2), (sV, scan_used[kn])],
                             inc=[(sSc, 1)], chain=False)
                        wab = sSc.v
                        ini = 0.0 if c == 0 else carry[:, j * DS + n: j * DS + n + 1]
                        P.do("vector", lambda e, n=n, kn=kn, ini=ini: e.tensor_tensor_scan(
                            h_all[:, n * TCH:(n + 1) * TCH], ab_f[kn][:], inp2[kn][:], ini,
                            ALU.mult, ALU.add),
                             wait=[(sSc, wab), (sSc, carry_sc[n])],
                             inc=[(sV, 1)], chain=False)
                        scan_used[kn] = sV.v
                        scan_v[n] = sV.v
                        if n >= 2:
                            P.do("scalar", lambda e, j=j, n2=n - 2: e.activation(
                                carry[:, j * DS + n2: j * DS + n2 + 1],
                                h_all[:, (n2 + 1) * TCH - 1:(n2 + 1) * TCH], ACT.Copy),
                                 wait=[(sV, scan_v[n - 2])], inc=[(sSc, 1)], chain=False)
                            carry_sc[n - 2] = sSc.v
                        if n + 2 < 16:
                            P.do("vector", lambda e, n=n, kn=kn: e.tensor_tensor(
                                inp2[kn][:], du_f[:], Bb[:, (n + 2) * TCH:(n + 3) * TCH], ALU.mult),
                                 chain=False)
                        if n == 0:
                            P.do("vector", lambda e, n=n: e.tensor_tensor(
                                y_f[:], h_all[:, n * TCH:(n + 1) * TCH],
                                Cb[:, n * TCH:(n + 1) * TCH], ALU.mult),
                                 inc=[(sV, 1)], chain=False)
                        else:
                            P.do("vector", lambda e, n=n, kn=kn: e.tensor_tensor(
                                tmp2[kn][:], h_all[:, n * TCH:(n + 1) * TCH],
                                Cb[:, n * TCH:(n + 1) * TCH], ALU.mult),
                                 chain=False)
                            P.do("vector", lambda e, kn=kn: e.tensor_tensor(
                                y_f[:], y_f[:], tmp2[kn][:], ALU.add),
                                 inc=[(sV, 1)], chain=False)
                        cm_done[n] = sV.v
                    P._b8useC = cm_done[15]
                    P._b8useB = cm_done[15]
                    for n2 in (14, 15):
                        P.do("scalar", lambda e, j=j, n2=n2: e.activation(
                            carry[:, j * DS + n2: j * DS + n2 + 1],
                            h_all[:, (n2 + 1) * TCH - 1:(n2 + 1) * TCH], ACT.Copy),
                             wait=[(sV, scan_v[n2])], inc=[(sSc, 1)], chain=False)
                        carry_sc[n2] = sSc.v
                    ld_sc[k] = sSc.v
                    P.do("vector", lambda e: e.scalar_tensor_tensor(
                        g_f[:], e_f[:], 1.0, ffh[:], ALU.add, ALU.mult),
                         wait=[(sSc, wsg)], inc=[(sV, 1)], chain=False)
                    P._b8g = sV.v
                    P.do("vector", lambda e, j=j, k=k: e.scalar_tensor_tensor(
                        y_f[:], ub[k][:], dp_sb[:, j:j + 1], y_f[:], ALU.mult, ALU.add),
                         inc=[(sV, 1)], chain=False)
                    ld_v[k] = sV.v
                    P.do("vector", lambda e: e.scalar_tensor_tensor(
                        yg_f[:], y_f[:], 0.5, g_f[:], ALU.mult, ALU.mult),
                         inc=[(sV, 1)], chain=False)
                    wdel2 = sV.v
                    for ii in range(4):
                        def evac(e, pap, ii=ii, j=j):
                            return e.tensor_tensor(out_sb[ii][:, j * 128:(j + 1) * 128], pap,
                                                   x1t[ii][:, j * 128:(j + 1) * 128], ALU.add)
                        transp_one(yg_f[:, ii * 128:(ii + 1) * 128], ident,
                                   [(sV, wdel2), (sLc, wx1), (sSd, out_prev)], evac=evac)
                wout = sV.v
                for ii in range(4):
                    P.do("scalar", lambda e, c=c, ii=ii: e.dma_start(
                        y_out[(c * 4 + ii) * 128:(c * 4 + ii + 1) * 128, :], out_sb[ii][:]),
                         wait=[(sV, wout)], inc=[(sSd, 16)])
                out_prev = sSd.v
            P.do("gpsimd", lambda e: None, wait=[(sSd, sSd.v)])
            P.barrier()

    nc.finalize()
    return nc


_CACHE = {}


def _consts():
    ident = np.eye(128, dtype=np.float32)
    iota8 = np.tile(np.arange(NE, dtype=np.float32), (128, 1))
    freqs = 1.0 / (10000.0 ** (np.arange(0, RD, 2, dtype=np.float32) / RD))
    t = np.arange(S, dtype=np.float32)
    ang = np.outer(t, freqs)
    cosr = np.tile(np.cos(ang).astype(np.float32), (1, H))
    sinr = np.tile(np.sin(ang).astype(np.float32), (1, H))
    return ident, iota8, cosr, sinr


def kernel(**inputs):
    if "nc" not in _CACHE:
        _CACHE["nc"] = build_program()
    nc = _CACHE["nc"]
    ident, iota8, cosr, sinr = _consts()
    xfull = np.ascontiguousarray(inputs["L0_attn_x"], dtype=np.float32)
    common = {
        "Wq_down": inputs["L0_attn_Wq_down"], "q_norm_w": inputs["L0_attn_q_norm_w"],
        "Wq_up": inputs["L0_attn_Wq_up"], "Wq_rope": inputs["L0_attn_Wq_rope"],
        "Wkv_down": inputs["L0_attn_Wkv_down"], "kv_norm_w": inputs["L0_attn_kv_norm_w"],
        "Wk_up": inputs["L0_attn_Wk_up"], "Wv_up": inputs["L0_attn_Wv_up"],
        "Wk_rope": inputs["L0_attn_Wk_rope"], "Wout": inputs["L0_attn_Wout"],
        "gate_w": inputs["L0_ffn_gate_w"], "expert_w": inputs["L0_ffn_expert_w"],
        "expert_b": inputs["L0_ffn_expert_b"], "Wx": inputs["L1_attn_Wx"],
        "Wdt": inputs["L1_attn_Wdt"], "bdt": inputs["L1_attn_bdt"],
        "A_log": inputs["L1_attn_A_log"], "Dp": inputs["L1_attn_Dp"],
        "w8": inputs["L1_ffn_weight"], "w_scale": inputs["L1_ffn_w_scale"],
        "bias8": inputs["L1_ffn_bias"],
        "ident": ident, "iota8": iota8, "cosrep": cosr, "sinrep": sinr,
    }
    common = {k: np.ascontiguousarray(v) for k, v in common.items()}
    in_maps = []
    for b in range(8):
        m = dict(common)
        m["x"] = np.ascontiguousarray(xfull[b])
        in_maps.append(m)
    res = run_bass_kernel_spmd(nc, in_maps, core_ids=list(range(8)))
    out = np.stack([np.asarray(res.results[b]["y"], dtype=np.float32) for b in range(8)], axis=0)
    return out


# version 14
# speedup vs baseline: 1.3059x; 1.2024x over previous
import numpy as np
import ml_dtypes
from contextlib import ExitStack

import concourse.bass as bass
import concourse.mybir as mybir
from concourse.bass_utils import run_bass_kernel_spmd

F32 = mybir.dt.float32
BF16 = mybir.dt.bfloat16
F16 = mybir.dt.float16
I8 = mybir.dt.int8
ALU = mybir.AluOpType
ACT = mybir.ActivationFunctionType
AX = mybir.AxisListType

S, D = 2048, 2048
NT, ND = S // 128, D // 128
H, HD, RD = 16, 128, 64
DQL, DKVL = 384, 512
NE = 8
DI, DS, DTR = 2048, 16, 64
SCALE_ATTN = float((HD + RD) ** -0.5)
LOG_SCALE_ATTN = float(np.log(SCALE_ATTN))
RND_C = 12582912.0
BIG = 1.0e7
TCH = 512
NCH = S // TCH
LN_EPS = 1e-5
RMS_EPS = 1e-6
Q_EPS = 1e-7

ENGS = ("sync", "gpsimd", "vector", "scalar", "tensor")
CHAIN_ENGS = ("vector",)


class Ctr:
    __slots__ = ("h", "v", "synced")

    def __init__(self, h):
        self.h = h
        self.v = 0
        self.synced = 0


class Prog:
    def __init__(self, nc):
        self.nc = nc
        self.items = []
        self.last_wait = {}
        self.nblk = 0

    def do(self, eng, fn, wait=(), inc=(), chain=None):
        w = []
        for c, val in wait:
            if val is None or val <= 0:
                continue
            key = (eng, id(c))
            if self.last_wait.get(key, -1) >= val:
                continue
            self.last_wait[key] = val
            w.append((c.h, val))
        incs = []
        for c, n in inc:
            c.v += n
            incs.append((c.h, n))
        if chain is None:
            chain = eng in ("vector", "scalar")
        self.items.append((eng, w, fn, incs, chain))

    def barrier(self):
        for c in getattr(self, "dma_ctrs", []):
            if c.v > c.synced:
                self.do("sync", None, wait=[(c, c.v)])
                c.synced = c.v
        items = self.items
        self.items = []
        self.last_wait = {}
        if not items:
            return
        self.nblk += 1
        with self.nc.Block(f"seg{self.nblk}") as blk:
            for ename in ENGS:
                mine = [it for it in items if it[0] == ename]
                if not mine:
                    continue

                def body(eng, mine=mine, ename=ename):
                    for _, waits, fn, incs, chain in mine:
                        for h, val in waits:
                            eng.wait_ge(h, val)
                        if chain and fn is not None:
                            eng.drain()
                        inst = fn(eng) if fn is not None else None
                        if incs:
                            if inst is None:
                                inst = eng.drain() if ename != "sync" else eng.nop()
                            for h, n in incs:
                                inst = inst.then_inc(h, n)
                getattr(blk, ename)(body)


def build_program():
    nc = bass.Bass("TRN2", target_bir_lowering=False)

    def din(name, shape, dt=F32):
        return nc.dram_tensor(name, shape, dt, kind="ExternalInput").ap()

    x_in = din("x", [S, D])
    Wq_down_d = din("Wq_down", [D, DQL])
    q_norm_w_d = din("q_norm_w", [DQL])
    Wq_up_d = din("Wq_up", [DQL, H * HD])
    Wq_rope_d = din("Wq_rope", [D, H * RD])
    Wkv_down_d = din("Wkv_down", [D, DKVL])
    kv_norm_w_d = din("kv_norm_w", [DKVL])
    Wk_up_d = din("Wk_up", [DKVL, H * HD])
    Wv_up_d = din("Wv_up", [DKVL, H * HD])
    Wk_rope_d = din("Wk_rope", [D, H * RD])
    Wout_d = din("Wout", [H * HD, D])
    gate_w_d = din("gate_w", [NE, D])
    expert_w_d = din("expert_w", [NE, D, D])
    expert_b_d = din("expert_b", [NE, D])
    Wx_d = din("Wx", [DI, DTR + 2 * DS])
    Wdt_d = din("Wdt", [DTR, DI])
    bdt_d = din("bdt", [DI])
    A_log_d = din("A_log", [DI, DS])
    Dp_d = din("Dp", [DI])
    w8_d = din("w8", [D, D], I8)
    w_scale_d = din("w_scale", [D, 1])
    bias8_d = din("bias8", [D])
    ident_d = din("ident", [128, 128])
    iota8_d = din("iota8", [128, NE])
    cos_d = din("cosrep", [S, H * RD // 2])
    sin_d = din("sinrep", [S, H * RD // 2])

    y_out = nc.dram_tensor("y", [S, D], F32, kind="ExternalOutput").ap()

    def dscr(name, shape, dt):
        return nc.dram_tensor(name, shape, dt, kind="Internal").ap()

    g0_scr = dscr("g0_scr", [S, D], BF16)
    ao_scr = dscr("ao_scr", [H * HD, S], BF16)
    x1_scr = dscr("x1_scr", [S, D], F32)
    qrT_scr = dscr("qrT_scr", [H * RD, S], BF16)
    krT_scr = dscr("krT_scr", [H * RD, S], BF16)
    uT_scr = dscr("uT_scr", [ND * NT * 128, 128], F32)
    dtT_scr = dscr("dtT_scr", [ND * NT * 128, 128], F32)
    ffnT_scr = dscr("ffnT_scr", [ND * NT * 128, 128], F16)
    bcT_scr = dscr("bcT_scr", [2 * DS, S], F32)

    P = Prog(nc)
    es = ExitStack()
    with es:
        def sb(name, shape, dt=F32, side=None):
            return es.enter_context(nc.sbuf_tensor(name, shape, dt, side=side))

        def mkctr(name):
            return Ctr(es.enter_context(nc.semaphore(name)))

        sW = mkctr("sW")
        sLa = mkctr("sLa")
        sLb = mkctr("sLb")
        sLc = mkctr("sLc")
        sLd = mkctr("sLd")
        sLe = mkctr("sLe")
        sSa = mkctr("sSa")
        sSb = mkctr("sSb")
        sSe = mkctr("sSe")
        sSd = mkctr("sSd")
        sWg = mkctr("sWg")
        sLg0 = mkctr("sLg0")
        sLg1 = mkctr("sLg1")
        sT = mkctr("sT")
        sV = mkctr("sV")
        sSc = mkctr("sSc")
        sP = mkctr("sP")
        sL = [sLa, sLb]
        sLg = [sLg0, sLg1]
        sS2 = [sSa, sSb]
        P.dma_ctrs = [sW, sWg, sLa, sLb, sLc, sLd, sLe, sLg0, sLg1, sSa, sSb, sSe, sSd]

        ident = sb("identsb", [128, 128])
        ident_b = sb("ident_b", [128, 128], BF16)
        ident_h = sb("ident_h", [128, 128], F16)
        iota8 = sb("iota8sb", [128, NE])
        ones_b = sb("ones_b", [128, 1], BF16)
        onesr_b = sb("onesr_b", [1, 128], BF16)
        onesr_f = sb("onesr_f", [1, 128])
        wdense = sb("wdense", [128, NT * NE])
        wdenseT = sb("wdenseT", [8, NT * 128], BF16)
        insc = sb("insc", [128, NT])
        bc_sb = sb("bc_sb", [128, NT * 2 * DS])
        cols = [sb(f"col{i}", [128, 1]) for i in range(8)]
        c_lneps = sb("c_lneps", [128, 1])
        c_rmseps = sb("c_rmseps", [128, 1])
        c_lsa = sb("c_lsa", [128, 1])
        c_one = sb("c_one", [128, 1])
        rows8 = [sb(f"row8{i}", [128, NE]) for i in range(2)]

        PS = [es.enter_context(nc.psum_tensor(f"ps{i}", [128, 512], F32)) for i in range(8)]

        trb = {"bank": 0, 0: 0, 1: 0}

        def transp_one(src, idm, waits, dsts=(), evac=None, evac_inc=()):
            p, w = src.shape
            b = trb["bank"]
            trb["bank"] = 1 - b
            pb = PS[b]
            dt_ = src.dtype
            P.do("tensor",
                 lambda e, src=src, pb=pb, p=p, w=w, idm=idm, dt_=dt_:
                 e.transpose(pb.ap().bitcast(dt_)[0:w, 0:p], src, idm[0:p, 0:p]),
                 wait=list(waits) + [(sV, trb[b])], inc=[(sT, 1)])
            t = sT.v
            if evac is not None:
                P.do("vector", lambda e, pb=pb, p=p, w=w, dt_=dt_: evac(e, pb.ap().bitcast(dt_)[0:w, 0:p]),
                     wait=[(sT, t)], inc=[(sV, 1)] + list(evac_inc), chain=False)
            else:
                nd = len(dsts)
                for ii, dst in enumerate(dsts):
                    P.do("vector",
                         lambda e, dst=dst, pb=pb, p=p, w=w, dt_=dt_:
                         e.tensor_copy(dst, pb.ap().bitcast(dt_)[0:w, 0:p]),
                         wait=[(sT, t)],
                         inc=([(sV, 1)] + list(evac_inc)) if ii == nd - 1 else (),
                         chain=False)
            trb[b] = sV.v
            return sV.v

        P.do("sync", lambda e: e.dma_start(ident[:], ident_d), inc=[(sW, 16)])
        P.do("sync", lambda e: e.dma_start(iota8[:], iota8_d), inc=[(sW, 16)])
        w0 = sW.v
        P.do("vector", lambda e: e.tensor_copy(ident_b[:], ident[:]), wait=[(sW, w0)])
        P.do("vector", lambda e: e.tensor_copy(ident_h[:], ident[:]))
        P.do("vector", lambda e: e.memset(ones_b[:], 1.0))
        P.do("vector", lambda e: e.memset(onesr_b[:], 1.0))
        P.do("vector", lambda e: e.memset(onesr_f[:], 1.0))
        P.do("vector", lambda e: e.memset(c_lneps[:], LN_EPS))
        P.do("vector", lambda e: e.memset(c_rmseps[:], RMS_EPS))
        P.do("vector", lambda e: e.memset(c_lsa[:], LOG_SCALE_ATTN))
        P.do("vector", lambda e: e.memset(c_one[:], 1.0))
        P.barrier()

        rstk = ExitStack()
        xnT = rstk.enter_context(nc.sbuf_tensor("xnT", [128, ND * S], BF16, side="right"))

        def emit_ln(buf, out_f, eps, done_ctr_wait, final_wait=()):
            P.do("vector", lambda e, buf=buf: e.tensor_scalar(
                sq_f[:], buf[:], 1.0, None, ALU.mult, ALU.add, accum_out=cols[0][:]),
                 wait=done_ctr_wait, inc=[(sV, 1)])
            vsum = sV.v
            P.do("scalar", lambda e, buf=buf: e.activation(sq_f[:], buf[:], ACT.Square,
                                                           accum_out=cols[1][:]),
                 wait=done_ctr_wait + [(sV, vsum)], inc=[(sSc, 1)])
            wsq = sSc.v
            P.do("vector", lambda e: e.tensor_scalar(cols[0][:], cols[0][:], 1.0 / D, None, ALU.mult),
                 wait=[(sSc, wsq)])
            P.do("vector", lambda e: e.tensor_scalar(cols[1][:], cols[1][:], 1.0 / D, None, ALU.mult))
            P.do("vector", lambda e: e.tensor_tensor(cols[2][:], cols[0][:], cols[0][:], ALU.mult))
            P.do("vector", lambda e: e.tensor_tensor(cols[1][:], cols[1][:], cols[2][:], ALU.subtract),
                 inc=[(sV, 1)])
            wvar = sV.v
            P.do("scalar", lambda e, eps=eps: e.activation(cols[1][:], cols[1][:], ACT.Ln, bias=eps[:]),
                 wait=[(sV, wvar)])
            P.do("scalar", lambda e: e.activation(cols[1][:], cols[1][:], ACT.Exp, scale=-0.5),
                 inc=[(sSc, 1)])
            wrs = sSc.v
            P.do("vector", lambda e: e.tensor_scalar(cols[2][:], cols[0][:], cols[1][:], -1.0,
                                                     ALU.mult, ALU.mult),
                 wait=[(sSc, wrs)], inc=[(sV, 1)])
            wnm = sV.v
            P.do("scalar", lambda e, buf=buf, out_f=out_f: e.activation(
                out_f[:], buf[:], ACT.Identity, bias=cols[2][:], scale=cols[1][:]),
                 wait=[(sV, wnm)] + list(final_wait), inc=[(sSc, 1)])
            return sSc.v

        with ExitStack() as ph:
            def psb(n, s_, dt=F32):
                return ph.enter_context(nc.sbuf_tensor(n, s_, dt))

            gw_sb = psb("gw_sb", [8, D])
            gwT = psb("gwT", [128, ND * NE])
            x_b = [psb(f"x{k}", [128, D]) for k in range(2)]
            xn_f = psb("xn_f", [128, D])
            sq_f = psb("sq_f", [128, D])
            strip = psb("strip", [128, ND * 128])
            lg = psb("lg", [128, NE])

            P.do("sync", lambda e: e.dma_start(gw_sb[:], gate_w_d), inc=[(sW, 16)])
            wgw = sW.v
            for j in range(ND):
                transp_one(gw_sb[0:8, j * 128:(j + 1) * 128], ident, [(sW, wgw)],
                           [gwT[:, j * NE:(j + 1) * NE]])
            tile_done = [0] * NT
            xn_lastT = 0
            for i in range(NT):
                buf = x_b[i % 2]
                P.do("sync", lambda e, buf=buf, i=i: e.dma_start(buf[:], x_in[i * 128:(i + 1) * 128, :]),
                     wait=([(sSc, tile_done[i - 2])] if i >= 2 else []), inc=[(sL[i % 2], 16)])
                wx = sL[i % 2].v
                wxn = emit_ln(buf, xn_f, c_lneps, [(sL[i % 2], wx)], final_wait=[(sT, xn_lastT)])
                tile_done[i] = wxn
                for j in range(ND):
                    transp_one(xn_f[:, j * 128:(j + 1) * 128], ident, [(sSc, wxn)],
                               [strip[:, j * 128:(j + 1) * 128],
                                xnT[:, j * S + i * 128: j * S + (i + 1) * 128]])
                xn_lastT = sT.v
                wstrip = sV.v
                for j in range(ND):
                    P.do("tensor", lambda e, j=j: e.matmul(
                        PS[6].ap()[:, 0:NE], strip[:, j * 128:(j + 1) * 128],
                        gwT[:, j * NE:(j + 1) * NE], start=(j == 0), stop=(j == ND - 1)),
                         wait=[(sV, wstrip)], inc=([(sT, 1)] if j == ND - 1 else []))
                wlg = sT.v
                P.do("vector", lambda e: e.tensor_copy(lg[:], PS[6].ap()[:, 0:NE]),
                     wait=[(sT, wlg)], inc=[(sV, 1)])
                P.do("vector", lambda e: e.tensor_reduce(cols[0][:], lg[:], AX.X, ALU.max))
                P.do("vector", lambda e: e.tensor_scalar(rows8[0][:], lg[:], cols[0][:], None, ALU.is_equal))
                P.do("vector", lambda e: e.tensor_scalar(rows8[0][:], rows8[0][:], -BIG, BIG, ALU.mult, ALU.add))
                P.do("vector", lambda e: e.tensor_tensor(rows8[0][:], iota8[:], rows8[0][:], ALU.add))
                P.do("vector", lambda e: e.tensor_reduce(cols[1][:], rows8[0][:], AX.X, ALU.min))
                P.do("vector", lambda e: e.tensor_scalar(rows8[0][:], iota8[:], cols[1][:], None, ALU.is_equal))
                P.do("vector", lambda e: e.scalar_tensor_tensor(rows8[1][:], rows8[0][:], -BIG, lg[:],
                                                                ALU.mult, ALU.add))
                P.do("vector", lambda e: e.tensor_reduce(cols[2][:], rows8[1][:], AX.X, ALU.max))
                P.do("vector", lambda e: e.tensor_scalar(rows8[1][:], rows8[1][:], cols[2][:], None, ALU.is_equal))
                P.do("vector", lambda e: e.tensor_scalar(rows8[1][:], rows8[1][:], -BIG, BIG, ALU.mult, ALU.add))
                P.do("vector", lambda e: e.tensor_tensor(rows8[1][:], iota8[:], rows8[1][:], ALU.add))
                P.do("vector", lambda e: e.tensor_reduce(cols[3][:], rows8[1][:], AX.X, ALU.min))
                P.do("vector", lambda e: e.tensor_scalar(rows8[1][:], iota8[:], cols[3][:], None, ALU.is_equal))
                P.do("vector", lambda e: e.tensor_tensor(cols[4][:], cols[2][:], cols[0][:], ALU.subtract),
                     inc=[(sV, 1)])
                wd = sV.v
                P.do("scalar", lambda e: e.activation(cols[4][:], cols[4][:], ACT.Exp),
                     wait=[(sV, wd)], inc=[(sSc, 1)])
                we = sSc.v
                P.do("vector", lambda e: e.tensor_scalar(cols[5][:], cols[4][:], 1.0, None, ALU.add),
                     wait=[(sSc, we)])
                P.do("vector", lambda e: e.reciprocal(cols[5][:], cols[5][:]))
                P.do("vector", lambda e: e.tensor_tensor(cols[6][:], cols[4][:], cols[5][:], ALU.mult))
                P.do("vector", lambda e: e.tensor_scalar(rows8[0][:], rows8[0][:], cols[5][:], None, ALU.mult))
                P.do("vector", lambda e, i=i: e.scalar_tensor_tensor(
                    wdense[:, i * NE:(i + 1) * NE], rows8[1][:], cols[6][:], rows8[0][:],
                    ALU.mult, ALU.add), inc=[(sV, 1)])
                wwd = sV.v
                transp_one(wdense[:, i * NE:(i + 1) * NE], ident, [(sV, wwd)],
                           [wdenseT[:, i * 128:(i + 1) * 128]])
            P.barrier()

        with ExitStack() as ph:
            def psb(n, s_, dt=F32):
                return ph.enter_context(nc.sbuf_tensor(n, s_, dt))

            eb_sb = psb("eb_sb", [8, D], BF16)
            Wsb = [psb(f"Wsb{k}", [128, 4 * D], BF16) for k in range(2)]
            WeT = [psb(f"WeT{k}", [128, ND * 512], BF16) for k in range(2)]
            acc = psb("acc", [128, NT * 512], BF16)
            sil = psb("sil", [128, 512], BF16)

            P.do("gpsimd", lambda e: e.dma_start(eb_sb[:], expert_b_d), inc=[(sWg, 16)])
            web = sWg.v
            wet_pe_done = [0, 0]
            wsb_free = [0, 0]
            seed_v = 0
            acc_v = [0] * NT
            silu_done = [0] * NT
            st_g = 0
            for oc in range(4):
                for i in range(NT):
                    P.do("tensor", lambda e, i=i, oc=oc: e.matmul(
                        PS[7].ap()[:], wdenseT[:, i * 128:(i + 1) * 128],
                        eb_sb[0:8, oc * 512:(oc + 1) * 512], start=True, stop=True),
                         wait=[(sWg, web), (sV, seed_v)], inc=[(sT, 1)])
                    ws = sT.v
                    P.do("vector", lambda e, i=i: e.tensor_copy(acc[:, i * 512:(i + 1) * 512],
                                                                PS[7].ap()[:]),
                         wait=[(sT, ws), (sSc, silu_done[i])], inc=[(sV, 1)], chain=False)
                    seed_v = sV.v
                    acc_v[i] = sV.v
                for e_i in range(NE):
                    k = e_i % 2
                    P.do("gpsimd", lambda e, e_i=e_i, oc=oc, k=k: e.dma_start(
                        Wsb[k][:].rearrange("p (o c) -> p o c", o=4),
                        expert_w_d[e_i, oc * 512:(oc + 1) * 512, :].rearrange("(o p) c -> p o c", p=128)),
                         wait=[(sV, wsb_free[k])], inc=[(sLg[k], 16)])
                    wload = sLg[k].v
                    for ot in range(4):
                        for j in range(ND):
                            transp_one(Wsb[k][:, ot * D + j * 128: ot * D + (j + 1) * 128],
                                       ident_b, [(sLg[k], wload), (sT, wet_pe_done[k])],
                                       [WeT[k][:, j * 512 + ot * 128: j * 512 + (ot + 1) * 128]])
                    wsb_free[k] = sV.v
                    wtr = sV.v
                    for i in range(NT):
                        pb = PS[2 + (i % 2)]
                        bank_prev = acc_v[i - 2] if i >= 2 else acc_v[NT - 2 + i]
                        for j in range(ND):
                            P.do("tensor", lambda e, i=i, j=j, k=k, pb=pb: e.matmul(
                                pb.ap()[:], xnT[:, j * S + i * 128: j * S + (i + 1) * 128],
                                WeT[k][:, j * 512:(j + 1) * 512],
                                start=(j == 0), stop=(j == ND - 1)),
                                 wait=[(sV, wtr), (sV, bank_prev)],
                                 inc=([(sT, 1)] if j == ND - 1 else []))
                        wmm = sT.v
                        if i == NT - 1:
                            wet_pe_done[k] = wmm
                        P.do("vector", lambda e, i=i, e_i=e_i, pb=pb: e.scalar_tensor_tensor(
                            acc[:, i * 512:(i + 1) * 512], pb.ap()[:],
                            wdense[:, i * NE + e_i: i * NE + e_i + 1],
                            acc[:, i * 512:(i + 1) * 512], ALU.mult, ALU.add),
                             wait=[(sT, wmm)], inc=[(sV, 1)], chain=False)
                        acc_v[i] = sV.v
                for i in range(NT):
                    P.do("scalar", lambda e, i=i: e.activation(sil[:], acc[:, i * 512:(i + 1) * 512],
                                                               ACT.Silu),
                         wait=[(sV, acc_v[i]), (sSa, st_g)], inc=[(sSc, 1)], chain=False)
                    wsil = sSc.v
                    silu_done[i] = wsil
                    P.do("gpsimd", lambda e, i=i, oc=oc: e.dma_start(
                        g0_scr[i * 128:(i + 1) * 128, oc * 512:(oc + 1) * 512], sil[:]),
                         wait=[(sSc, wsil)], inc=[(sSa, 16)])
                    st_g = sSa.v
            P.barrier()

        c_qT = rstk.enter_context(nc.sbuf_tensor("c_qT", [128, 3 * S], BF16, side="right"))
        c_kvT = rstk.enter_context(nc.sbuf_tensor("c_kvT", [128, 4 * S], BF16, side="right"))
        with ExitStack() as ph:
            def psb(n, s_, dt=F32):
                return ph.enter_context(nc.sbuf_tensor(n, s_, dt))

            Wd_sb = psb("Wd_sb", [128, ND * DKVL], BF16)
            c_sb = psb("c_sb", [128, DKVL], BF16)
            csq = psb("csq", [128, DKVL])
            Wr_sb = psb("Wr_sb", [128, ND * 512], BF16)
            cs_c = psb("cs_c", [128, 256])
            cs_s = psb("cs_s", [128, 256])
            re_f = psb("re_f", [128, 256])
            ro_f = psb("ro_f", [128, 256])
            rr_sb = psb("rr_sb", [128, 512], BF16)
            rp_b = psb("rp_b", [128, 4 * 128], BF16)

            wfree = 0
            for (W_d, RL, cT) in ((Wq_down_d, DQL, c_qT), (Wkv_down_d, DKVL, c_kvT)):
                NR = RL // 128
                P.do("gpsimd", lambda e, W_d=W_d, RL=RL: e.dma_start(
                    Wd_sb[:, 0:ND * RL].rearrange("p (j c) -> p j c", j=ND),
                    W_d.rearrange("(j p) c -> p j c", p=128)),
                     wait=[(sV, wfree)], inc=[(sWg, 16)])
                wW = sWg.v
                psfree = [0, 0]
                for i in range(NT):
                    pb = PS[2 + (i % 2)]
                    for j in range(ND):
                        P.do("tensor", lambda e, i=i, j=j, RL=RL, pb=pb: e.matmul(
                            pb.ap()[:, 0:RL], xnT[:, j * S + i * 128: j * S + (i + 1) * 128],
                            Wd_sb[:, j * RL:(j + 1) * RL], start=(j == 0), stop=(j == ND - 1)),
                             wait=[(sWg, wW), (sSc, psfree[i % 2])],
                             inc=([(sT, 1)] if j == ND - 1 else []))
                    wmm = sT.v
                    P.do("scalar", lambda e, RL=RL, pb=pb: e.activation(
                        csq[:, 0:RL], pb.ap()[:, 0:RL], ACT.Square, accum_out=cols[0][:]),
                         wait=[(sT, wmm)], inc=[(sSc, 1)], chain=False)
                    wsq = sSc.v
                    P.do("vector", lambda e, RL=RL: e.tensor_scalar(
                        cols[0][:], cols[0][:], 1.0 / RL, None, ALU.mult),
                         wait=[(sSc, wsq)], inc=[(sV, 1)])
                    wv = sV.v
                    P.do("scalar", lambda e: e.activation(cols[0][:], cols[0][:], ACT.Ln, bias=c_rmseps[:]),
                         wait=[(sV, wv)])
                    P.do("scalar", lambda e: e.activation(cols[0][:], cols[0][:], ACT.Exp, scale=-0.5))
                    P.do("scalar", lambda e, RL=RL, pb=pb: e.activation(
                        c_sb[:, 0:RL], pb.ap()[:, 0:RL], ACT.Copy, scale=cols[0][:]),
                         wait=[(sT, getattr(P, "_b3ct", 0))], inc=[(sSc, 1)])
                    psfree[i % 2] = sSc.v
                    wcs = sSc.v
                    for j in range(NR):
                        transp_one(c_sb[:, j * 128:(j + 1) * 128], ident_b, [(sSc, wcs)],
                                   [cT[:, j * S + i * 128: j * S + (i + 1) * 128]])
                    P._b3ct = sT.v
                P.do("vector", lambda e: None, inc=[(sV, 1)])
                wfree = sV.v

            for (Wr_d, rT_scr) in ((Wq_rope_d, qrT_scr), (Wk_rope_d, krT_scr)):
                for half in range(2):
                    P.do("gpsimd", lambda e, Wr_d=Wr_d, half=half: e.dma_start(
                        Wr_sb[:].rearrange("p (j c) -> p j c", j=ND),
                        Wr_d[:, half * 512:(half + 1) * 512].rearrange("(j p) c -> p j c", p=128)),
                         wait=[(sV, wfree)], inc=[(sWg, 16)])
                    wW = sWg.v
                    psfree = [0, 0]
                    wst = 0
                    for i in range(NT):
                        pb = PS[2 + (i % 2)]
                        for j in range(ND):
                            P.do("tensor", lambda e, i=i, j=j, pb=pb: e.matmul(
                                pb.ap()[:], xnT[:, j * S + i * 128: j * S + (i + 1) * 128],
                                Wr_sb[:, j * 512:(j + 1) * 512], start=(j == 0), stop=(j == ND - 1)),
                                 wait=[(sWg, wW), (sV, psfree[i % 2])],
                                 inc=([(sT, 1)] if j == ND - 1 else []))
                        wmm = sT.v
                        P.do("sync", lambda e, i=i, half=half: e.dma_start(
                            cs_c[:], cos_d[i * 128:(i + 1) * 128, half * 256:(half + 1) * 256]),
                             wait=[(sV, getattr(P, "_b3cs", 0))], inc=[(sLc, 16)])
                        P.do("sync", lambda e, i=i, half=half: e.dma_start(
                            cs_s[:], sin_d[i * 128:(i + 1) * 128, half * 256:(half + 1) * 256]),
                             inc=[(sLc, 16)])
                        wcs2 = sLc.v
                        pe = pb.ap().rearrange("p (k two) -> p k two", two=2)
                        rv = rr_sb[:].rearrange("p (k two) -> p k two", two=2)
                        P.do("vector", lambda e, pe=pe: e.tensor_tensor(re_f[:], pe[:, :, 0], cs_c[:], ALU.mult),
                             wait=[(sT, wmm), (sLc, wcs2)], chain=False)
                        P.do("vector", lambda e, pe=pe: e.tensor_tensor(ro_f[:], pe[:, :, 1], cs_s[:], ALU.mult),
                             chain=False)
                        P.do("vector", lambda e, rv=rv: e.tensor_tensor(rv[:, :, 0], re_f[:], ro_f[:], ALU.subtract),
                             chain=False)
                        P.do("vector", lambda e, pe=pe: e.tensor_tensor(re_f[:], pe[:, :, 0], cs_s[:], ALU.mult),
                             chain=False)
                        P.do("vector", lambda e, pe=pe: e.tensor_tensor(ro_f[:], pe[:, :, 1], cs_c[:], ALU.mult),
                             inc=[(sV, 1)], chain=False)
                        psfree[i % 2] = sV.v
                        P._b3cs = sV.v
                        P.do("vector", lambda e, rv=rv: e.tensor_tensor(rv[:, :, 1], re_f[:], ro_f[:], ALU.add),
                             inc=[(sV, 1)], chain=False)
                        wrr = sV.v
                        for c4 in range(4):
                            transp_one(rr_sb[:, c4 * 128:(c4 + 1) * 128], ident_b,
                                       [(sV, wrr), (sSa, wst)],
                                       [rp_b[:, c4 * 128:(c4 + 1) * 128]])
                        wev = sV.v
                        P.do("gpsimd", lambda e, rT_scr=rT_scr, half=half, i=i: e.dma_start(
                            rT_scr[half * 512:(half + 1) * 512, i * 128:(i + 1) * 128]
                            .rearrange("(c p) t -> p c t", p=128),
                            rp_b[:].rearrange("p (c t) -> p c t", c=4)),
                             wait=[(sV, wev)], inc=[(sSa, 16)])
                        wst = sSa.v
                    P.do("vector", lambda e: None, wait=[(sSa, wst)], inc=[(sV, 1)])
                    wfree = sV.v
            P.barrier()

        with ExitStack() as ph:
            def psb(n, s_, dt=F32):
                return ph.enter_context(nc.sbuf_tensor(n, s_, dt))

            Wqu = psb("Wqu", [128, 3 * 2048], BF16)
            Wku = psb("Wku", [128, 4 * 2048], BF16)
            Wvu = psb("Wvu", [128, 4 * 2048], BF16)
            nwq = psb("nwq", [128, 3])
            nwkv = psb("nwkv", [128, 4])
            qTh = [psb(f"qTh{k}", [128, S], BF16) for k in range(2)]
            kTh = [psb(f"kTh{k}", [128, S], BF16) for k in range(2)]
            v_sb = [psb(f"v_sb{k}", [128, NT * 128], BF16) for k in range(2)]
            qr_h = [psb(f"qr_h{k}", [64, S], BF16) for k in range(2)]
            kr_h = [psb(f"kr_h{k}", [64, S], BF16) for k in range(2)]
            sq_b = psb("sq_b", [128, S], BF16)
            sqr_b = psb("sqr_b", [64, S], BF16)
            row4 = psb("row4", [1, 4])
            sca = psb("sca", [1, 1])
            scb = psb("scb", [1, 1])
            shiftb = [psb(f"shiftb{k}", [128, 1]) for k in range(2)]
            ex_b = [psb(f"ex_b{k}", [128, 512], BF16) for k in range(2)]
            rec_f = [psb(f"rec_f{k}", [1, 512]) for k in range(2)]
            rec_b = [psb(f"rec_b{k}", [1, 512], BF16) for k in range(2)]
            rb_f = [psb(f"rb_f{k}", [128, 512]) for k in range(2)]
            ao_b = [psb(f"ao_b{k}", [128, 512], BF16) for k in range(2)]

            P.do("gpsimd", lambda e: e.dma_start(
                Wqu[:].rearrange("p (j c) -> p j c", j=3), Wq_up_d.rearrange("(j p) c -> p j c", p=128)),
                 inc=[(sWg, 16)])
            P.do("gpsimd", lambda e: e.dma_start(
                Wku[:].rearrange("p (j c) -> p j c", j=4), Wk_up_d.rearrange("(j p) c -> p j c", p=128)),
                 inc=[(sWg, 16)])
            P.do("gpsimd", lambda e: e.dma_start(
                Wvu[:].rearrange("p (j c) -> p j c", j=4), Wv_up_d.rearrange("(j p) c -> p j c", p=128)),
                 inc=[(sWg, 16)])
            P.do("sync", lambda e: e.dma_start(nwq[:, 0:3], q_norm_w_d.rearrange("(j p) -> p j", p=128),
                                               allow_slow_non_contiguous=True), inc=[(sW, 16)])
            P.do("sync", lambda e: e.dma_start(nwkv[:, 0:4], kv_norm_w_d.rearrange("(j p) -> p j", p=128),
                                               allow_slow_non_contiguous=True), inc=[(sW, 16)])
            wld = sWg.v
            wldn = sW.v
            for j in range(3):
                P.do("vector", lambda e, j=j: e.tensor_scalar(
                    Wqu[:, j * 2048:(j + 1) * 2048], Wqu[:, j * 2048:(j + 1) * 2048],
                    nwq[:, j:j + 1], None, ALU.mult), wait=[(sWg, wld), (sW, wldn)], chain=False)
            for j in range(4):
                P.do("vector", lambda e, j=j: e.tensor_scalar(
                    Wku[:, j * 2048:(j + 1) * 2048], Wku[:, j * 2048:(j + 1) * 2048],
                    nwkv[:, j:j + 1], None, ALU.mult), chain=False)
                P.do("vector", lambda e, j=j: e.tensor_scalar(
                    Wvu[:, j * 2048:(j + 1) * 2048], Wvu[:, j * 2048:(j + 1) * 2048],
                    nwkv[:, j:j + 1], None, ALU.mult), inc=([(sV, 1)] if j == 3 else []), chain=False)
            wwup = sV.v

            proj_rd = [0, 0]
            shift_rd = [0, 0]
            shift_gp = [0, 0]
            exp_done = {2: 0, 3: 0}
            av_free = [0, 0]
            den_free = [0, 0]
            pend_norm = [None, None]
            ao_st = [0, 0]
            bank01_v = [0, 0]
            for h in range(H):
                k = h % 2
                P.do("sync", lambda e, h=h, k=k: e.dma_start(qr_h[k][:], qrT_scr[h * 64:(h + 1) * 64, :]),
                     wait=[(sT, proj_rd[k])], inc=[(sL[k], 16)])
                P.do("sync", lambda e, h=h, k=k: e.dma_start(kr_h[k][:], krT_scr[h * 64:(h + 1) * 64, :]),
                     inc=[(sL[k], 16)])
                wrope = sL[k].v
                for qb in range(4):
                    b = qb % 2
                    pb = PS[b]
                    for jr in range(3):
                        P.do("tensor", lambda e, h=h, qb=qb, jr=jr, pb=pb: e.matmul(
                            pb.ap()[:], Wqu[:, jr * 2048 + h * 128: jr * 2048 + (h + 1) * 128],
                            c_qT[:, jr * S + qb * 512: jr * S + (qb + 1) * 512],
                            start=(jr == 0), stop=(jr == 2)),
                             wait=[(sV, wwup), (sV, bank01_v[b])],
                             inc=([(sT, 1)] if jr == 2 else []))
                    wmm = sT.v
                    P.do("vector", lambda e, qb=qb, k=k, pb=pb: e.tensor_copy(
                        qTh[k][:, qb * 512:(qb + 1) * 512], pb.ap()[:]),
                         wait=[(sT, wmm), (sT, proj_rd[k])], inc=[(sV, 1)], chain=False)
                    bank01_v[b] = sV.v
                for kb in range(4):
                    b = kb % 2
                    pb = PS[b]
                    for jr in range(4):
                        P.do("tensor", lambda e, h=h, kb=kb, jr=jr, pb=pb: e.matmul(
                            pb.ap()[:], Wku[:, jr * 2048 + h * 128: jr * 2048 + (h + 1) * 128],
                            c_kvT[:, jr * S + kb * 512: jr * S + (kb + 1) * 512],
                            start=(jr == 0), stop=(jr == 3)),
                             wait=[(sV, bank01_v[b])], inc=([(sT, 1)] if jr == 3 else []))
                    wmm = sT.v
                    P.do("vector", lambda e, kb=kb, k=k, pb=pb: e.tensor_copy(
                        kTh[k][:, kb * 512:(kb + 1) * 512], pb.ap()[:]),
                         wait=[(sT, wmm)], inc=[(sV, 1)], chain=False)
                    bank01_v[b] = sV.v
                wproj = sV.v
                for (th, rh, sc_dst) in ((qTh, qr_h, sca), (kTh, kr_h, scb)):
                    P.do("scalar", lambda e, k=k, th=th: e.activation(sq_b[:], th[k][:], ACT.Square),
                         wait=[(sV, wproj), (sT, getattr(P, "_b4sq", 0))], inc=[(sSc, 1)],
                         chain=False)
                    P.do("scalar", lambda e, k=k, rh=rh: e.activation(sqr_b[:], rh[k][:], ACT.Square),
                         wait=[(sL[k], wrope)], inc=[(sSc, 1)], chain=False)
                    wsq2 = sSc.v
                    for qb in range(4):
                        b = qb % 2
                        P.do("tensor", lambda e, qb=qb, b=b: e.matmul(
                            PS[b].ap()[0:1, :], ones_b[:], sq_b[:, qb * 512:(qb + 1) * 512],
                            start=True, stop=False),
                             wait=[(sSc, wsq2), (sV, bank01_v[b])])
                        P.do("tensor", lambda e, qb=qb, b=b: e.matmul(
                            PS[b].ap()[0:1, :], ones_b[0:64, :], sqr_b[0:64, qb * 512:(qb + 1) * 512],
                            start=False, stop=True), inc=[(sT, 1)])
                        wnr = sT.v
                        P.do("vector", lambda e, qb=qb, b=b: e.tensor_reduce(
                            row4[0:1, qb:qb + 1], PS[b].ap()[0:1, :], AX.X, ALU.max),
                             wait=[(sT, wnr)], inc=[(sV, 1)])
                        bank01_v[b] = sV.v
                    P._b4sq = sT.v
                    P.do("vector", lambda e, sc_dst=sc_dst: e.tensor_reduce(
                        sc_dst[0:1, :], row4[0:1, :], AX.X, ALU.max),
                         inc=[(sV, 1)])
                wmk = sV.v
                P.do("scalar", lambda e: e.activation(sca[0:1, :], sca[0:1, :], ACT.Ln),
                     wait=[(sV, wmk)])
                P.do("scalar", lambda e: e.activation(scb[0:1, :], scb[0:1, :], ACT.Ln),
                     inc=[(sSc, 1)])
                wln = sSc.v
                P.do("vector", lambda e: e.tensor_tensor(sca[0:1, :], sca[0:1, :], scb[0:1, :], ALU.add),
                     wait=[(sSc, wln)], inc=[(sV, 1)])
                wadd = sV.v
                P.do("scalar", lambda e: e.activation(sca[0:1, :], sca[0:1, :], ACT.Exp,
                                                      scale=0.5, bias=c_lsa[0:1, :]),
                     wait=[(sV, wadd)], inc=[(sSc, 1)])
                wexp = sSc.v
                P.do("vector", lambda e: e.tensor_scalar(sca[0:1, :], sca[0:1, :], -1.0, None, ALU.mult),
                     wait=[(sSc, wexp)], inc=[(sV, 1)])
                wneg = sV.v
                for tt in range(NT):
                    b = tt % 2
                    pb = PS[b]
                    for jr in range(4):
                        P.do("tensor", lambda e, h=h, tt=tt, jr=jr, pb=pb: e.matmul(
                            pb.ap()[:, 0:128], c_kvT[:, jr * S + tt * 128: jr * S + (tt + 1) * 128],
                            Wvu[:, jr * 2048 + h * 128: jr * 2048 + (h + 1) * 128],
                            start=(jr == 0), stop=(jr == 3)),
                             wait=[(sV, bank01_v[b])], inc=([(sT, 1)] if jr == 3 else []))
                    wmm = sT.v
                    P.do("vector", lambda e, tt=tt, k=k, pb=pb: e.tensor_copy(
                        v_sb[k][:, tt * 128:(tt + 1) * 128], pb.ap()[:, 0:128]),
                         wait=[(sT, wmm)], inc=[(sV, 1)], chain=False)
                    bank01_v[b] = sV.v
                P.do("tensor", lambda e: e.matmul(PS[0].ap()[:, 0:1], onesr_f[:], sca[0:1, 0:1],
                                                  start=True, stop=True),
                     wait=[(sV, wneg), (sV, bank01_v[0])], inc=[(sT, 1)])
                wbc = sT.v
                P.do("vector", lambda e, k=k: e.tensor_copy(shiftb[k][:], PS[0].ap()[:, 0:1]),
                     wait=[(sT, wbc), (sSc, shift_rd[k])], inc=[(sV, 1)])
                bank01_v[0] = sV.v
                shift_done = sV.v
                for qb in range(4):
                    a = qb % 2
                    exp_v = {}

                    def flush_norm(a2):
                        if pend_norm[a2] is None:
                            return
                        h2, qb2, wavden2, wrecb = pend_norm[a2]
                        pend_norm[a2] = None
                        P.do("tensor", lambda e, a2=a2: e.matmul(
                            PS[6 + a2].ap()[:], onesr_b[:], rec_b[a2][0:1, :],
                            start=True, stop=True),
                             wait=[(sV, wrecb)], inc=[(sT, 1)])
                        wbc2 = sT.v
                        P.do("vector", lambda e, a2=a2: e.tensor_copy(rb_f[a2][:], PS[6 + a2].ap()[:]),
                             wait=[(sT, wbc2)], chain=False)
                        P.do("vector", lambda e, a2=a2: e.tensor_tensor(
                            ao_b[a2][:], PS[4 + a2].ap()[:], rb_f[a2][:], ALU.mult),
                             wait=[(sS2[a2], ao_st[a2])],
                             inc=[(sV, 1)], chain=False)
                        av_free[a2] = sV.v
                        den_free[a2] = sV.v
                        P.do("sync", lambda e, h2=h2, qb2=qb2, a2=a2: e.dma_start(
                            ao_scr[h2 * 128:(h2 + 1) * 128, qb2 * 512:(qb2 + 1) * 512],
                            ao_b[a2][:]),
                             wait=[(sV, av_free[a2])], inc=[(sS2[a2], 16)])
                        ao_st[a2] = sS2[a2].v

                    def emit_avden(kt, a=a, k=k, qb=qb, exp_v=exp_v):
                        P.do("tensor", lambda e, k=k, kt=kt, a=a: e.matmul(
                            PS[4 + a].ap()[:], v_sb[k][:, kt * 128:(kt + 1) * 128], ex_b[kt % 2][:],
                            start=(kt == 0), stop=(kt == NT - 1)),
                             wait=[(sSc, exp_v[kt])] + ([(sV, av_free[a])] if kt == 0 else []),
                             inc=([(sT, 1)] if kt == NT - 1 else []))
                        P.do("tensor", lambda e, kt=kt, a=a: e.matmul(
                            PS[6 + a].ap()[0:1, :], ones_b[:], ex_b[kt % 2][:],
                            start=(kt == 0), stop=(kt == NT - 1)),
                             wait=([(sV, den_free[a])] if kt == 0 else []),
                             inc=([(sT, 1)] if kt == NT - 1 else []))

                    for kt in range(NT):
                        if kt == 8:
                            flush_norm(1 - a)
                        if kt >= 2:
                            emit_avden(kt - 2)
                        sb_i = 2 + (kt % 2)
                        pb = PS[sb_i]
                        P.do("tensor", lambda e, h=h, k=k, kt=kt, qb=qb, pb=pb: e.matmul(
                            pb.ap()[:], kTh[k][:, kt * 128:(kt + 1) * 128],
                            qTh[k][:, qb * 512:(qb + 1) * 512], start=True, stop=False),
                             wait=[(sSc, exp_done[sb_i])])
                        P.do("tensor", lambda e, h=h, k=k, kt=kt, qb=qb, pb=pb: e.matmul(
                            pb.ap()[:], kr_h[k][0:64, kt * 128:(kt + 1) * 128],
                            qr_h[k][0:64, qb * 512:(qb + 1) * 512], start=False, stop=True),
                             inc=[(sT, 1)])
                        wsc = sT.v
                        if qb == 3 and kt == NT - 1:
                            proj_rd[k] = wsc
                        P.do("scalar", lambda e, kt=kt, k=k, pb=pb: e.activation(
                            ex_b[kt % 2][:], pb.ap()[:], ACT.Exp, scale=SCALE_ATTN,
                            bias=shiftb[k][:]),
                             wait=[(sT, wsc), (sV, shift_done)], inc=[(sSc, 1)], chain=False)
                        exp_done[sb_i] = sSc.v
                        exp_v[kt] = sSc.v
                    emit_avden(NT - 2)
                    emit_avden(NT - 1)
                    if qb == 3:
                        shift_rd[k] = exp_v[NT - 1]
                    wavden = sT.v
                    P.do("vector", lambda e, a=a: e.reciprocal(rec_f[a][0:1, :], PS[6 + a].ap()[0:1, :]),
                         wait=[(sT, wavden)], inc=[(sV, 1)], chain=False)
                    P.do("vector", lambda e, a=a: e.tensor_copy(rec_b[a][0:1, :], rec_f[a][0:1, :]),
                         inc=[(sV, 1)], chain=False)
                    pend_norm[a] = (h, qb, wavden, sV.v)
            flush_norm(0)
            flush_norm(1)
            P.barrier()
        rstk.close()

        with ExitStack() as ph:
            def psb(n, s_, dt=F32):
                return ph.enter_context(nc.sbuf_tensor(n, s_, dt))

            Wo_sb = psb("Wo_sb", [128, ND * 2048], BF16)
            aoL = psb("aoL", [128, H * 2048], BF16)
            x_sb = [psb(f"b5x{k}", [128, D]) for k in range(2)]
            g0_sb = [psb(f"b5g{k}", [128, D], BF16) for k in range(2)]
            sg_f = psb("sg_f", [128, 512])
            x1_sb = [psb(f"b5o{k}", [128, D]) for k in range(2)]

            P.do("gpsimd", lambda e: e.dma_start(
                Wo_sb[:].rearrange("p (j c) -> p j c", j=ND), Wout_d.rearrange("(j p) c -> p j c", p=128)),
                 inc=[(sWg, 16)])
            P.do("sync", lambda e: e.dma_start(
                aoL[:].rearrange("p (j c) -> p j c", j=H), ao_scr.rearrange("(j p) c -> p j c", p=128)),
                 inc=[(sW, 16)])
            wW = sWg.v
            wao2 = sW.v
            st_x1 = [0, 0]
            for i in range(NT):
                P.do("sync", lambda e, i=i: e.dma_start(x_sb[i % 2][:], x_in[i * 128:(i + 1) * 128, :]),
                     wait=[(sS2[i % 2], st_x1[i % 2])], inc=[(sL[i % 2], 16)])
                P.do("sync", lambda e, i=i: e.dma_start(g0_sb[i % 2][:], g0_scr[i * 128:(i + 1) * 128, :]),
                     inc=[(sL[i % 2], 16)])
                wx = sL[i % 2].v
                for nb in range(4):
                    pb = PS[2 + (nb % 2)]
                    for ct in range(H):
                        P.do("tensor", lambda e, i=i, nb=nb, ct=ct, pb=pb: e.matmul(
                            pb.ap()[:], aoL[:, ct * 2048 + i * 128: ct * 2048 + (i + 1) * 128],
                            Wo_sb[:, ct * 2048 + nb * 512: ct * 2048 + (nb + 1) * 512],
                            start=(ct == 0), stop=(ct == H - 1)),
                             wait=[(sWg, wW), (sW, wao2), (sV, getattr(P, f"_b5p{nb % 2}", 0))],
                             inc=([(sT, 1)] if ct == H - 1 else []))
                    wmm = sT.v
                    P.do("vector", lambda e, i=i, nb=nb, pb=pb: e.tensor_tensor(
                        sg_f[:], pb.ap()[:], g0_sb[i % 2][:, nb * 512:(nb + 1) * 512], ALU.mult),
                         wait=[(sT, wmm), (sL[i % 2], wx)], inc=[(sV, 1)], chain=False)
                    setattr(P, f"_b5p{nb % 2}", sV.v)
                    P.do("vector", lambda e, i=i, nb=nb: e.tensor_tensor(
                        x1_sb[i % 2][:, nb * 512:(nb + 1) * 512], sg_f[:],
                        x_sb[i % 2][:, nb * 512:(nb + 1) * 512], ALU.add),
                         inc=([(sV, 1)] if nb == 3 else []), chain=False)
                wx1 = sV.v
                P.do("gpsimd", lambda e, i=i: e.dma_start(x1_scr[i * 128:(i + 1) * 128, :], x1_sb[i % 2][:]),
                     wait=[(sV, wx1)], inc=[(sS2[i % 2], 16)])
                st_x1[i % 2] = sS2[i % 2].v
            P.barrier()

        qT = rstk.enter_context(nc.sbuf_tensor("qT", [128, ND * S], BF16, side="right"))
        with ExitStack() as ph:
            def psb(n, s_, dt=F32):
                return ph.enter_context(nc.sbuf_tensor(n, s_, dt))

            x1_b = [psb(f"b6x{k}", [128, D]) for k in range(2)]
            xn_f = psb("b6xn", [128, D])
            sq_f = psb("b6sq", [128, D])
            scr_f = psb("b6scr", [128, D])
            q_b = psb("b6q", [128, D], BF16)
            strip = psb("b6strip", [128, ND * 128])
            strip2 = psb("b6strip2", [128, ND * 128])
            Wx_sb = psb("Wx_sb", [128, ND * 96])
            Wdt_sb = psb("Wdt_sb", [128, D])
            del_f = psb("del_f", [128, 65])
            delT = psb("delT", [128, 128])
            dt_f = psb("dt_f", [128, D])
            bt6 = psb("bt6", [32, 128])

            P.do("sync", lambda e: e.dma_start(
                Wx_sb[:].rearrange("p (j c) -> p j c", j=ND), Wx_d.rearrange("(j p) c -> p j c", p=128)),
                 inc=[(sW, 16)])
            P.do("sync", lambda e: e.dma_start(Wdt_sb[0:64, :], Wdt_d), inc=[(sW, 16)])
            P.do("sync", lambda e: e.dma_start(Wdt_sb[64:65, :], bdt_d.rearrange("(o c) -> o c", c=DI)),
                 inc=[(sW, 16)])
            wwx = sW.v
            P.do("vector", lambda e: e.memset(del_f[:, 64:65], 1.0))
            tile_done = [0] * NT
            st_u = 0
            st_dt = 0
            xn_lastT = 0
            wq = 0
            for i in range(NT):
                P.do("sync", lambda e, i=i: e.dma_start(x1_b[i % 2][:], x1_scr[i * 128:(i + 1) * 128, :]),
                     wait=([(sSc, tile_done[i - 2])] if i >= 2 else []), inc=[(sL[i % 2], 16)])
                wx = sL[i % 2].v
                wxn = emit_ln(x1_b[i % 2], xn_f, c_lneps, [(sL[i % 2], wx)],
                              final_wait=[(sT, xn_lastT), (sV, wq)])
                P.do("vector", lambda e: e.reduce_max(cols[3][:], xn_f[:], axis=AX.X,
                                                      apply_absolute_value=True),
                     wait=[(sSc, wxn)], chain=False)
                P.do("vector", lambda e: e.tensor_scalar(cols[3][:], cols[3][:], Q_EPS, None, ALU.max))
                P.do("vector", lambda e, i=i: e.tensor_scalar(insc[:, i:i + 1], cols[3][:], 1.0 / 127.0,
                                                              None, ALU.mult))
                P.do("vector", lambda e: e.reciprocal(cols[4][:], cols[3][:]))
                P.do("vector", lambda e: e.tensor_scalar(cols[4][:], cols[4][:], 127.0, None, ALU.mult))
                P.do("vector", lambda e: e.tensor_scalar(scr_f[:], xn_f[:], cols[4][:], None, ALU.mult))
                P.do("vector", lambda e: e.tensor_scalar(q_b[:], scr_f[:], RND_C, -RND_C, ALU.add, ALU.add),
                     inc=[(sV, 1)], chain=False)
                wq = sV.v
                for j in range(ND):
                    transp_one(q_b[:, j * 128:(j + 1) * 128], ident_b, [(sV, wq)],
                               [qT[:, j * S + i * 128: j * S + (i + 1) * 128]])
                for j in range(ND):
                    transp_one(xn_f[:, j * 128:(j + 1) * 128], ident, [(sSc, wxn), (sSa, st_u)],
                               [strip[:, j * 128:(j + 1) * 128]])
                xn_lastT = sT.v
                wstrip = sV.v
                tile_done[i] = wxn
                for hhalf in range(2):
                    P.do("gpsimd", lambda e, i=i, hh=hhalf: e.dma_start(
                        uT_scr[0:0 + ND * NT * 128, :]
                        .rearrange("(j i2 p) t -> p j i2 t", p=128, i2=NT)[:, hh * 8:(hh + 1) * 8, i, :],
                        strip[:, hh * 8 * 128:(hh + 1) * 8 * 128].rearrange("p (j t) -> p j t", j=8)),
                         wait=[(sV, wstrip)], inc=[(sSa, 16)])
                st_u = sSa.v
                for j in range(ND):
                    P.do("tensor", lambda e, j=j: e.matmul(
                        PS[7].ap()[:, 0:96], strip[:, j * 128:(j + 1) * 128],
                        Wx_sb[:, j * 96:(j + 1) * 96], start=(j == 0), stop=(j == ND - 1)),
                         wait=[(sW, wwx), (sV, wstrip), (sV, getattr(P, "_b6xz", 0))],
                         inc=([(sT, 1)] if j == ND - 1 else []))
                wxz = sT.v
                P.do("vector", lambda e: e.tensor_copy(del_f[:, 0:64], PS[7].ap()[:, 0:64]),
                     wait=[(sT, wxz)], chain=False)
                P.do("vector", lambda e, i=i: e.tensor_copy(bc_sb[:, i * 32:(i + 1) * 32],
                                                            PS[7].ap()[:, 64:96]), inc=[(sV, 1)], chain=False)
                P._b6xz = sV.v
                wdel = sV.v
                transp_one(bc_sb[:, i * 32:(i + 1) * 32], ident,
                           [(sV, wdel), (sSe, getattr(P, "_b6bc", 0))], [bt6[:]])
                wbt6 = sV.v
                P.do("gpsimd", lambda e, i=i: e.dma_start(
                    bcT_scr[:, i * 128:(i + 1) * 128], bt6[:]),
                     wait=[(sV, wbt6)], inc=[(sSe, 16)])
                P._b6bc = sSe.v
                transp_one(del_f[:], ident, [(sV, wdel)], [delT[0:65, :]])
                wdelT = sV.v
                for nb in range(4):
                    pb = PS[2 + (nb % 2)]
                    P.do("tensor", lambda e, nb=nb, pb=pb: e.matmul(
                        pb.ap()[:], delT[0:65, :], Wdt_sb[0:65, nb * 512:(nb + 1) * 512],
                        start=True, stop=True),
                         wait=[(sW, wwx), (sV, wdelT), (sSc, getattr(P, f"_b6dt{nb % 2}", 0))],
                         inc=[(sT, 1)])
                    wdt = sT.v
                    P.do("scalar", lambda e, nb=nb, pb=pb: e.activation(
                        scr_f[:, 0:512], pb.ap()[:], ACT.Exp),
                         wait=[(sT, wdt), (sV, wq)], inc=[(sSc, 1)], chain=False)
                    setattr(P, f"_b6dt{nb % 2}", sSc.v)
                    P.do("scalar", lambda e, nb=nb: e.activation(
                        dt_f[:, nb * 512:(nb + 1) * 512], scr_f[:, 0:512], ACT.Ln, bias=c_one[:]),
                         inc=[(sSc, 1)], chain=False)
                wsp = sSc.v
                for j in range(ND):
                    transp_one(dt_f[:, j * 128:(j + 1) * 128], ident, [(sSc, wsp), (sSb, st_dt)],
                               [strip2[:, j * 128:(j + 1) * 128]])
                wst2 = sV.v
                for hhalf in range(2):
                    P.do("gpsimd", lambda e, i=i, hh=hhalf: e.dma_start(
                        dtT_scr.rearrange("(j i2 p) t -> p j i2 t", p=128, i2=NT)[:, hh * 8:(hh + 1) * 8, i, :],
                        strip2[:, hh * 8 * 128:(hh + 1) * 8 * 128].rearrange("p (j t) -> p j t", j=8)),
                         wait=[(sV, wst2)], inc=[(sSb, 16)])
                st_dt = sSb.v
            P.barrier()

        with ExitStack() as ph:
            def psb(n, s_, dt=F32):
                return ph.enter_context(nc.sbuf_tensor(n, s_, dt))

            w8_sb = [psb(f"w8s{k}", [128, 4 * D], I8) for k in range(2)]
            w8b = [psb(f"w8b{k}", [128, 4 * D], BF16) for k in range(2)]
            wT = [psb(f"wT{k}", [128, ND * 512], BF16) for k in range(2)]
            wsc_bc = psb("wsc_bc", [128, D])
            bias_bc = psb("bias_bc", [128, D])
            dq_f = psb("dq_f", [128, 512])
            ff_h = psb("ff_h", [128, 512], F16)
            fp_b = psb("fp_b", [128, 4 * 128], F16)

            P.do("gpsimd", lambda e: e.dma_start(
                wsc_bc[:], w_scale_d.rearrange("c o -> o c").partition_broadcast(128)
                .rearrange("p o c -> p (o c)")), inc=[(sWg, 16)])
            P.do("gpsimd", lambda e: e.dma_start(
                bias_bc[:], bias8_d.rearrange("(o c) -> o c", c=D).partition_broadcast(128)
                .rearrange("p o c -> p (o c)")), inc=[(sWg, 16)])
            wrows = sWg.v
            P.do("vector", lambda e: None, wait=[(sWg, wrows)], inc=[(sV, 1)])

            wt_pe_done = [0, 0]
            st_ff = 0
            for oc in range(4):
                k = oc % 2
                P.do("sync", lambda e, oc=oc, k=k: e.dma_start(
                    w8_sb[k][:].rearrange("p (o c) -> p o c", o=4),
                    w8_d[oc * 512:(oc + 1) * 512, :].rearrange("(o p) c -> p o c", p=128)),
                     wait=[(sT, wt_pe_done[k])], inc=[(sL[k], 16)])
                wld = sL[k].v
                for ot in range(4):
                    P.do("vector", lambda e, k=k, ot=ot: e.tensor_copy(
                        w8b[k][:, ot * D:(ot + 1) * D], w8_sb[k][:, ot * D:(ot + 1) * D]),
                         wait=[(sL[k], wld)], inc=[(sV, 1)], chain=False)
                wup = sV.v
                for ot in range(4):
                    for j in range(ND):
                        transp_one(w8b[k][:, ot * D + j * 128: ot * D + (j + 1) * 128],
                                   ident_b, [(sV, wup), (sT, wt_pe_done[k])],
                                   [wT[k][:, j * 512 + ot * 128: j * 512 + (ot + 1) * 128]])
                wtr = sV.v
                for i in range(NT):
                    pb = PS[2 + (i % 2)]
                    for j in range(ND):
                        P.do("tensor", lambda e, i=i, j=j, k=k, pb=pb: e.matmul(
                            pb.ap()[:], qT[:, j * S + i * 128: j * S + (i + 1) * 128],
                            wT[k][:, j * 512:(j + 1) * 512], start=(j == 0), stop=(j == ND - 1)),
                             wait=[(sV, wtr), (sSc, getattr(P, f"_b7p{i % 2}", 0))],
                             inc=([(sT, 1)] if j == ND - 1 else []))
                    wmm = sT.v
                    if i == NT - 1:
                        wt_pe_done[k] = wmm
                    P.do("scalar", lambda e, i=i, pb=pb: e.activation(
                        dq_f[:], pb.ap()[:], ACT.Copy, scale=insc[:, i:i + 1]),
                         wait=[(sT, wmm), (sV, getattr(P, "_b7ff", 0))], inc=[(sSc, 1)], chain=False)
                    setattr(P, f"_b7p{i % 2}", sSc.v)
                    wdq = sSc.v
                    P.do("vector", lambda e, oc=oc: e.tensor_tensor(
                        dq_f[:], dq_f[:], wsc_bc[:, oc * 512:(oc + 1) * 512], ALU.mult),
                         wait=[(sSc, wdq)], chain=False)
                    P.do("vector", lambda e, oc=oc: e.tensor_tensor(
                        ff_h[:], dq_f[:], bias_bc[:, oc * 512:(oc + 1) * 512], ALU.add),
                         inc=[(sV, 1)], chain=False)
                    P._b7ff = sV.v
                    wffh = sV.v
                    for ot in range(4):
                        transp_one(ff_h[:, ot * 128:(ot + 1) * 128], ident_h,
                                   [(sV, wffh), (sSa, st_ff)],
                                   [fp_b[:, ot * 128:(ot + 1) * 128]])
                    wev = sV.v
                    P.do("gpsimd", lambda e, i=i, oc=oc: e.dma_start(
                        ffnT_scr.rearrange("(j i2 p) t -> p j i2 t", p=128, i2=NT)
                        [:, oc * 4:(oc + 1) * 4, i, :],
                        fp_b[:].rearrange("p (o t) -> p o t", o=4)),
                         wait=[(sV, wev)], inc=[(sSa, 16)])
                    st_ff = sSa.v
            P.barrier()
        rstk.close()

        with ExitStack() as ph:
            def psb(n, s_, dt=F32):
                return ph.enter_context(nc.sbuf_tensor(n, s_, dt))

            al_sb = psb("al_sb", [128, ND * DS])
            negAn = psb("negAn", [128, ND * DS])
            dp_sb = psb("dp_sb", [128, ND])
            Bb = psb("Bb", [128, 16 * TCH])
            Cb = psb("Cb", [128, 16 * TCH])
            carry = psb("carry", [128, ND * DS])
            dtb = [psb(f"dtb{k}", [128, TCH]) for k in range(2)]
            ub = [psb(f"ub{k}", [128, TCH]) for k in range(2)]
            du_f = psb("du_f", [128, TCH])
            ab_f = [psb(f"ab{k}", [128, TCH]) for k in range(2)]
            inp2 = [psb(f"inp{k}", [128, TCH]) for k in range(2)]
            tmp2 = [psb(f"tmp{k}", [128, TCH]) for k in range(2)]
            h_all = psb("h_all", [128, DS * TCH])
            y_f = psb("y_f", [128, TCH])
            yg_f = psb("yg_f", [128, TCH])
            g_f = psb("g_f", [128, TCH])
            ffh = psb("ffh", [128, TCH], F16)
            e_f = psb("e_f", [128, TCH])
            r_scr = psb("r_scr", [128, TCH])
            x1t = [psb(f"b8x{k}", [128, D]) for k in range(4)]
            out_sb = [psb(f"b8o{k}", [128, D]) for k in range(4)]

            P.do("sync", lambda e: e.dma_start(
                al_sb[:].rearrange("p (j n) -> p j n", j=ND),
                A_log_d.rearrange("(j p) n -> p j n", p=128)), inc=[(sW, 16)])
            P.do("sync", lambda e: e.dma_start(
                dp_sb[:], Dp_d.rearrange("(j p) -> p j", p=128), allow_slow_non_contiguous=True),
                 inc=[(sW, 16)])
            wal = sW.v
            P.do("scalar", lambda e: e.activation(negAn[:], al_sb[:], ACT.Exp), wait=[(sW, wal)],
                 inc=[(sSc, 1)])
            wna = sSc.v
            P.do("vector", lambda e: e.tensor_scalar(negAn[:], negAn[:], -1.0, None, ALU.mult),
                 wait=[(sSc, wna)], inc=[(sV, 1)])
            wna2 = sV.v

            out_prev = 0
            ld_sc = [0, 0]
            ld_gp = [0, 0]
            ld_v = [0, 0]
            scan_used = [0, 0]
            tmp_used = [0, 0]
            carry_sc = [0] * 16
            for c in range(NCH):
                useB = getattr(P, "_b8useB", 0)
                useC = getattr(P, "_b8useC", 0)
                for (boff, dst, wt) in ((0, Bb, [(sV, useB)]), (DS, Cb, [(sV, useC)])):
                    P.do("sync", lambda e, c=c, boff=boff, dst=dst: e.dma_start(
                        dst[:].rearrange("p (n t) -> p n t", n=16),
                        bcT_scr[boff:boff + DS, c * TCH:(c + 1) * TCH].partition_broadcast(128)),
                         wait=wt, inc=[(sLe, 16)])
                wbcd = sLe.v
                for ii in range(4):
                    P.do("sync", lambda e, c=c, ii=ii: e.dma_start(
                        x1t[ii][:], x1_scr[(c * 4 + ii) * 128:(c * 4 + ii + 1) * 128, :]),
                         wait=[(sSd, out_prev)], inc=[(sLc, 16)])
                wx1 = sLc.v
                for j in range(ND):
                    k = j % 2
                    blkr = (j * NT + c * 4) * 128
                    P.do("sync", lambda e, blkr=blkr, k=k: e.dma_start(
                        dtb[k][:].rearrange("p (i2 t) -> p i2 t", i2=4),
                        dtT_scr[blkr:blkr + 512, :].rearrange("(i2 p) t -> p i2 t", p=128)),
                         wait=[(sSc, ld_sc[k]), (sV, ld_v[k])],
                         inc=[(sL[k], 16)])
                    P.do("sync", lambda e, blkr=blkr, k=k: e.dma_start(
                        ub[k][:].rearrange("p (i2 t) -> p i2 t", i2=4),
                        uT_scr[blkr:blkr + 512, :].rearrange("(i2 p) t -> p i2 t", p=128)),
                         inc=[(sL[k], 16)])
                    wld = sL[k].v
                    P.do("sync", lambda e, blkr=blkr: e.dma_start(
                        ffh[:].rearrange("p (i2 t) -> p i2 t", i2=4),
                        ffnT_scr[blkr:blkr + 512, :].rearrange("(i2 p) t -> p i2 t", p=128)),
                         wait=[(sV, getattr(P, "_b8g", 0))], inc=[(sLd, 16)])
                    wff = sLd.v
                    P.do("scalar", lambda e: e.activation(e_f[:], ffh[:], ACT.Tanh, scale=0.5),
                         wait=[(sLd, wff)], inc=[(sSc, 1)], chain=False)
                    wsg = sSc.v
                    P.do("vector", lambda e, k=k: e.tensor_tensor(du_f[:], dtb[k][:], ub[k][:], ALU.mult),
                         wait=[(sL[k], wld), (sLe, wbcd)], inc=[(sV, 1)], chain=False)
                    for n in range(2):
                        P.do("vector", lambda e, n=n: e.tensor_tensor(
                            inp2[n][:], du_f[:], Bb[:, n * TCH:(n + 1) * TCH], ALU.mult),
                             inc=[(sV, 1)], chain=False)
                    cm_done = {}
                    scan_v = {}
                    for n in range(16):
                        kn = n % 2
                        P.do("scalar", lambda e, j=j, n=n, k=k, kn=kn: e.activation(
                            ab_f[kn][:], dtb[k][:], ACT.Exp, scale=negAn[:, j * DS + n: j * DS + n + 1]),
                             wait=[(sL[k], wld), (sV, wna2), (sV, scan_used[kn])],
                             inc=[(sSc, 1)], chain=False)
                        wab = sSc.v
                        ini = 0.0 if c == 0 else carry[:, j * DS + n: j * DS + n + 1]
                        P.do("vector", lambda e, n=n, kn=kn, ini=ini: e.tensor_tensor_scan(
                            h_all[:, n * TCH:(n + 1) * TCH], ab_f[kn][:], inp2[kn][:], ini,
                            ALU.mult, ALU.add),
                             wait=[(sSc, wab), (sSc, carry_sc[n])],
                             inc=[(sV, 1)], chain=False)
                        scan_used[kn] = sV.v
                        scan_v[n] = sV.v
                        if n >= 2:
                            P.do("scalar", lambda e, j=j, n2=n - 2: e.activation(
                                carry[:, j * DS + n2: j * DS + n2 + 1],
                                h_all[:, (n2 + 1) * TCH - 1:(n2 + 1) * TCH], ACT.Copy),
                                 wait=[(sV, scan_v[n - 2])], inc=[(sSc, 1)], chain=False)
                            carry_sc[n - 2] = sSc.v
                        if n + 2 < 16:
                            P.do("vector", lambda e, n=n, kn=kn: e.tensor_tensor(
                                inp2[kn][:], du_f[:], Bb[:, (n + 2) * TCH:(n + 3) * TCH], ALU.mult),
                                 chain=False)
                        if n == 0:
                            P.do("vector", lambda e, n=n: e.tensor_tensor(
                                y_f[:], h_all[:, n * TCH:(n + 1) * TCH],
                                Cb[:, n * TCH:(n + 1) * TCH], ALU.mult),
                                 inc=[(sV, 1)], chain=False)
                        else:
                            P.do("vector", lambda e, n=n, kn=kn: e.tensor_tensor(
                                tmp2[kn][:], h_all[:, n * TCH:(n + 1) * TCH],
                                Cb[:, n * TCH:(n + 1) * TCH], ALU.mult),
                                 chain=False)
                            P.do("vector", lambda e, kn=kn: e.tensor_tensor(
                                y_f[:], y_f[:], tmp2[kn][:], ALU.add),
                                 inc=[(sV, 1)], chain=False)
                        cm_done[n] = sV.v
                    P._b8useC = cm_done[15]
                    P._b8useB = cm_done[15]
                    for n2 in (14, 15):
                        P.do("scalar", lambda e, j=j, n2=n2: e.activation(
                            carry[:, j * DS + n2: j * DS + n2 + 1],
                            h_all[:, (n2 + 1) * TCH - 1:(n2 + 1) * TCH], ACT.Copy),
                             wait=[(sV, scan_v[n2])], inc=[(sSc, 1)], chain=False)
                        carry_sc[n2] = sSc.v
                    ld_sc[k] = sSc.v
                    P.do("vector", lambda e: e.scalar_tensor_tensor(
                        g_f[:], e_f[:], 1.0, ffh[:], ALU.add, ALU.mult),
                         wait=[(sSc, wsg)], inc=[(sV, 1)], chain=False)
                    P._b8g = sV.v
                    P.do("vector", lambda e, j=j, k=k: e.scalar_tensor_tensor(
                        y_f[:], ub[k][:], dp_sb[:, j:j + 1], y_f[:], ALU.mult, ALU.add),
                         inc=[(sV, 1)], chain=False)
                    ld_v[k] = sV.v
                    P.do("vector", lambda e: e.scalar_tensor_tensor(
                        yg_f[:], y_f[:], 0.5, g_f[:], ALU.mult, ALU.mult),
                         inc=[(sV, 1)], chain=False)
                    wdel2 = sV.v
                    for ii in range(4):
                        def evac(e, pap, ii=ii, j=j):
                            return e.tensor_tensor(out_sb[ii][:, j * 128:(j + 1) * 128], pap,
                                                   x1t[ii][:, j * 128:(j + 1) * 128], ALU.add)
                        transp_one(yg_f[:, ii * 128:(ii + 1) * 128], ident,
                                   [(sV, wdel2), (sLc, wx1), (sSd, out_prev)], evac=evac)
                wout = sV.v
                for ii in range(4):
                    P.do("scalar", lambda e, c=c, ii=ii: e.dma_start(
                        y_out[(c * 4 + ii) * 128:(c * 4 + ii + 1) * 128, :], out_sb[ii][:]),
                         wait=[(sV, wout)], inc=[(sSd, 16)])
                out_prev = sSd.v
            P.do("gpsimd", lambda e: None, wait=[(sSd, sSd.v)])
            P.barrier()

    nc.finalize()
    return nc


_CACHE = {}


def _consts():
    ident = np.eye(128, dtype=np.float32)
    iota8 = np.tile(np.arange(NE, dtype=np.float32), (128, 1))
    freqs = 1.0 / (10000.0 ** (np.arange(0, RD, 2, dtype=np.float32) / RD))
    t = np.arange(S, dtype=np.float32)
    ang = np.outer(t, freqs)
    cosr = np.tile(np.cos(ang).astype(np.float32), (1, H))
    sinr = np.tile(np.sin(ang).astype(np.float32), (1, H))
    return ident, iota8, cosr, sinr


def kernel(**inputs):
    if "nc" not in _CACHE:
        _CACHE["nc"] = build_program()
    nc = _CACHE["nc"]
    ident, iota8, cosr, sinr = _consts()
    xfull = np.ascontiguousarray(inputs["L0_attn_x"], dtype=np.float32)
    common = {
        "Wq_down": inputs["L0_attn_Wq_down"], "q_norm_w": inputs["L0_attn_q_norm_w"],
        "Wq_up": inputs["L0_attn_Wq_up"], "Wq_rope": inputs["L0_attn_Wq_rope"],
        "Wkv_down": inputs["L0_attn_Wkv_down"], "kv_norm_w": inputs["L0_attn_kv_norm_w"],
        "Wk_up": inputs["L0_attn_Wk_up"], "Wv_up": inputs["L0_attn_Wv_up"],
        "Wk_rope": inputs["L0_attn_Wk_rope"], "Wout": inputs["L0_attn_Wout"],
        "gate_w": inputs["L0_ffn_gate_w"], "expert_w": inputs["L0_ffn_expert_w"],
        "expert_b": inputs["L0_ffn_expert_b"], "Wx": inputs["L1_attn_Wx"],
        "Wdt": inputs["L1_attn_Wdt"], "bdt": inputs["L1_attn_bdt"],
        "A_log": inputs["L1_attn_A_log"], "Dp": inputs["L1_attn_Dp"],
        "w8": inputs["L1_ffn_weight"], "w_scale": inputs["L1_ffn_w_scale"],
        "bias8": inputs["L1_ffn_bias"],
        "ident": ident, "iota8": iota8, "cosrep": cosr, "sinrep": sinr,
    }
    common = {k: np.ascontiguousarray(v) for k, v in common.items()}
    in_maps = []
    for b in range(8):
        m = dict(common)
        m["x"] = np.ascontiguousarray(xfull[b])
        in_maps.append(m)
    res = run_bass_kernel_spmd(nc, in_maps, core_ids=list(range(8)))
    out = np.stack([np.asarray(res.results[b]["y"], dtype=np.float32) for b in range(8)], axis=0)
    return out
